# version 09
# baseline (speedup 1.0000x reference)
import math
import contextlib

import numpy as np
import ml_dtypes

import concourse.bass as bass
import concourse.mybir as mybir
from concourse.bass_utils import run_bass_kernel_spmd

F32 = mybir.dt.float32
BF16 = mybir.dt.bfloat16
AF = mybir.ActivationFunctionType
ALU = mybir.AluOpType
AX = mybir.AxisListType

S = 4096
DM = 4096
HD = 128
NHL = 8
NCHUNK = 3 * NHL
NT = S // 128
NTBLK = 8
NQT = 32
NE = 64
KTOP = 8
Q_SCALE = 1.0 / math.sqrt(HD)
NEG_BIG = -1.0e9
EPS = 1e-5
RS_GROUPS = [[0, 1, 2, 3], [4, 5, 6, 7]]

def _slot_window(j):
    slope = 2.0 ** (-(4 * j + 3 + 1) / 4.0)
    w = 384.0 / slope
    if w >= S:
        return S
    return int(math.ceil(w / 512.0) * 512)

SLOT_W = [_slot_window(j) for j in range(NHL)]

L0, L1, L2, L3, L4, L5, LC = "l0", "l1", "l2", "l3", "l4", "l5", "lc"
G0, G1, G2, G3, G4, G5 = "g0", "g1", "g2", "g3", "g4", "g5"
TT, VV, AA, CC = "tt", "vv", "aa", "cc"
ALL_SEMS = (L0, L1, L2, L3, L4, L5, LC, G0, G1, G2, G3, G4, G5, TT, VV, AA, CC)


class Em:

    ENGS = ("sync", "tensor", "vector", "scalar", "gpsimd")

    def __init__(self):
        self.ops = {e: [] for e in self.ENGS}
        self.cnt = {n: 0 for n in ALL_SEMS}
        self.lastwait = {e: {} for e in self.ENGS}

    def inc(self, sem, by=1):
        self.cnt[sem] += by
        return self.cnt[sem]

    def wait(self, eng, sem, val):
        if val is None or val <= 0:
            return
        if self.lastwait[eng].get(sem, 0) >= val:
            return
        self.lastwait[eng][sem] = val

        def f(e, SM, sem=sem, val=val):
            e.wait_ge(SM[sem], val)

        self.ops[eng].append(f)

    def waitrel(self, eng, rel):
        if rel is None:
            return
        if isinstance(rel, list):
            for r in rel:
                self.waitrel(eng, r)
        else:
            self.wait(eng, rel[0], rel[1])

    def op(self, eng, fn):
        self.ops[eng].append(lambda e, SM, fn=fn: fn(e))

    def eop(self, eng, fn, inc=None):
        if inc is None:
            self.op(eng, fn)
            return None
        v = self.inc(inc)

        def g(e, SM, fn=fn, s=inc):
            fn(e).then_inc(SM[s], 1)

        self.ops[eng].append(g)
        return v

    def barrier(self):
        snap = dict(self.cnt)
        for e in self.ENGS:
            for s, v in snap.items():
                self.wait(e, s, v)


class Builder:
    def __init__(self, upto=5, debug=(), t5_count=8, p5b_stage=3):
        self.upto = upto
        self.t5_count = t5_count
        self.p5b_stage = p5b_stage
        self.debug = debug
        self.nc = bass.Bass("TRN2", target_bir_lowering=False)
        self.em = Em()
        self.stack = contextlib.ExitStack()

    def sbuf(self, name, shape, dtype, st=None, tdest=False, side=None):
        return (st or self.stack).enter_context(
            self.nc.sbuf_tensor(name, shape, dtype, is_transpose_dest=tdest,
                                side=side))

    def psum(self, name, shape, dtype, st=None):
        return (st or self.stack).enter_context(self.nc.psum_tensor(name, shape, dtype))

    def chain(self, eng, fn, sem):
        v = self.em.eop(eng, fn, inc=sem)
        self.em.wait(eng, sem, v)
        return v

    def dma(self, eng, sem, out, in_, transpose=False):
        v = self.em.inc(sem, 16)

        if transpose:
            def f(e, SM, o=out, i=in_, s=sem):
                e.dma_start_transpose(out=o, in_=i).then_inc(SM[s], 16)
        else:
            def f(e, SM, o=out, i=in_, s=sem):
                e.dma_start(out=o, in_=i).then_inc(SM[s], 16)

        self.em.ops[eng].append(f)
        return v

    def build(self):
        nc = self.nc
        em = self.em

        self.x_in = nc.dram_tensor("x", [S, DM], F32, kind="ExternalInput")
        self.xq_in = nc.dram_tensor("xq", [1024, DM], F32, kind="ExternalInput")
        self.wqkv_in = nc.dram_tensor("wqkv", [NCHUNK * 128, DM], F32, kind="ExternalInput")
        self.bqkv_in = nc.dram_tensor("bqkv", [128, NCHUNK], F32, kind="ExternalInput")
        self.wout_in = nc.dram_tensor("wout", [DM, NHL * 128], F32, kind="ExternalInput")
        self.bout_in = nc.dram_tensor("bout_bc", [128, DM], F32, kind="ExternalInput")
        self.rw_in = nc.dram_tensor("rw", [NE, DM], F32, kind="ExternalInput")
        self.rb_in = nc.dram_tensor("rb_bc", [128, NE], F32, kind="ExternalInput")
        self.eb_in = nc.dram_tensor("eb", [NE, DM], F32, kind="ExternalInput")
        self.alibi3_in = nc.dram_tensor("alibi3", [3 * NHL, S], BF16, kind="ExternalInput")
        self.ones3_in = nc.dram_tensor("ones3", [3, 128], BF16, kind="ExternalInput")
        self.masks_in = nc.dram_tensor("masks", [4 * 128, 512], F32, kind="ExternalInput")
        self.idb_in = nc.dram_tensor("idb", [128, 128], BF16, kind="ExternalInput")
        self.y_out = nc.dram_tensor("y", [1024, DM], F32, kind="ExternalOutput")

        self.xn_sc = nc.dram_tensor("xn_sc", [S, DM], BF16)
        self.wq_sc = nc.dram_tensor("wq_sc", [128, NCHUNK * 32 * 128], BF16)
        self.qkv_sc = nc.dram_tensor("qkv_sc", [NCHUNK * 128, S], BF16)
        self.ao_sc = nc.dram_tensor("ao_sc", [NHL * 128, S], BF16)
        self.wout_bf = nc.dram_tensor("wout_bf", [DM, NHL * 128], BF16)
        self.ypart = nc.dram_tensor("ypart", [S, DM], BF16)
        self.yred = nc.dram_tensor("yred", [1024, DM], BF16)

        dbg = {}
        for name, shape, dt in (("dbg_xn", [S, DM], BF16),
                                ("dbg_wq", [128, NCHUNK * 32 * 128], BF16),
                                ("dbg_qkv", [NCHUNK * 128, S], BF16),
                                ("dbg_ao", [NHL * 128, S], BF16),
                                ("dbg_wout", [DM, NHL * 128], BF16),
                                ("dbg_ypart", [S, DM], BF16),
                                ("dbg_yred", [1024, DM], BF16)):
            if name in self.debug:
                dbg[name] = nc.dram_tensor(name, shape, dt, kind="ExternalOutput")
        self.dbg = dbg

        with self.stack:
            sems = {}
            for n in ALL_SEMS:
                sems[n] = self.stack.enter_context(nc.semaphore(n))

            self.masks_sb = self.sbuf("masks_sb", [128, 4 * 512], F32)
            self.idb_sb = self.sbuf("idb_sb", [128, 128], BF16)
            self.ones3_sb = self.sbuf("ones3_sb", [3, 128], BF16)
            self.bqs_sb = self.sbuf("bqs_sb", [128, NCHUNK], F32)
            self.rw_fm = self.sbuf("rw_fm", [128, 32 * NE], BF16)
            self.eb_bf = self.sbuf("eb_bf", [NE, DM], BF16)
            self.rb_sb = self.sbuf("rb_sb", [128, NE], F32)

            for v in range(4):
                self.dma("sync", LC, self.masks_sb[:, v * 512:(v + 1) * 512],
                         self.masks_in[v * 128:(v + 1) * 128, :])
            self.dma("sync", LC, self.idb_sb[:, :], self.idb_in[:, :])
            self.dma("sync", LC, self.ones3_sb[:, :], self.ones3_in[:, :])
            self.v_consts = self.dma("sync", LC, self.rb_sb[:, :], self.rb_in[:, :])

            self.phase12()
            em.barrier()
            if self.upto >= 3:
                self.phase3()
                em.barrier()
            if self.upto >= 4:
                self.phase5a()
            if self.upto >= 5:
                self.phase5b()
            elif self.upto == 4:
                self.emit_collectives()
                self.p5.close()

            self.emit_debug()
            em.barrier()

            with nc.Block() as block:
                ops = em.ops

                @block.sync
                def _(e):
                    for f in ops["sync"]:
                        f(e, sems)

                @block.tensor
                def _(e):
                    for f in ops["tensor"]:
                        f(e, sems)

                @block.vector
                def _(e):
                    for f in ops["vector"]:
                        f(e, sems)

                @block.scalar
                def _(e):
                    for f in ops["scalar"]:
                        f(e, sems)

                @block.gpsimd
                def _(e):
                    for f in ops["gpsimd"]:
                        f(e, sems)

        return nc

    def ln_stats(self, xa, stat, c0, prev_act_rel, prev_vec_rel, dump):
        em = self.em
        em.waitrel("vector", prev_act_rel)

        self.chain("vector", lambda e, o=stat, c=c0, i_=xa:
                   e.reduce_sum(o[:, c:c + 1], i_, axis=AX.X, negate=True), VV)
        self.chain("vector", lambda e, o=stat, c=c0:
                   e.tensor_scalar(o[:, c + 1:c + 2], o[:, c:c + 1], 1.0 / DM, None, ALU.mult), VV)

        em.waitrel("scalar", prev_vec_rel)

        v_sq = em.eop("scalar", lambda e, o=dump, i_=xa, a=stat, c=c0:
                      e.activation(o, i_, AF.Square, accum_out=a[:, c + 2:c + 3]),
                      inc=AA)

        em.wait("vector", AA, v_sq)
        self.chain("vector", lambda e, o=stat, c=c0:
                   e.tensor_scalar(o[:, c + 3:c + 4], o[:, c + 2:c + 3], 1.0 / DM, None, ALU.mult), VV)
        self.chain("vector", lambda e, o=stat, c=c0:
                   e.tensor_tensor(o[:, c + 4:c + 5], o[:, c + 1:c + 2], o[:, c + 1:c + 2], ALU.mult), VV)
        self.chain("vector", lambda e, o=stat, c=c0:
                   e.tensor_tensor(o[:, c + 5:c + 6], o[:, c + 3:c + 4], o[:, c + 4:c + 5], ALU.subtract), VV)
        v_var = self.chain("vector", lambda e, o=stat, c=c0:
                           e.tensor_scalar(o[:, c + 5:c + 6], o[:, c + 5:c + 6], EPS, None, ALU.add), VV)

        em.wait("scalar", VV, v_var)
        v_std = em.eop("scalar", lambda e, o=stat, c=c0:
                       e.activation(o[:, c + 6:c + 7], o[:, c + 5:c + 6], AF.Sqrt), inc=AA)

        em.wait("vector", AA, v_std)
        v_rstd = self.chain("vector", lambda e, o=stat, c=c0:
                            e.reciprocal(o[:, c + 7:c + 8], o[:, c + 6:c + 7]), VV)
        return v_rstd, v_sq, v_std

    def phase12(self):
        em = self.em
        with contextlib.ExitStack() as p12:
            x_t = self.sbuf("p1_x", [128, DM], F32, p12)
            xn_t = self.sbuf("p1_xn", [128, DM], BF16, p12)
            stat = self.sbuf("p1_stat", [128, 16], F32, p12)
            wrow_f = self.sbuf("p1_wf", [128, DM], F32, p12)
            wrow_b = self.sbuf("p1_wb", [128, DM], BF16, p12)
            wqe = self.sbuf("p1_wqe", [128, 2 * 1024], BF16, p12)
            bq_raw = self.sbuf("p1_bq", [128, NCHUNK], F32, p12)
            tp_bank = [self.psum("p1_tp0", [128, 1024], BF16, p12),
                       self.psum("p1_tp1", [128, 1024], BF16, p12)]
            NPC = 6
            NPASS = 4
            xnb = self.sbuf("p2_xnb", [128, 2 * 32 * 512], BF16, p12, tdest=True)
            panel = self.sbuf("p2_panel", [128, NPC * 32 * 128], BF16, p12)
            strip = self.sbuf("p2_strip", [128, 2 * NPC * 512], BF16, p12)
            banks = [self.psum(f"p2_b{i}", [128, 512], F32, p12) for i in range(NPC)]

            v_bq = self.dma("sync", LC, bq_raw[:, :], self.bqkv_in[:, :])
            em.wait("vector", LC, v_bq)
            self.chain("vector", lambda e, o=self.bqs_sb, i=bq_raw:
                       e.tensor_scalar(o[:, 0:NCHUNK], i[:, 0:NCHUNK], 1.0, None, ALU.mult), VV)
            v_bqs = self.chain("vector", lambda e, o=self.bqs_sb, i=bq_raw:
                               e.tensor_scalar(o[:, 0:NHL], i[:, 0:NHL], Q_SCALE, None,
                                               ALU.mult), VV)

            rel_x = None
            rel_xn = None
            rel_stat_a = [None, None]
            rel_stat_v = [None, None]
            rel_wf = None
            rel_wb = None
            rel_wqe = [None, None]
            rel_tp = [None, None]
            xn_wv = [None] * NT
            wq_wv = [None] * NCHUNK
            self.v_bqs = v_bqs

            def p1_tile(i):
                nonlocal rel_x, rel_xn
                sx = L0 if i % 2 == 0 else L1
                c0 = (i % 2) * 8
                em.waitrel("sync", rel_x)
                v_ld = self.dma("sync", sx, x_t[:, :], self.x_in[i * 128:(i + 1) * 128, :])
                em.wait("vector", sx, v_ld)
                em.wait("scalar", sx, v_ld)
                em.waitrel("scalar", rel_xn)
                v_rstd, v_sq, v_std = self.ln_stats(
                    x_t[:, :], stat, c0, rel_stat_a[i % 2], rel_stat_v[i % 2],
                    dump=xn_t[:, :])
                em.waitrel("vector", rel_xn)
                v_xn = em.eop("vector", lambda e, o=xn_t, i_=x_t, a=stat, c=c0:
                              e.tensor_scalar(o[:, :], i_[:, :], a[:, c + 1:c + 2],
                                              a[:, c + 7:c + 8], ALU.add, ALU.mult),
                              inc=VV)
                rel_x = [(VV, v_xn), (AA, v_sq)]
                rel_stat_a[i % 2] = (AA, v_std)
                rel_stat_v[i % 2] = (VV, v_xn)
                gx = G0 if i % 2 == 0 else G1
                em.wait("gpsimd", VV, v_xn)
                v_w = self.dma("gpsimd", gx, self.xn_sc[i * 128:(i + 1) * 128, :], xn_t[:, :])
                rel_xn = (gx, v_w)
                xn_wv[i] = (gx, v_w)

            def prep_chunk(kind, j):
                nonlocal rel_wf, rel_wb
                ws = j % 2
                sw = L2 if ws == 0 else L3
                src_t = self.wqkv_in if kind == 0 else self.wout_in
                ncol = DM if kind == 0 else NHL * 128
                em.waitrel("sync", rel_wf)
                v_wl = self.dma("sync", sw, wrow_f[:, 0:ncol],
                                src_t[j * 128:(j + 1) * 128, :])
                em.wait("scalar", sw, v_wl)
                em.waitrel("scalar", rel_wb)
                v_cast = em.eop("scalar", lambda e, o=wrow_b[:, 0:ncol], i_=wrow_f[:, 0:ncol]:
                                e.activation(o, i_, AF.Identity), inc=AA)
                rel_wf = (AA, v_cast)
                if kind == 1:
                    gw = G2 if ws == 0 else G3
                    em.wait("gpsimd", AA, v_cast)
                    v_d = self.dma("gpsimd", gw,
                                   self.wout_bf[j * 128:(j + 1) * 128, :], wrow_b[:, 0:ncol])
                    rel_wb = (gw, v_d)
                    return
                em.wait("tensor", AA, v_cast)
                v_tp_last = 0
                wv = []
                for g in range(4):
                    bk = (j * 4 + g) % 2
                    es = (j * 4 + g) % 2
                    em.waitrel("tensor", rel_tp[bk])
                    for t8 in range(8):
                        t = g * 8 + t8
                        fn = (lambda e, o=tp_bank[bk][:, t8 * 128:(t8 + 1) * 128],
                              i_=wrow_b[:, t * 128:(t + 1) * 128], idn=self.idb_sb:
                              e.transpose(o, i_, idn[:, :]))
                        if t8 == 7:
                            v_tp = em.eop("tensor", fn, inc=TT)
                        else:
                            em.eop("tensor", fn)
                    v_tp_last = v_tp
                    em.wait("vector", TT, v_tp)
                    em.waitrel("vector", rel_wqe[es])
                    dst = wqe[:, es * 1024:(es + 1) * 1024]
                    v_ev = em.eop("vector", lambda e, o=dst, i_=tp_bank[bk]:
                                  e.tensor_copy(o, i_[:, :]), inc=VV)
                    rel_tp[bk] = (VV, v_ev)
                    ge = G2 if es == 0 else G3
                    em.wait("gpsimd", VV, v_ev)
                    col = (j * 32 + g * 8) * 128
                    v_d = self.dma("gpsimd", ge, self.wq_sc[:, col:col + 1024], dst)
                    rel_wqe[es] = (ge, v_d)
                    wv.append((ge, v_d))
                rel_wb = (TT, v_tp_last)
                wq_wv[j] = wv

            rel_xnb = [None, None]
            rel_strip = [None, None]
            bank_rel = [None] * NPC
            rel_panel = None
            v_pan = [0] * NPASS

            def p2_panel_load(wpass):
                em.waitrel("sync", rel_panel)
                for j in range(NPC):
                    em.waitrel("sync", wq_wv[wpass * NPC + j])
                v_pan[wpass] = self.dma(
                    "sync", LC, panel[:, :],
                    self.wq_sc[:, wpass * NPC * 4096:(wpass + 1) * NPC * 4096])

            def p2_issue_loads(tb):
                sl = tb % 2
                sem = L4 if sl == 0 else L5
                em.waitrel("sync", rel_xnb[sl])
                for t in range(4 * tb, 4 * tb + 4):
                    em.waitrel("sync", xn_wv[t])
                v = 0
                for dc in range(32):
                    dst = xnb[:, (sl * 32 + dc) * 512:(sl * 32 + dc + 1) * 512]
                    srcd = self.xn_sc[tb * 512:(tb + 1) * 512, dc * 128:(dc + 1) * 128]
                    v = self.dma("sync", sem, dst, srcd, transpose=True)
                return (sem, v)

            def p2_tblk(wpass, tb, cur, nxt):
                sl = tb % 2
                v_fin = [0] * NPC
                em.wait("tensor", LC, v_pan[wpass])
                em.waitrel("tensor", cur)
                for dc in range(32):
                    xa = xnb[:, (sl * 32 + dc) * 512:(sl * 32 + dc + 1) * 512]
                    for o8 in range(NPC):
                        if dc == 0:
                            em.waitrel("tensor", bank_rel[o8])
                        wsl = panel[:, (o8 * 32 + dc) * 128:(o8 * 32 + dc + 1) * 128]
                        fn = (lambda e, o=banks[o8][:, :], w=wsl, x=xa,
                              st=(dc == 0), sp=(dc == 31):
                              e.matmul(o, w, x, start=st, stop=sp))
                        if dc == 31:
                            v_fin[o8] = em.eop("tensor", fn, inc=TT)
                        else:
                            em.eop("tensor", fn)
                rel_xnb[sl] = (TT, v_fin[NPC - 1])

                ssl = tb % 2
                v_d_last = 0
                em.wait("scalar", VV, self.v_bqs)
                for o8 in range(NPC):
                    oc = wpass * NPC + o8
                    em.wait("scalar", TT, v_fin[o8])
                    em.waitrel("scalar", rel_strip[ssl])
                    dsts = strip[:, (ssl * NPC + o8) * 512:(ssl * NPC + o8 + 1) * 512]
                    scl = Q_SCALE if oc < NHL else 1.0
                    v_ev = em.eop("scalar",
                                  lambda e, o=dsts, i_=banks[o8][:, :],
                                  b=self.bqs_sb[:, oc:oc + 1], s=scl:
                                  e.activation(o, i_, AF.Identity, bias=b, scale=s),
                                  inc=AA)
                    bank_rel[o8] = (AA, v_ev)
                    em.wait("gpsimd", AA, v_ev)
                    gs = G4 if ssl == 0 else G5
                    v_d_last = self.dma(
                        "gpsimd", gs,
                        self.qkv_sc[oc * 128:(oc + 1) * 128, tb * 512:(tb + 1) * 512],
                        dsts)
                rel_strip[ssl] = (gs, v_d_last)

            prep_q = [(0, j) for j in range(NCHUNK)] + [(1, j) for j in range(32)]
            prep_pos = 0

            def run_prep(n):
                nonlocal prep_pos
                for _ in range(n):
                    if prep_pos < len(prep_q):
                        prep_chunk(*prep_q[prep_pos])
                        prep_pos += 1

            pending = None
            for s in range(8):
                for t in range(4 * s, 4 * s + 4):
                    p1_tile(t)
                    run_prep(2)
                if s == 0:
                    p2_panel_load(0)
                toks = p2_issue_loads(s)
                if pending is not None:
                    p2_tblk(0, pending[0], pending[1], None)
                pending = (s, toks)
            run_prep(len(prep_q))
            p2_tblk(0, pending[0], pending[1], None)
            for wpass in range(1, NPASS):
                rel_panel = (TT, em.cnt[TT])
                p2_panel_load(wpass)
                cur = p2_issue_loads(0)
                for tb in range(NTBLK):
                    nxt = p2_issue_loads(tb + 1) if tb + 1 < NTBLK else None
                    p2_tblk(wpass, tb, cur, nxt)
                    cur = nxt

    def phase3(self):
        em = self.em
        with contextlib.ExitStack() as p3:
            q_sb = self.sbuf("p3_q", [128, 2 * S], BF16, p3)
            k_sb = self.sbuf("p3_k", [128, 2 * S], BF16, p3)
            v_sb = self.sbuf("p3_v", [128, 2 * S], BF16, p3)
            vt_sb = self.sbuf("p3_vt", [128, 2 * 32 * 128], BF16, p3)
            al_sb = self.sbuf("p3_al", [3, 2 * S], BF16, p3)
            srow = self.sbuf("p3_srow", [128, 3 * S], F32, p3)
            prow = self.sbuf("p3_prow", [128, 3 * S], BF16, p3)
            pt_sb = self.sbuf("p3_pt", [128, 32 * 128], BF16, p3)
            ao_sb = self.sbuf("p3_ao", [128, 2 * S], BF16, p3)
            zc = self.sbuf("p3_zc", [128, 12], F32, p3)

            sc_banks = [self.psum(f"p3_s{i}", [128, 512], F32, p3) for i in range(3)]
            tp_banks = [self.psum("p3_t0", [128, 1024], BF16, p3),
                        self.psum("p3_t1", [128, 1024], BF16, p3)]
            ao_banks = [self.psum("p3_aob0", [128, 512], F32, p3),
                        self.psum("p3_aob1", [128, 512], F32, p3)]

            st = dict(rel_pair=None, rel_sc=[None, None, None],
                      rel_tp=[None, None], rel_ao_bank=[None, None],
                      rel_ao_dma=[None, None],
                      sc_ctr=0, tp_ctr=0,
                      v_exp=[0, 0, 0], v_tpg=[0, 0, 0], v_max=[0, 0, 0],
                      v_nrm=[0, 0, 0])

            def load_pair(p):
                sem = L0 if p % 2 == 0 else L1
                em.waitrel("sync", st["rel_pair"])
                v = 0
                for hw in range(2):
                    h = 2 * p + hw
                    self.dma("sync", sem, q_sb[:, hw * S:(hw + 1) * S],
                             self.qkv_sc[h * 128:(h + 1) * 128, :])
                    self.dma("sync", sem, k_sb[:, hw * S:(hw + 1) * S],
                             self.qkv_sc[(NHL + h) * 128:(NHL + h + 1) * 128, :])
                    self.dma("sync", sem, v_sb[:, hw * S:(hw + 1) * S],
                             self.qkv_sc[(2 * NHL + h) * 128:(2 * NHL + h + 1) * 128, :])
                    v = self.dma("sync", sem, al_sb[:, hw * S:(hw + 1) * S],
                                 self.alibi3_in[3 * h:3 * h + 3, :])
                return (sem, v)

            def v_transposes(v_l):
                em.waitrel("tensor", v_l)
                for hw in range(2):
                    for g in range(4):
                        bk = st["tp_ctr"] % 2
                        st["tp_ctr"] += 1
                        em.waitrel("tensor", st["rel_tp"][bk])
                        for t8 in range(8):
                            t = g * 8 + t8
                            fn = (lambda e, o=tp_banks[bk][:, t8 * 128:(t8 + 1) * 128],
                                  i_=v_sb[:, hw * S + t * 128:hw * S + (t + 1) * 128],
                                  idn=self.idb_sb:
                                  e.transpose(o, i_, idn[:, :]))
                            if t8 == 7:
                                v_t = em.eop("tensor", fn, inc=TT)
                            else:
                                em.eop("tensor", fn)
                        em.wait("vector", TT, v_t)
                        v_e = em.eop("vector",
                                     lambda e,
                                     o=vt_sb[:, (hw * 4 + g) * 1024:(hw * 4 + g + 1) * 1024],
                                     i_=tp_banks[bk]:
                                     e.tensor_copy(o, i_[:, :]), inc=VV)
                        st["rel_tp"][bk] = (VV, v_e)

            def emit_scores_evac_max(it):
                i, qt, hw = it[0], it[1], it[2]
                L = (qt + 1) * 128
                k_lo = it[3]
                nk = (L - k_lo + 511) // 512
                psl = i % 3
                for j in range(nk):
                    c0 = k_lo + j * 512
                    w = min(512, L - c0)
                    bk = st["sc_ctr"] % 3
                    st["sc_ctr"] += 1
                    em.waitrel("tensor", st["rel_sc"][bk])
                    qa = q_sb[:, hw * S + qt * 128:hw * S + (qt + 1) * 128]
                    ka = k_sb[:, hw * S + c0:hw * S + c0 + w]
                    aa_ = al_sb[:, hw * S + c0:hw * S + c0 + w]
                    em.eop("tensor", lambda e, o=sc_banks[bk][:, 0:w], q=qa, k=ka:
                           e.matmul(o, q, k, start=True, stop=False))
                    v_mm = em.eop("tensor",
                                  lambda e, o=sc_banks[bk][:, 0:w], n=self.ones3_sb, a=aa_:
                                  e.matmul(o, n[:, :], a, start=False, stop=True), inc=TT)
                    dst = srow[:, psl * S + (c0 - k_lo):psl * S + (c0 - k_lo) + w]
                    if j == nk - 1:
                        mv = qt % 4
                        em.wait("vector", TT, v_mm)
                        em.wait("vector", AA, st["v_exp"][psl])
                        v = em.eop("vector",
                                   lambda e, o=dst, i_=sc_banks[bk][:, 0:w],
                                   m=self.masks_sb[:, mv * 512:mv * 512 + w]:
                                   e.tensor_tensor(o, i_, m, ALU.add), inc=VV)
                        st["rel_sc"][bk] = (VV, v)
                    else:
                        em.wait("scalar", TT, v_mm)
                        v = em.eop("scalar", lambda e, o=dst, i_=sc_banks[bk][:, 0:w]:
                                   e.activation(o, i_, AF.Identity), inc=AA)
                        st["rel_sc"][bk] = (AA, v)
                em.wait("vector", AA, em.cnt[AA])
                em.wait("vector", AA, st["v_exp"][psl])
                v_max = em.eop("vector",
                               lambda e, o=zc[:, psl * 4:psl * 4 + 1],
                               i_=srow[:, psl * S:psl * S + (L - k_lo)]:
                               e.reduce_max(o, i_, axis=AX.X, negate=True), inc=VV)
                st["v_max"][psl] = v_max

            def emit_exp_norm(it):
                i, qt, hw = it[0], it[1], it[2]
                L = (qt + 1) * 128 - it[3]
                psl = i % 3
                em.wait("scalar", VV, st["v_max"][psl])
                em.wait("scalar", TT, st["v_tpg"][psl])
                v_exp = em.eop("scalar",
                               lambda e, o=prow[:, psl * S:psl * S + L],
                               i_=srow[:, psl * S:psl * S + L],
                               b=zc[:, psl * 4:psl * 4 + 1],
                               a=zc[:, psl * 4 + 1:psl * 4 + 2]:
                               e.activation(o, i_, AF.Exp, bias=b, accum_out=a), inc=AA)
                st["v_exp"][psl] = v_exp

                em.wait("vector", AA, v_exp)
                self.chain("vector", lambda e, o=zc[:, psl * 4 + 2:psl * 4 + 3],
                           i_=zc[:, psl * 4 + 1:psl * 4 + 2]:
                           e.reciprocal(o, i_), VV)
                v_nrm = em.eop("vector",
                               lambda e, o=prow[:, psl * S:psl * S + L],
                               s=zc[:, psl * 4 + 2:psl * 4 + 3]:
                               e.tensor_scalar(o, o, s, None, ALU.mult), inc=VV)
                st["v_nrm"][psl] = v_nrm

            def emit_tp_av(it):
                i, qt, hw = it[0], it[1], it[2]
                k_lo = it[3]
                L = (qt + 1) * 128 - k_lo
                T = L // 128
                psl = i % 3
                em.wait("tensor", VV, st["v_nrm"][psl])
                v_e_last = 0
                v_tp_last = 0
                for g0 in range(0, T, 8):
                    gn = min(8, T - g0)
                    bk = st["tp_ctr"] % 2
                    st["tp_ctr"] += 1
                    em.waitrel("tensor", st["rel_tp"][bk])
                    for t8 in range(gn):
                        t = g0 + t8
                        fn = (lambda e, o=tp_banks[bk][:, t8 * 128:(t8 + 1) * 128],
                              i_=prow[:, psl * S + t * 128:psl * S + (t + 1) * 128],
                              idn=self.idb_sb:
                              e.transpose(o, i_, idn[:, :]))
                        if t8 == gn - 1:
                            v_tp_last = em.eop("tensor", fn, inc=TT)
                        else:
                            em.eop("tensor", fn)
                    em.wait("vector", TT, v_tp_last)
                    v_e_last = em.eop("vector",
                                      lambda e, o=pt_sb[:, g0 * 128:(g0 + gn) * 128],
                                      i_=tp_banks[bk][:, 0:gn * 128]:
                                      e.tensor_copy(o, i_), inc=VV)
                    st["rel_tp"][bk] = (VV, v_e_last)
                st["v_tpg"][psl] = v_tp_last

                apar = i % 2
                em.wait("tensor", VV, v_e_last)
                em.waitrel("tensor", st["rel_ao_bank"][apar])
                for t in range(T):
                    kt = k_lo // 128 + t
                    fn = (lambda e, o=ao_banks[apar][:, 0:128],
                          vt=vt_sb[:, hw * 4096 + kt * 128:hw * 4096 + (kt + 1) * 128],
                          pt=pt_sb[:, t * 128:(t + 1) * 128],
                          stt=(t == 0), sp=(t == T - 1):
                          e.matmul(o, vt, pt, start=stt, stop=sp))
                    if t == T - 1:
                        v_av = em.eop("tensor", fn, inc=TT)
                    else:
                        em.eop("tensor", fn)
                em.wait("vector", TT, v_av)
                em.waitrel("vector", st["rel_ao_dma"][hw])
                v_ao = em.eop("vector",
                              lambda e, o=ao_sb[:, hw * S + qt * 128:hw * S + (qt + 1) * 128],
                              i_=ao_banks[apar][:, 0:128]:
                              e.tensor_copy(o, i_), inc=VV)
                st["rel_ao_bank"][apar] = (VV, v_ao)
                return v_ao

            v_l = load_pair(0)
            em.waitrel("tensor", v_l)
            em.barrier()
            for p in range(NHL // 2):
                v_transposes(v_l)
                pend = []
                v_ao_last = [0, 0]
                for i in range(2 * NQT):
                    qt, hw = i // 2, i % 2
                    W = SLOT_W[2 * p + hw]
                    L = (qt + 1) * 128
                    k_lo = max(0, (L - W) // 512 * 512) if W < S else 0
                    it = (i, qt, hw, k_lo)
                    if len(pend) >= 3:
                        it3 = pend.pop(0)
                        v_ao_last[it3[2]] = emit_tp_av(it3)
                    emit_scores_evac_max(it)
                    pend.append(it)
                    if len(pend) >= 2:
                        emit_exp_norm(pend[-2])
                emit_exp_norm(pend[-1])
                for it3 in pend:
                    v_ao_last[it3[2]] = emit_tp_av(it3)
                st["rel_pair"] = (TT, em.cnt[TT])
                if p + 1 < NHL // 2:
                    v_l = load_pair(p + 1)
                for hw in range(2):
                    h = 2 * p + hw
                    ga = G0 if hw == 0 else G1
                    em.wait("gpsimd", VV, v_ao_last[hw])
                    v_d = self.dma("gpsimd", ga,
                                   self.ao_sc[h * 128:(h + 1) * 128, :],
                                   ao_sb[:, hw * S:(hw + 1) * S])
                    st["rel_ao_dma"][hw] = (ga, v_d)

    def phase5a(self):
        em = self.em
        self.p5 = contextlib.ExitStack()
        self.p5a = contextlib.ExitStack()

        self.ao5 = self.sbuf("p5_ao", [128, 8 * S], BF16, self.p5a, side="right")
        self.wo5 = self.sbuf("p5_wo", [128, 8 * S], BF16, self.p5a, tdest=True,
                             side="right")
        self.strip5 = self.sbuf("p5_strip", [128, 2 * 512], BF16, self.p5a,
                                side="right")
        self.pa = [self.psum("p5_a0", [128, 512], F32, self.p5),
                   self.psum("p5_a1", [128, 512], F32, self.p5)]

        for h in range(NHL):
            v_ao = self.dma("sync", L0, self.ao5[:, h * S:(h + 1) * S],
                            self.ao_sc[h * 128:(h + 1) * 128, :])
        for dc in range(NHL):
            v_wo = self.dma("sync", L5, self.wo5[:, dc * S:(dc + 1) * S],
                            self.wout_bf[:, dc * 128:(dc + 1) * 128], transpose=True)

        rel_bank = [None, None]
        rel_strip = [None, None]
        rs_done = []

        for oc in range(8):
            last_dma = 0
            for tt in range(32):
                bk = tt % 2
                em.waitrel("tensor", rel_bank[bk])
                em.wait("tensor", L0, v_ao)
                em.wait("tensor", L5, v_wo)
                for dc in range(8):
                    aoa = self.ao5[:, dc * S + tt * 128:dc * S + (tt + 1) * 128]
                    rhs = self.wo5[:, dc * S + oc * 512:dc * S + (oc + 1) * 512]
                    fn = (lambda e, o=self.pa[bk][:, :], a=aoa, r=rhs,
                          stt=(dc == 0), sp=(dc == 7):
                          e.matmul(o, a, r, start=stt, stop=sp))
                    if dc == 7:
                        v_mm = em.eop("tensor", fn, inc=TT)
                    else:
                        em.eop("tensor", fn)

                ssl = tt % 2
                em.wait("scalar", TT, v_mm)
                em.waitrel("scalar", rel_strip[ssl])
                dsts = self.strip5[:, ssl * 512:(ssl + 1) * 512]
                v_ev = em.eop("scalar", lambda e, o=dsts, i_=self.pa[bk][:, :]:
                              e.activation(o, i_, AF.Identity), inc=AA)
                rel_bank[bk] = (AA, v_ev)

                em.wait("gpsimd", AA, v_ev)
                gs = G0 if ssl == 0 else G1
                last_dma = self.dma("gpsimd", gs,
                                    self.ypart[tt * 128:(tt + 1) * 128,
                                               oc * 512:(oc + 1) * 512], dsts)
                rel_strip[ssl] = (gs, last_dma)
            rel_wo = (TT, em.cnt[TT])
        self.rs_done = rs_done

    def emit_collectives(self):
        em = self.em
        em.barrier()
        v_cc = em.inc(CC)

        def f(e, SM, i_=self.ypart, o=self.yred):
            e.collective_compute(
                "ReduceScatter", ALU.add, replica_groups=RS_GROUPS,
                ins=[i_.ap().opt()], outs=[o.ap().opt()]).then_inc(SM[CC], 1)
        em.ops["gpsimd"].append(f)
        self.rs_done = [v_cc] * 8
        em.wait("gpsimd", CC, em.cnt[CC])
        em.barrier()

    def phase5b(self):
        em = self.em
        with self.p5:
            p5 = self.p5
            tpb = [self.psum("p5_tp0", [128, 1024], BF16, p5),
                   self.psum("p5_tp1", [128, 1024], BF16, p5)]
            lgp = self.psum("p5_lgp", [128, 512], F32, p5)
            wdtp = self.psum("p5_wdtp", [128, 512], BF16, p5)
            bmp = [self.psum("p5_bm0", [128, 512], F32, p5),
                   self.psum("p5_bm1", [128, 512], F32, p5)]

            v_rwfm = 0
            with contextlib.ExitStack() as prep:
                rwf = self.sbuf("p5_rwf", [64, DM], F32, prep)
                rwb = self.sbuf("p5_rwb", [128, DM], BF16, prep)
                ebf = self.sbuf("p5_ebf", [64, DM], F32, prep)

                if self.p5b_stage >= 1:
                    v_rw = self.dma("sync", LC, rwf[:, :], self.rw_in[:, :])
                    v_eb = self.dma("sync", LC, ebf[:, :], self.eb_in[:, :])
                    em.wait("scalar", LC, v_eb)
                    em.eop("scalar", lambda e, o=rwb, i_=rwf:
                           e.activation(o[0:64, :], i_[:, :], AF.Identity))
                    em.wait("scalar", LC, v_eb)
                    v_casts = em.eop("scalar", lambda e, o=self.eb_bf, i_=ebf:
                                     e.activation(o[:, :], i_[:, :], AF.Identity), inc=AA)

                if self.p5b_stage >= 2:
                    em.wait("tensor", AA, v_casts)
                    v_ev_g = [0, 0]
                    for g in range(4):
                        bk = g % 2
                        em.wait("tensor", VV, v_ev_g[bk])
                        for t8 in range(8):
                            dc = g * 8 + t8
                            fn = (lambda e, o=tpb[bk][:, t8 * 128:(t8 + 1) * 128],
                                  i_=rwb[:, dc * 128:(dc + 1) * 128], idn=self.idb_sb:
                                  e.transpose(o, i_, idn[:, :]))
                            if t8 == 7:
                                v_t = em.eop("tensor", fn, inc=TT)
                            else:
                                em.eop("tensor", fn)
                        if self.p5b_stage < 3:
                            continue
                        em.wait("vector", TT, v_t)
                        for t8 in range(8):
                            dc = g * 8 + t8
                            fn = (lambda e, o=self.rw_fm[:, dc * NE:(dc + 1) * NE],
                                  i_=tpb[bk][:, t8 * 128:t8 * 128 + NE]:
                                  e.tensor_copy(o, i_))
                            if t8 == 7:
                                v_rwfm = em.eop("vector", fn, inc=VV)
                            else:
                                em.eop("vector", fn)
                        v_ev_g[bk] = v_rwfm
                em.barrier()

            self.emit_collectives()
            self.p5a.close()

            y_ring = self.sbuf("p5_y", [128, 2 * DM], F32, p5)
            xn5 = self.sbuf("p5_xn5", [128, 2 * DM], BF16, p5)
            xnfm = self.sbuf("p5_xnfm", [128, 32 * 128], BF16, p5)
            xnw = self.sbuf("p5_xnw", [128, DM], F32, p5)
            bout_sb = self.sbuf("p5_bout", [128, DM], F32, p5)
            yb = self.sbuf("p5_yb", [128, 2 * 512], BF16, p5)
            st5 = self.sbuf("p5_st5", [128, 16], F32, p5)
            lg = self.sbuf("p5_lg", [128, NE], F32, p5)
            ex5 = self.sbuf("p5_ex5", [128, NE], F32, p5)
            wk5 = self.sbuf("p5_wk5", [128, NE], F32, p5)
            eq5 = self.sbuf("p5_eq5", [128, NE], F32, p5)
            wd5 = self.sbuf("p5_wd5", [128, NE], F32, p5)
            wdb = self.sbuf("p5_wdb", [128, NE], BF16, p5)
            wdt = self.sbuf("p5_wdt", [64, 128], BF16, p5)
            cols5 = self.sbuf("p5_c5", [128, 8], F32, p5)

            v_bout = self.dma("sync", LC, bout_sb[:, :], self.bout_in[:, :])

            rel_y = [None, None]
            rel_xn5 = [None, None]
            rel_tpb = [(VV, v_rwfm), (VV, v_rwfm)]
            rel_lgp = None
            rel_wdtp = None
            rel_bmp = [None, None]
            rel_yb = [None, None]
            rel_st5 = [[None, None], [None, None]]
            v_ex_prev = 0

            for t5 in range(self.t5_count):
                sl = t5 % 2
                ya = y_ring[:, sl * DM:(sl + 1) * DM]
                xa = xn5[:, sl * DM:(sl + 1) * DM]
                c0 = sl * 8

                sx = L0 if sl == 0 else L1
                em.waitrel("sync", rel_y[sl])
                v_x = self.dma("sync", sx, ya, self.xq_in[t5 * 128:(t5 + 1) * 128, :])
                em.wait("vector", sx, v_x)
                em.wait("scalar", sx, v_x)
                em.waitrel("scalar", rel_xn5[sl])

                v_rstd, v_sq, v_std = self.ln_stats(
                    ya, st5, c0, rel_st5[sl][0], rel_st5[sl][1], dump=xa)

                em.waitrel("vector", rel_xn5[sl])
                v_xn = em.eop("vector", lambda e, o=xa, i_=ya, a=st5, c=c0:
                              e.tensor_scalar(o, i_, a[:, c + 1:c + 2], a[:, c + 7:c + 8],
                                              ALU.add, ALU.mult), inc=VV)

                em.wait("tensor", VV, v_xn)
                ev_last = 0
                for g in range(4):
                    bk = g % 2
                    em.waitrel("tensor", rel_tpb[bk])
                    for t8 in range(8):
                        dc = g * 8 + t8
                        fn = (lambda e, o=tpb[bk][:, t8 * 128:(t8 + 1) * 128],
                              i_=xa[:, dc * 128:(dc + 1) * 128], idn=self.idb_sb:
                              e.transpose(o, i_, idn[:, :]))
                        if t8 == 7:
                            v_t = em.eop("tensor", fn, inc=TT)
                        else:
                            em.eop("tensor", fn)
                    em.wait("vector", TT, v_t)
                    ev_last = em.eop("vector",
                                     lambda e, o=xnfm[:, g * 1024:(g + 1) * 1024],
                                     i_=tpb[bk]:
                                     e.tensor_copy(o, i_[:, :]), inc=VV)
                    rel_tpb[bk] = (VV, ev_last)

                em.wait("tensor", VV, ev_last)
                em.waitrel("tensor", rel_lgp)
                for dc in range(32):
                    fn = (lambda e, o=lgp[:, 0:NE], x=xnfm[:, dc * 128:(dc + 1) * 128],
                          r=self.rw_fm[:, dc * NE:(dc + 1) * NE],
                          stt=(dc == 0), sp=(dc == 31):
                          e.matmul(o, x, r, start=stt, stop=sp))
                    if dc == 31:
                        v_lg = em.eop("tensor", fn, inc=TT)
                    else:
                        em.eop("tensor", fn)

                em.wait("vector", TT, v_lg)
                em.wait("vector", LC, self.v_consts)
                v_lgs = self.chain("vector", lambda e, o=lg, i_=lgp, r=self.rb_sb:
                                   e.tensor_tensor(o[:, :], i_[:, 0:NE], r[:, :], ALU.add),
                                   VV)
                rel_lgp = (VV, v_lgs)

                em.wait("vector", AA, v_ex_prev)
                v_m = em.eop("vector", lambda e, o=cols5, i_=lg:
                             e.reduce_max(o[:, 0:1], i_[:, :], axis=AX.X, negate=True),
                             inc=VV)
                em.wait("scalar", VV, v_m)
                v_ex = em.eop("scalar", lambda e, o=ex5, i_=lg, b=cols5:
                              e.activation(o[:, :], i_[:, :], AF.Exp, bias=b[:, 0:1],
                                           accum_out=b[:, 1:2]), inc=AA)
                v_ex_prev = v_ex
                em.wait("vector", AA, v_ex)

                self.chain("vector", lambda e, o=wk5, i_=ex5:
                           e.tensor_copy(o[:, :], i_[:, :]), VV)
                for it in range(KTOP - 1):
                    self.chain("vector", lambda e, o=cols5, i_=wk5:
                               e.reduce_max(o[:, 2:3], i_[:, :], axis=AX.X), VV)
                    self.chain("vector", lambda e, o=eq5, i_=wk5, m=cols5:
                               e.tensor_scalar(o[:, :], i_[:, :], m[:, 2:3], 1.0e7,
                                               ALU.is_equal, ALU.mult), VV)
                    self.chain("vector", lambda e, o=wk5, i_=eq5:
                               e.tensor_tensor(o[:, :], o[:, :], i_[:, :], ALU.subtract), VV)
                self.chain("vector", lambda e, o=cols5, i_=wk5:
                           e.reduce_max(o[:, 3:4], i_[:, :], axis=AX.X), VV)
                self.chain("vector", lambda e, o=eq5, i_=ex5, t=cols5:
                           e.tensor_scalar(o[:, :], i_[:, :], t[:, 3:4], None, ALU.is_ge), VV)
                self.chain("vector", lambda e, o=wd5, a=ex5, b=eq5:
                           e.tensor_tensor(o[:, :], a[:, :], b[:, :], ALU.mult), VV)
                self.chain("vector", lambda e, o=cols5:
                           e.reciprocal(o[:, 4:5], o[:, 1:2]), VV)
                self.chain("vector", lambda e, o=wd5, r=cols5:
                           e.tensor_scalar(o[:, :], o[:, :], r[:, 4:5], None, ALU.mult,
                                           ALU.add, accum_out=r[:, 5:6]), VV)
                v_wd = em.eop("vector", lambda e, o=wdb, i_=wd5:
                              e.tensor_copy(o[:, :], i_[:, :]), inc=VV)

                em.wait("tensor", VV, v_wd)
                em.waitrel("tensor", rel_wdtp)
                v_wt = em.eop("tensor", lambda e, o=wdtp[0:64, 0:128],
                              i_=wdb[:, 0:NE], idn=self.idb_sb:
                              e.transpose(o, i_, idn[:, :]), inc=TT)
                em.wait("vector", TT, v_wt)
                v_wte = em.eop("vector", lambda e, o=wdt, i_=wdtp:
                               e.tensor_copy(o[:, :], i_[0:64, 0:128]), inc=VV)
                rel_wdtp = (VV, v_wte)

                self.chain("vector", lambda e, o=cols5, s=st5, c=c0:
                           e.tensor_tensor(o[:, 6:7], s[:, c + 7:c + 8], o[:, 5:6], ALU.mult), VV)
                em.eop("vector", lambda e, o=xnw, i_=ya, s=st5, c=c0, r=cols5:
                       e.tensor_scalar(o[:, :], i_, s[:, c + 1:c + 2], r[:, 6:7],
                                       ALU.add, ALU.mult))
                em.wait("vector", LC, v_bout)
                em.eop("vector", lambda e, o=xnw, b=bout_sb:
                       e.tensor_tensor(o[:, :], o[:, :], b[:, :], ALU.add))
                em.eop("vector", lambda e, o=ya, a=xnw:
                       e.tensor_tensor(o, o, a[:, :], ALU.add))

                for oc in range(8):
                    bk = oc % 2
                    em.wait("tensor", VV, v_wte)
                    em.waitrel("tensor", rel_bmp[bk])
                    v_bm = em.eop("tensor", lambda e, o=bmp[bk][:, :], w=wdt,
                                  b=self.eb_bf[:, oc * 512:(oc + 1) * 512]:
                                  e.matmul(o, w[:, :], b, start=True, stop=True), inc=TT)

                    ybsl = oc % 2
                    sy = L2 if ybsl == 0 else L3
                    em.wait("sync", CC, self.rs_done[oc])
                    em.waitrel("sync", rel_yb[ybsl])
                    v_yr = self.dma("sync", sy, yb[:, ybsl * 512:(ybsl + 1) * 512],
                                    self.yred[t5 * 128:(t5 + 1) * 128,
                                              oc * 512:(oc + 1) * 512])

                    em.wait("vector", sy, v_yr)
                    em.wait("vector", TT, v_bm)
                    em.eop("vector", lambda e, o=ya[:, oc * 512:(oc + 1) * 512],
                           r=yb[:, ybsl * 512:(ybsl + 1) * 512]:
                           e.tensor_tensor(o, o, r, ALU.add))
                    v_add = em.eop("vector", lambda e, o=ya[:, oc * 512:(oc + 1) * 512],
                                   b=bmp[bk][:, :]:
                                   e.tensor_tensor(o, o, b, ALU.add), inc=VV)
                    rel_bmp[bk] = (VV, v_add)
                    rel_yb[ybsl] = (VV, v_add)

                v_fin = em.cnt[VV]
                rel_xn5[sl] = (TT, em.cnt[TT])
                rel_st5[sl][0] = (AA, v_std)
                rel_st5[sl][1] = (VV, v_fin)
                gy = G0 if sl == 0 else G1
                em.wait("gpsimd", VV, v_fin)
                v_store = self.dma("gpsimd", gy, self.y_out[t5 * 128:(t5 + 1) * 128, :], ya)
                rel_y[sl] = (gy, v_store)

            em.wait("gpsimd", G0, em.cnt[G0])
            em.wait("gpsimd", G1, em.cnt[G1])

    def emit_debug(self):
        em = self.em
        if not self.dbg:
            return
        em.barrier()
        src_map = {"dbg_xn": self.xn_sc, "dbg_wq": self.wq_sc, "dbg_qkv": self.qkv_sc,
                   "dbg_ao": self.ao_sc, "dbg_wout": self.wout_bf}
        if self.upto >= 4:
            src_map["dbg_ypart"] = self.ypart
            src_map["dbg_yred"] = self.yred
        for name, dst in self.dbg.items():
            src = src_map[name]
            v = self.dma("gpsimd", G0, dst.ap(), src.ap())
            em.wait("gpsimd", G0, v)



def _alibi_slopes(n_heads=32, alibi_bias_max=8):
    _n = 2 ** math.ceil(math.log2(n_heads))
    m = np.arange(1, _n + 1, dtype=np.float64) * (alibi_bias_max / _n)
    slopes = 1.0 / np.power(2.0, m)
    if _n != n_heads:
        slopes = np.concatenate([slopes[1::2], slopes[::2]])[:n_heads]
    return slopes


def _build_const_inputs(heads):
    bf = ml_dtypes.bfloat16
    slopes = _alibi_slopes()
    pos = np.arange(S, dtype=np.float64) - (S - 1)
    al3 = np.zeros((3 * NHL, S), dtype=bf)
    for i, h in enumerate(heads):
        a = slopes[h] * pos
        hi = a.astype(np.float32).astype(bf)
        r1 = a - hi.astype(np.float64)
        mid = r1.astype(np.float32).astype(bf)
        r2 = r1 - mid.astype(np.float64)
        lo = r2.astype(np.float32).astype(bf)
        al3[3 * i + 0] = hi
        al3[3 * i + 1] = mid
        al3[3 * i + 2] = lo

    masks = np.zeros((4 * 128, 512), dtype=np.float32)
    ii = np.arange(128)[:, None]
    jj = np.arange(512)[None, :]
    for v in range(4):
        wd = (v + 1) * 128
        m = np.where(jj <= (wd - 128) + ii, 0.0, np.float32(NEG_BIG)).astype(np.float32)
        m[:, wd:] = 0.0
        masks[v * 128:(v + 1) * 128] = m

    ones3 = np.ones((3, 128), dtype=bf)
    idb = np.eye(128, dtype=np.float32).astype(bf)
    return al3, masks, ones3, idb


_BUILD_CACHE = {}


def get_program(upto=5, debug=(), t5_count=8, p5b_stage=3):
    key = (upto, tuple(sorted(debug)), t5_count, p5b_stage)
    if key not in _BUILD_CACHE:
        _BUILD_CACHE[key] = Builder(upto=upto, debug=debug, t5_count=t5_count,
                                    p5b_stage=p5b_stage).build()
    return _BUILD_CACHE[key]


def make_in_maps(L0_attn_x, L0_attn_w_qkv, L0_attn_b_qkv, L0_attn_w_out,
                 L0_attn_b_out, L0_ffn_router_weight, L0_ffn_router_bias,
                 L0_ffn_expert_bias):
    x = np.ascontiguousarray(np.asarray(L0_attn_x, dtype=np.float32))
    wqkv = np.asarray(L0_attn_w_qkv, dtype=np.float32)
    bqkv = np.asarray(L0_attn_b_qkv, dtype=np.float32)
    wout = np.asarray(L0_attn_w_out, dtype=np.float32)
    bout = np.asarray(L0_attn_b_out, dtype=np.float32)
    rw = np.ascontiguousarray(np.asarray(L0_ffn_router_weight, dtype=np.float32))
    rb = np.asarray(L0_ffn_router_bias, dtype=np.float32)
    eb = np.ascontiguousarray(np.asarray(L0_ffn_expert_bias, dtype=np.float32))

    in_maps = []
    for c in range(8):
        b, g = c // 4, c % 4
        heads = [g + 4 * j for j in range(NHL)]
        rows = []
        for kind in range(3):
            for h in heads:
                rows.append(wqkv[kind * DM + 128 * h: kind * DM + 128 * (h + 1)])
        wqkv_c = np.ascontiguousarray(np.concatenate(rows, axis=0))
        bq = []
        for kind in range(3):
            for h in heads:
                bq.append(bqkv[kind * DM + 128 * h: kind * DM + 128 * (h + 1)])
        bqkv_c = np.ascontiguousarray(np.stack(bq, axis=1).astype(np.float32))
        wout_c = np.ascontiguousarray(
            np.concatenate([wout[:, 128 * h:128 * (h + 1)] for h in heads], axis=1))
        al3, masks, ones3, idb = _build_const_inputs(heads)
        in_maps.append({
            "x": x[b],
            "xq": np.ascontiguousarray(x[b, 1024 * g:1024 * (g + 1)]),
            "wqkv": wqkv_c,
            "bqkv": bqkv_c,
            "wout": wout_c,
            "bout_bc": np.ascontiguousarray(np.tile(bout[None, :], (128, 1))),
            "rw": rw,
            "rb_bc": np.ascontiguousarray(np.tile(rb[None, :], (128, 1))),
            "eb": eb,
            "alibi3": al3,
            "ones3": ones3,
            "masks": masks,
            "idb": idb,
        })
    return in_maps


def kernel(L0_attn_x, L0_attn_w_qkv, L0_attn_b_qkv, L0_attn_w_out, L0_attn_b_out,
           L0_ffn_router_weight, L0_ffn_router_bias, L0_ffn_expert_bias):
    nc = get_program()
    in_maps = make_in_maps(L0_attn_x, L0_attn_w_qkv, L0_attn_b_qkv, L0_attn_w_out,
                           L0_attn_b_out, L0_ffn_router_weight, L0_ffn_router_bias,
                           L0_ffn_expert_bias)
    res = run_bass_kernel_spmd(nc, in_maps, core_ids=list(range(8)))
    out = np.empty((2, S, DM), dtype=np.float32)
    for c in range(8):
        b, g = c // 4, c % 4
        out[b, 1024 * g:1024 * (g + 1), :] = res.results[c]["y"]
    return out


# version 10
# speedup vs baseline: 1.1141x; 1.1141x over previous
import math
import contextlib

import numpy as np
import ml_dtypes

import concourse.bass as bass
import concourse.mybir as mybir
from concourse.bass_utils import run_bass_kernel_spmd

F32 = mybir.dt.float32
BF16 = mybir.dt.bfloat16
AF = mybir.ActivationFunctionType
ALU = mybir.AluOpType
AX = mybir.AxisListType

S = 4096
DM = 4096
HD = 128
NHL = 8
NCHUNK = 3 * NHL
NT = S // 128
NTBLK = 8
NQT = 32
NE = 64
KTOP = 8
Q_SCALE = 1.0 / math.sqrt(HD)
NEG_BIG = -1.0e9
EPS = 1e-5
RS_GROUPS = [[0, 1, 2, 3], [4, 5, 6, 7]]

def _slot_window(j):
    slope = 2.0 ** (-(4 * j + 3 + 1) / 4.0)
    w = 384.0 / slope
    if w >= S:
        return S
    return int(math.ceil(w / 512.0) * 512)

SLOT_W = [_slot_window(j) for j in range(NHL)]

L0, L1, L2, L3, L4, L5, LC = "l0", "l1", "l2", "l3", "l4", "l5", "lc"
G0, G1, G2, G3 = "g0", "g1", "g2", "g3"
TT, VV, AA, CC = "tt", "vv", "aa", "cc"
ALL_SEMS = (L0, L1, L2, L3, L4, L5, LC, G0, G1, G2, G3, TT, VV, AA, CC)


class Em:

    ENGS = ("sync", "tensor", "vector", "scalar", "gpsimd")

    def __init__(self):
        self.ops = {e: [] for e in self.ENGS}
        self.cnt = {n: 0 for n in ALL_SEMS}
        self.lastwait = {e: {} for e in self.ENGS}

    def inc(self, sem, by=1):
        self.cnt[sem] += by
        return self.cnt[sem]

    def wait(self, eng, sem, val):
        if val is None or val <= 0:
            return
        if self.lastwait[eng].get(sem, 0) >= val:
            return
        self.lastwait[eng][sem] = val

        def f(e, SM, sem=sem, val=val):
            e.wait_ge(SM[sem], val)

        self.ops[eng].append(f)

    def waitrel(self, eng, rel):
        if rel is None:
            return
        if isinstance(rel, list):
            for r in rel:
                self.waitrel(eng, r)
        else:
            self.wait(eng, rel[0], rel[1])

    def op(self, eng, fn):
        self.ops[eng].append(lambda e, SM, fn=fn: fn(e))

    def eop(self, eng, fn, inc=None):
        if inc is None:
            self.op(eng, fn)
            return None
        v = self.inc(inc)

        def g(e, SM, fn=fn, s=inc):
            fn(e).then_inc(SM[s], 1)

        self.ops[eng].append(g)
        return v

    def barrier(self):
        snap = dict(self.cnt)
        for e in self.ENGS:
            for s, v in snap.items():
                self.wait(e, s, v)


class Builder:
    def __init__(self, upto=5, debug=(), t5_count=8, p5b_stage=3):
        self.upto = upto
        self.t5_count = t5_count
        self.p5b_stage = p5b_stage
        self.debug = debug
        self.nc = bass.Bass("TRN2", target_bir_lowering=False)
        self.em = Em()
        self.stack = contextlib.ExitStack()

    def sbuf(self, name, shape, dtype, st=None, tdest=False, side=None):
        return (st or self.stack).enter_context(
            self.nc.sbuf_tensor(name, shape, dtype, is_transpose_dest=tdest,
                                side=side))

    def psum(self, name, shape, dtype, st=None):
        return (st or self.stack).enter_context(self.nc.psum_tensor(name, shape, dtype))

    def chain(self, eng, fn, sem):
        v = self.em.eop(eng, fn, inc=sem)
        self.em.wait(eng, sem, v)
        return v

    def dma(self, eng, sem, out, in_, transpose=False):
        v = self.em.inc(sem, 16)

        if transpose:
            def f(e, SM, o=out, i=in_, s=sem):
                e.dma_start_transpose(out=o, in_=i).then_inc(SM[s], 16)
        else:
            def f(e, SM, o=out, i=in_, s=sem):
                e.dma_start(out=o, in_=i).then_inc(SM[s], 16)

        self.em.ops[eng].append(f)
        return v

    def build(self):
        nc = self.nc
        em = self.em

        self.x_in = nc.dram_tensor("x", [S, DM], F32, kind="ExternalInput")
        self.xq_in = nc.dram_tensor("xq", [1024, DM], F32, kind="ExternalInput")
        self.wqkv_in = nc.dram_tensor("wqkv", [NCHUNK * 128, DM], F32, kind="ExternalInput")
        self.bqkv_in = nc.dram_tensor("bqkv", [128, NCHUNK], F32, kind="ExternalInput")
        self.wout_in = nc.dram_tensor("wout", [DM, NHL * 128], F32, kind="ExternalInput")
        self.bout_in = nc.dram_tensor("bout_bc", [128, DM], F32, kind="ExternalInput")
        self.rw_in = nc.dram_tensor("rw", [NE, DM], F32, kind="ExternalInput")
        self.rb_in = nc.dram_tensor("rb_bc", [128, NE], F32, kind="ExternalInput")
        self.eb_in = nc.dram_tensor("eb", [NE, DM], F32, kind="ExternalInput")
        self.alibi3_in = nc.dram_tensor("alibi3", [3 * NHL, S], BF16, kind="ExternalInput")
        self.ones3_in = nc.dram_tensor("ones3", [3, 128], BF16, kind="ExternalInput")
        self.masks_in = nc.dram_tensor("masks", [4 * 128, 512], F32, kind="ExternalInput")
        self.idb_in = nc.dram_tensor("idb", [128, 128], BF16, kind="ExternalInput")
        self.y_out = nc.dram_tensor("y", [1024, DM], F32, kind="ExternalOutput")

        self.xn_sc = nc.dram_tensor("xn_sc", [S, DM], BF16)
        self.wq_sc = nc.dram_tensor("wq_sc", [128, NCHUNK * 32 * 128], BF16)
        self.qkv_sc = nc.dram_tensor("qkv_sc", [NCHUNK * 128, S], BF16)
        self.ao_sc = nc.dram_tensor("ao_sc", [NHL * 128, S], BF16)
        self.wout_bf = nc.dram_tensor("wout_bf", [DM, NHL * 128], BF16)
        self.ypart = nc.dram_tensor("ypart", [S, DM], BF16)
        self.ysp = nc.dram_tensor("ysp", [1024, DM], F32)
        self.yred = nc.dram_tensor("yred", [1024, DM], BF16)

        dbg = {}
        for name, shape, dt in (("dbg_xn", [S, DM], BF16),
                                ("dbg_wq", [128, NCHUNK * 32 * 128], BF16),
                                ("dbg_qkv", [NCHUNK * 128, S], BF16),
                                ("dbg_ao", [NHL * 128, S], BF16),
                                ("dbg_wout", [DM, NHL * 128], BF16),
                                ("dbg_ypart", [S, DM], BF16),
                                ("dbg_yred", [1024, DM], BF16)):
            if name in self.debug:
                dbg[name] = nc.dram_tensor(name, shape, dt, kind="ExternalOutput")
        self.dbg = dbg

        with self.stack:
            sems = {}
            for n in ALL_SEMS:
                sems[n] = self.stack.enter_context(nc.semaphore(n))

            self.masks_sb = self.sbuf("masks_sb", [128, 4 * 512], F32)
            self.idb_sb = self.sbuf("idb_sb", [128, 128], BF16)
            self.ones3_sb = self.sbuf("ones3_sb", [3, 128], BF16)
            self.bqs_sb = self.sbuf("bqs_sb", [128, NCHUNK], F32)
            self.rw_fm = self.sbuf("rw_fm", [128, 32 * NE], BF16)
            self.eb_bf = self.sbuf("eb_bf", [NE, DM], BF16)
            self.rb_sb = self.sbuf("rb_sb", [128, NE], F32)

            for v in range(4):
                self.dma("sync", LC, self.masks_sb[:, v * 512:(v + 1) * 512],
                         self.masks_in[v * 128:(v + 1) * 128, :])
            self.dma("sync", LC, self.idb_sb[:, :], self.idb_in[:, :])
            self.dma("sync", LC, self.ones3_sb[:, :], self.ones3_in[:, :])
            self.v_consts = self.dma("sync", LC, self.rb_sb[:, :], self.rb_in[:, :])

            self.phase1()
            em.barrier()
            if self.upto >= 2:
                self.phase2()
                em.barrier()
            if self.upto >= 3:
                self.phase3()
                em.barrier()
            if self.upto >= 4:
                self.phase5a()
            if self.upto >= 5:
                self.phase5b()
            elif self.upto == 4:
                self.emit_collectives()
                self.p5.close()

            self.emit_debug()
            em.barrier()

            with nc.Block() as block:
                ops = em.ops

                @block.sync
                def _(e):
                    for f in ops["sync"]:
                        f(e, sems)

                @block.tensor
                def _(e):
                    for f in ops["tensor"]:
                        f(e, sems)

                @block.vector
                def _(e):
                    for f in ops["vector"]:
                        f(e, sems)

                @block.scalar
                def _(e):
                    for f in ops["scalar"]:
                        f(e, sems)

                @block.gpsimd
                def _(e):
                    for f in ops["gpsimd"]:
                        f(e, sems)

        return nc

    def ln_stats(self, xa, stat, c0, prev_act_rel, prev_vec_rel, dump):
        em = self.em
        em.waitrel("vector", prev_act_rel)

        self.chain("vector", lambda e, o=stat, c=c0, i_=xa:
                   e.reduce_sum(o[:, c:c + 1], i_, axis=AX.X, negate=True), VV)
        self.chain("vector", lambda e, o=stat, c=c0:
                   e.tensor_scalar(o[:, c + 1:c + 2], o[:, c:c + 1], 1.0 / DM, None, ALU.mult), VV)

        em.waitrel("scalar", prev_vec_rel)

        v_sq = em.eop("scalar", lambda e, o=dump, i_=xa, a=stat, c=c0:
                      e.activation(o, i_, AF.Square, accum_out=a[:, c + 2:c + 3]),
                      inc=AA)

        em.wait("vector", AA, v_sq)
        self.chain("vector", lambda e, o=stat, c=c0:
                   e.tensor_scalar(o[:, c + 3:c + 4], o[:, c + 2:c + 3], 1.0 / DM, None, ALU.mult), VV)
        self.chain("vector", lambda e, o=stat, c=c0:
                   e.tensor_tensor(o[:, c + 4:c + 5], o[:, c + 1:c + 2], o[:, c + 1:c + 2], ALU.mult), VV)
        self.chain("vector", lambda e, o=stat, c=c0:
                   e.tensor_tensor(o[:, c + 5:c + 6], o[:, c + 3:c + 4], o[:, c + 4:c + 5], ALU.subtract), VV)
        v_var = self.chain("vector", lambda e, o=stat, c=c0:
                           e.tensor_scalar(o[:, c + 5:c + 6], o[:, c + 5:c + 6], EPS, None, ALU.add), VV)

        em.wait("scalar", VV, v_var)
        v_std = em.eop("scalar", lambda e, o=stat, c=c0:
                       e.activation(o[:, c + 6:c + 7], o[:, c + 5:c + 6], AF.Sqrt), inc=AA)

        em.wait("vector", AA, v_std)
        v_rstd = self.chain("vector", lambda e, o=stat, c=c0:
                            e.reciprocal(o[:, c + 7:c + 8], o[:, c + 6:c + 7]), VV)
        return v_rstd, v_sq, v_std

    def phase1(self):
        em = self.em

        with contextlib.ExitStack() as p1:
            x_ring = self.sbuf("p1_x", [128, 2 * DM], F32, p1)
            xn_ring = self.sbuf("p1_xn", [128, 2 * DM], BF16, p1)
            stat = self.sbuf("p1_stat", [128, 16], F32, p1)
            wrow_f = self.sbuf("p1_wf", [128, 2 * DM], F32, p1)
            wrow_b = self.sbuf("p1_wb", [128, 2 * DM], BF16, p1)
            wqe = self.sbuf("p1_wqe", [128, 2 * 1024], BF16, p1)
            bq_raw = self.sbuf("p1_bq", [128, NCHUNK], F32, p1)
            tp_bank = [self.psum("p1_tp0", [128, 1024], BF16, p1),
                       self.psum("p1_tp1", [128, 1024], BF16, p1)]

            v_bq = self.dma("sync", LC, bq_raw[:, :], self.bqkv_in[:, :])
            em.wait("vector", LC, v_bq)
            self.chain("vector", lambda e, o=self.bqs_sb, i=bq_raw:
                       e.tensor_scalar(o[:, 0:NCHUNK], i[:, 0:NCHUNK], 1.0, None, ALU.mult), VV)
            self.chain("vector", lambda e, o=self.bqs_sb, i=bq_raw:
                       e.tensor_scalar(o[:, 0:NHL], i[:, 0:NHL], Q_SCALE, None, ALU.mult), VV)

            rel_x = [None, None]
            rel_xn = [None, None]
            rel_stat_a = [None, None]
            rel_stat_v = [None, None]
            rel_wf = [None, None]
            rel_wb = [None, None]
            rel_wqe = [None, None]
            rel_tp = [None, None]

            for i in range(NT):
                sl = i % 2
                xa = x_ring[:, sl * DM:(sl + 1) * DM]
                xna = xn_ring[:, sl * DM:(sl + 1) * DM]
                c0 = sl * 8

                sx = L0 if sl == 0 else L1
                em.waitrel("sync", rel_x[sl])
                v_ld = self.dma("sync", sx, xa, self.x_in[i * 128:(i + 1) * 128, :])
                em.wait("vector", sx, v_ld)
                em.wait("scalar", sx, v_ld)
                em.waitrel("scalar", rel_xn[sl])

                v_rstd, v_sq, v_std = self.ln_stats(
                    xa, stat, c0, rel_stat_a[sl], rel_stat_v[sl], dump=xna)

                em.waitrel("vector", rel_xn[sl])
                v_xn = em.eop("vector", lambda e, o=xna, i_=xa, a=stat, c=c0:
                              e.tensor_scalar(o, i_, a[:, c + 1:c + 2], a[:, c + 7:c + 8],
                                              ALU.add, ALU.mult), inc=VV)
                rel_x[sl] = [(VV, v_xn), (AA, v_sq)]
                rel_stat_a[sl] = (AA, v_std)
                rel_stat_v[sl] = (VV, v_xn)

                gx = G0 if sl == 0 else G1
                em.wait("gpsimd", VV, v_xn)
                v_w = self.dma("gpsimd", gx, self.xn_sc[i * 128:(i + 1) * 128, :], xna)
                rel_xn[sl] = (gx, v_w)

                if i < NCHUNK:
                    j = i
                    ws = j % 2
                    wfa = wrow_f[:, ws * DM:(ws + 1) * DM]
                    wba = wrow_b[:, ws * DM:(ws + 1) * DM]
                    sw = L2 if ws == 0 else L3
                    em.waitrel("sync", rel_wf[ws])
                    v_wl = self.dma("sync", sw, wfa, self.wqkv_in[j * 128:(j + 1) * 128, :])
                    em.wait("scalar", sw, v_wl)
                    em.waitrel("scalar", rel_wb[ws])
                    v_cast = em.eop("scalar", lambda e, o=wba, i_=wfa:
                                    e.activation(o, i_, AF.Identity), inc=AA)
                    rel_wf[ws] = (AA, v_cast)

                    em.wait("tensor", AA, v_cast)
                    v_tp_last = 0
                    for g in range(4):
                        bk = (j * 4 + g) % 2
                        es = (j * 4 + g) % 2
                        em.waitrel("tensor", rel_tp[bk])
                        for t8 in range(8):
                            t = g * 8 + t8
                            fn = (lambda e, o=tp_bank[bk][:, t8 * 128:(t8 + 1) * 128],
                                  i_=wba[:, t * 128:(t + 1) * 128], idn=self.idb_sb:
                                  e.transpose(o, i_, idn[:, :]))
                            if t8 == 7:
                                v_tp = em.eop("tensor", fn, inc=TT)
                            else:
                                em.eop("tensor", fn)
                        v_tp_last = v_tp
                        em.wait("vector", TT, v_tp)
                        em.waitrel("vector", rel_wqe[es])
                        dst = wqe[:, es * 1024:(es + 1) * 1024]
                        v_ev = em.eop("vector", lambda e, o=dst, i_=tp_bank[bk]:
                                      e.tensor_copy(o, i_[:, :]), inc=VV)
                        rel_tp[bk] = (VV, v_ev)
                        ge = G2 if es == 0 else G3
                        em.wait("gpsimd", VV, v_ev)
                        col = (j * 32 + g * 8) * 128
                        v_d = self.dma("gpsimd", ge,
                                       self.wq_sc[:, col:col + 1024], dst)
                        rel_wqe[es] = (ge, v_d)
                    rel_wb[ws] = (TT, v_tp_last)

    def phase2(self):
        em = self.em
        with contextlib.ExitStack() as p2:
            xnb = self.sbuf("p2_xnb", [128, 2 * 32 * 512], BF16, p2, tdest=True)
            panel = self.sbuf("p2_panel", [128, 8 * 32 * 128], BF16, p2)
            strip = self.sbuf("p2_strip", [128, 2 * 8 * 512], BF16, p2)
            banks = [self.psum(f"p2_b{i}", [128, 512], F32, p2) for i in range(8)]

            wof = self.sbuf("p2_wof", [128, 2 * 1024], F32, p2)
            wob = self.sbuf("p2_wob", [128, 2 * 1024], BF16, p2)

            rel_xnb = [None, None]
            rel_strip = [None, None]
            bank_rel = [None] * 8
            rel_panel = None
            rel_wof = [None, None]
            rel_wob = [None, None]

            def wout_prep(j):
                ws = j % 2
                sw = L2 if ws == 0 else L3
                gw = G2 if ws == 0 else G3
                wfa = wof[:, ws * 1024:(ws + 1) * 1024]
                wba = wob[:, ws * 1024:(ws + 1) * 1024]
                em.waitrel("sync", rel_wof[ws])
                v_wl = self.dma("sync", sw, wfa, self.wout_in[j * 128:(j + 1) * 128, :])
                em.wait("scalar", sw, v_wl)
                em.waitrel("scalar", rel_wob[ws])
                v_c = em.eop("scalar", lambda e, o=wba, i_=wfa:
                             e.activation(o, i_, AF.Identity), inc=AA)
                rel_wof[ws] = (AA, v_c)
                em.wait("gpsimd", AA, v_c)
                v_d = self.dma("gpsimd", gw,
                               self.wout_bf[j * 128:(j + 1) * 128, :], wba)
                rel_wob[ws] = (gw, v_d)

            def issue_loads(tb):
                sl = tb % 2
                sem = L0 if sl == 0 else L1
                em.waitrel("sync", rel_xnb[sl])
                v = 0
                for dc in range(32):
                    dst = xnb[:, (sl * 32 + dc) * 512:(sl * 32 + dc + 1) * 512]
                    src = self.xn_sc[tb * 512:(tb + 1) * 512, dc * 128:(dc + 1) * 128]
                    v = self.dma("sync", sem, dst, src, transpose=True)
                return (sem, v)

            for wpass in range(3):
                em.waitrel("sync", rel_panel)
                v_pan = self.dma("sync", L5, panel[:, :],
                                 self.wq_sc[:, wpass * 32768:(wpass + 1) * 32768])
                cur = issue_loads(0)
                for tb in range(NTBLK):
                    sl = tb % 2
                    if tb + 1 < NTBLK:
                        nxt = issue_loads(tb + 1)
                    else:
                        nxt = None
                    v_fin = [0] * 8
                    em.wait("tensor", L5, v_pan)
                    em.waitrel("tensor", cur)
                    for dc in range(32):
                        xa = xnb[:, (sl * 32 + dc) * 512:(sl * 32 + dc + 1) * 512]
                        for o8 in range(8):
                            oc = wpass * 8 + o8
                            if dc == 0:
                                em.waitrel("tensor", bank_rel[o8])
                            wsl = panel[:, (o8 * 32 + dc) * 128:(o8 * 32 + dc + 1) * 128]
                            fn = (lambda e, o=banks[o8][:, :], w=wsl, x=xa,
                                  st=(dc == 0), sp=(dc == 31):
                                  e.matmul(o, w, x, start=st, stop=sp))
                            if dc == 31:
                                v_fin[o8] = em.eop("tensor", fn, inc=TT)
                            else:
                                em.eop("tensor", fn)
                    rel_xnb[sl] = (TT, v_fin[7])

                    ssl = tb % 2
                    v_d_last = 0
                    for o8 in range(8):
                        oc = wpass * 8 + o8
                        em.wait("scalar", TT, v_fin[o8])
                        em.waitrel("scalar", rel_strip[ssl])
                        dsts = strip[:, (ssl * 8 + o8) * 512:(ssl * 8 + o8 + 1) * 512]
                        scl = Q_SCALE if wpass == 0 else 1.0
                        v_ev = em.eop("scalar",
                                      lambda e, o=dsts, i_=banks[o8][:, :],
                                      b=self.bqs_sb[:, oc:oc + 1], s=scl:
                                      e.activation(o, i_, AF.Identity, bias=b, scale=s),
                                      inc=AA)
                        bank_rel[o8] = (AA, v_ev)
                        em.wait("gpsimd", AA, v_ev)
                        gs = G0 if ssl == 0 else G1
                        v_d_last = self.dma(
                            "gpsimd", gs,
                            self.qkv_sc[oc * 128:(oc + 1) * 128, tb * 512:(tb + 1) * 512],
                            dsts)
                    rel_strip[ssl] = (gs, v_d_last)
                    slot = wpass * NTBLK + tb
                    if slot < 16:
                        wout_prep(2 * slot)
                        wout_prep(2 * slot + 1)
                    cur = nxt
                rel_panel = (TT, em.cnt[TT])

    def phase3(self):
        em = self.em
        with contextlib.ExitStack() as p3:
            q_sb = self.sbuf("p3_q", [128, 2 * S], BF16, p3)
            k_sb = self.sbuf("p3_k", [128, 2 * S], BF16, p3)
            v_sb = self.sbuf("p3_v", [128, 2 * S], BF16, p3)
            vt_sb = self.sbuf("p3_vt", [128, 2 * 32 * 128], BF16, p3)
            al_sb = self.sbuf("p3_al", [3, 2 * S], BF16, p3)
            srow = self.sbuf("p3_srow", [128, 3 * S], F32, p3)
            prow = self.sbuf("p3_prow", [128, 3 * S], BF16, p3)
            pt_sb = self.sbuf("p3_pt", [128, 32 * 128], BF16, p3)
            ao_sb = self.sbuf("p3_ao", [128, 2 * S], BF16, p3)
            zc = self.sbuf("p3_zc", [128, 12], F32, p3)

            sc_banks = [self.psum(f"p3_s{i}", [128, 512], F32, p3) for i in range(3)]
            tp_banks = [self.psum("p3_t0", [128, 1024], BF16, p3),
                        self.psum("p3_t1", [128, 1024], BF16, p3)]
            ao_banks = [self.psum("p3_aob0", [128, 512], F32, p3),
                        self.psum("p3_aob1", [128, 512], F32, p3)]

            st = dict(rel_pair=None, rel_sc=[None, None, None],
                      rel_tp=[None, None], rel_ao_bank=[None, None],
                      rel_ao_dma=[None, None],
                      sc_ctr=0, tp_ctr=0,
                      v_exp=[0, 0, 0], v_tpg=[0, 0, 0], v_max=[0, 0, 0],
                      v_nrm=[0, 0, 0])

            def load_pair(p):
                sem = L0 if p % 2 == 0 else L1
                em.waitrel("sync", st["rel_pair"])
                v = 0
                for hw in range(2):
                    h = 2 * p + hw
                    self.dma("sync", sem, q_sb[:, hw * S:(hw + 1) * S],
                             self.qkv_sc[h * 128:(h + 1) * 128, :])
                    self.dma("sync", sem, k_sb[:, hw * S:(hw + 1) * S],
                             self.qkv_sc[(NHL + h) * 128:(NHL + h + 1) * 128, :])
                    self.dma("sync", sem, v_sb[:, hw * S:(hw + 1) * S],
                             self.qkv_sc[(2 * NHL + h) * 128:(2 * NHL + h + 1) * 128, :])
                    v = self.dma("sync", sem, al_sb[:, hw * S:(hw + 1) * S],
                                 self.alibi3_in[3 * h:3 * h + 3, :])
                return (sem, v)

            def v_transposes(v_l):
                em.waitrel("tensor", v_l)
                for hw in range(2):
                    for g in range(4):
                        bk = st["tp_ctr"] % 2
                        st["tp_ctr"] += 1
                        em.waitrel("tensor", st["rel_tp"][bk])
                        for t8 in range(8):
                            t = g * 8 + t8
                            fn = (lambda e, o=tp_banks[bk][:, t8 * 128:(t8 + 1) * 128],
                                  i_=v_sb[:, hw * S + t * 128:hw * S + (t + 1) * 128],
                                  idn=self.idb_sb:
                                  e.transpose(o, i_, idn[:, :]))
                            if t8 == 7:
                                v_t = em.eop("tensor", fn, inc=TT)
                            else:
                                em.eop("tensor", fn)
                        em.wait("vector", TT, v_t)
                        v_e = em.eop("vector",
                                     lambda e,
                                     o=vt_sb[:, (hw * 4 + g) * 1024:(hw * 4 + g + 1) * 1024],
                                     i_=tp_banks[bk]:
                                     e.tensor_copy(o, i_[:, :]), inc=VV)
                        st["rel_tp"][bk] = (VV, v_e)

            def emit_scores_evac_max(it):
                i, qt, hw = it[0], it[1], it[2]
                L = (qt + 1) * 128
                k_lo = it[3]
                nk = (L - k_lo + 511) // 512
                psl = i % 3
                for j in range(nk):
                    c0 = k_lo + j * 512
                    w = min(512, L - c0)
                    bk = st["sc_ctr"] % 3
                    st["sc_ctr"] += 1
                    em.waitrel("tensor", st["rel_sc"][bk])
                    qa = q_sb[:, hw * S + qt * 128:hw * S + (qt + 1) * 128]
                    ka = k_sb[:, hw * S + c0:hw * S + c0 + w]
                    aa_ = al_sb[:, hw * S + c0:hw * S + c0 + w]
                    em.eop("tensor", lambda e, o=sc_banks[bk][:, 0:w], q=qa, k=ka:
                           e.matmul(o, q, k, start=True, stop=False))
                    v_mm = em.eop("tensor",
                                  lambda e, o=sc_banks[bk][:, 0:w], n=self.ones3_sb, a=aa_:
                                  e.matmul(o, n[:, :], a, start=False, stop=True), inc=TT)
                    dst = srow[:, psl * S + (c0 - k_lo):psl * S + (c0 - k_lo) + w]
                    if j == nk - 1:
                        mv = qt % 4
                        em.wait("vector", TT, v_mm)
                        em.wait("vector", AA, st["v_exp"][psl])
                        v = em.eop("vector",
                                   lambda e, o=dst, i_=sc_banks[bk][:, 0:w],
                                   m=self.masks_sb[:, mv * 512:mv * 512 + w]:
                                   e.tensor_tensor(o, i_, m, ALU.add), inc=VV)
                        st["rel_sc"][bk] = (VV, v)
                    else:
                        em.wait("scalar", TT, v_mm)
                        v = em.eop("scalar", lambda e, o=dst, i_=sc_banks[bk][:, 0:w]:
                                   e.activation(o, i_, AF.Identity), inc=AA)
                        st["rel_sc"][bk] = (AA, v)
                em.wait("vector", AA, em.cnt[AA])
                em.wait("vector", AA, st["v_exp"][psl])
                v_max = em.eop("vector",
                               lambda e, o=zc[:, psl * 4:psl * 4 + 1],
                               i_=srow[:, psl * S:psl * S + (L - k_lo)]:
                               e.reduce_max(o, i_, axis=AX.X, negate=True), inc=VV)
                st["v_max"][psl] = v_max

            def emit_exp_norm(it):
                i, qt, hw = it[0], it[1], it[2]
                L = (qt + 1) * 128 - it[3]
                psl = i % 3
                em.wait("scalar", VV, st["v_max"][psl])
                em.wait("scalar", TT, st["v_tpg"][psl])
                v_exp = em.eop("scalar",
                               lambda e, o=prow[:, psl * S:psl * S + L],
                               i_=srow[:, psl * S:psl * S + L],
                               b=zc[:, psl * 4:psl * 4 + 1],
                               a=zc[:, psl * 4 + 1:psl * 4 + 2]:
                               e.activation(o, i_, AF.Exp, bias=b, accum_out=a), inc=AA)
                st["v_exp"][psl] = v_exp

                em.wait("vector", AA, v_exp)
                self.chain("vector", lambda e, o=zc[:, psl * 4 + 2:psl * 4 + 3],
                           i_=zc[:, psl * 4 + 1:psl * 4 + 2]:
                           e.reciprocal(o, i_), VV)
                v_nrm = em.eop("vector",
                               lambda e, o=prow[:, psl * S:psl * S + L],
                               s=zc[:, psl * 4 + 2:psl * 4 + 3]:
                               e.tensor_scalar(o, o, s, None, ALU.mult), inc=VV)
                st["v_nrm"][psl] = v_nrm

            def emit_tp_av(it):
                i, qt, hw = it[0], it[1], it[2]
                k_lo = it[3]
                L = (qt + 1) * 128 - k_lo
                T = L // 128
                psl = i % 3
                em.wait("tensor", VV, st["v_nrm"][psl])
                v_e_last = 0
                v_tp_last = 0
                for g0 in range(0, T, 8):
                    gn = min(8, T - g0)
                    bk = st["tp_ctr"] % 2
                    st["tp_ctr"] += 1
                    em.waitrel("tensor", st["rel_tp"][bk])
                    for t8 in range(gn):
                        t = g0 + t8
                        fn = (lambda e, o=tp_banks[bk][:, t8 * 128:(t8 + 1) * 128],
                              i_=prow[:, psl * S + t * 128:psl * S + (t + 1) * 128],
                              idn=self.idb_sb:
                              e.transpose(o, i_, idn[:, :]))
                        if t8 == gn - 1:
                            v_tp_last = em.eop("tensor", fn, inc=TT)
                        else:
                            em.eop("tensor", fn)
                    em.wait("vector", TT, v_tp_last)
                    v_e_last = em.eop("vector",
                                      lambda e, o=pt_sb[:, g0 * 128:(g0 + gn) * 128],
                                      i_=tp_banks[bk][:, 0:gn * 128]:
                                      e.tensor_copy(o, i_), inc=VV)
                    st["rel_tp"][bk] = (VV, v_e_last)
                st["v_tpg"][psl] = v_tp_last

                apar = i % 2
                em.wait("tensor", VV, v_e_last)
                em.waitrel("tensor", st["rel_ao_bank"][apar])
                for t in range(T):
                    kt = k_lo // 128 + t
                    fn = (lambda e, o=ao_banks[apar][:, 0:128],
                          vt=vt_sb[:, hw * 4096 + kt * 128:hw * 4096 + (kt + 1) * 128],
                          pt=pt_sb[:, t * 128:(t + 1) * 128],
                          stt=(t == 0), sp=(t == T - 1):
                          e.matmul(o, vt, pt, start=stt, stop=sp))
                    if t == T - 1:
                        v_av = em.eop("tensor", fn, inc=TT)
                    else:
                        em.eop("tensor", fn)
                em.wait("vector", TT, v_av)
                em.waitrel("vector", st["rel_ao_dma"][hw])
                v_ao = em.eop("vector",
                              lambda e, o=ao_sb[:, hw * S + qt * 128:hw * S + (qt + 1) * 128],
                              i_=ao_banks[apar][:, 0:128]:
                              e.tensor_copy(o, i_), inc=VV)
                st["rel_ao_bank"][apar] = (VV, v_ao)
                return v_ao

            v_l = load_pair(0)
            em.waitrel("tensor", v_l)
            em.barrier()
            for p in range(NHL // 2):
                v_transposes(v_l)
                pend = []
                v_ao_last = [0, 0]
                for i in range(2 * NQT):
                    qt, hw = i // 2, i % 2
                    W = SLOT_W[2 * p + hw]
                    L = (qt + 1) * 128
                    k_lo = max(0, (L - W) // 512 * 512) if W < S else 0
                    it = (i, qt, hw, k_lo)
                    if len(pend) >= 3:
                        it3 = pend.pop(0)
                        v_ao_last[it3[2]] = emit_tp_av(it3)
                    emit_scores_evac_max(it)
                    pend.append(it)
                    if len(pend) >= 2:
                        emit_exp_norm(pend[-2])
                emit_exp_norm(pend[-1])
                for it3 in pend:
                    v_ao_last[it3[2]] = emit_tp_av(it3)
                st["rel_pair"] = (TT, em.cnt[TT])
                if p + 1 < NHL // 2:
                    v_l = load_pair(p + 1)
                for hw in range(2):
                    h = 2 * p + hw
                    ga = G0 if hw == 0 else G1
                    em.wait("gpsimd", VV, v_ao_last[hw])
                    v_d = self.dma("gpsimd", ga,
                                   self.ao_sc[h * 128:(h + 1) * 128, :],
                                   ao_sb[:, hw * S:(hw + 1) * S])
                    st["rel_ao_dma"][hw] = (ga, v_d)

    def phase5a(self):
        em = self.em
        self.p5 = contextlib.ExitStack()
        self.p5a = contextlib.ExitStack()

        self.ao5 = self.sbuf("p5_ao", [128, 8 * S], BF16, self.p5a, side="right")
        self.wo5 = self.sbuf("p5_wo", [128, 8 * 2048], BF16, self.p5a, tdest=True,
                             side="right")
        self.strip5 = self.sbuf("p5_strip", [128, 2 * 512], BF16, self.p5a,
                                side="right")
        self.pa = [self.psum("p5_a0", [128, 512], F32, self.p5),
                   self.psum("p5_a1", [128, 512], F32, self.p5)]

        for h in range(NHL):
            v_ao = self.dma("sync", L0, self.ao5[:, h * S:(h + 1) * S],
                            self.ao_sc[h * 128:(h + 1) * 128, :])
        rel_bank = [None, None]
        rel_strip = [None, None]
        rel_woh = None
        rs_done = []

        for oh in range(2):
            em.waitrel("sync", rel_woh)
            for dc in range(NHL):
                v_wo = self.dma("sync", L5, self.wo5[:, dc * 2048:(dc + 1) * 2048],
                                self.wout_bf[oh * 2048:(oh + 1) * 2048,
                                             dc * 128:(dc + 1) * 128], transpose=True)
            for oc in range(oh * 4, oh * 4 + 4):
                ocl = oc - oh * 4
                last_dma = 0
                for tt in range(32):
                    bk = tt % 2
                    em.waitrel("tensor", rel_bank[bk])
                    em.wait("tensor", L0, v_ao)
                    em.wait("tensor", L5, v_wo)
                    for dc in range(8):
                        aoa = self.ao5[:, dc * S + tt * 128:dc * S + (tt + 1) * 128]
                        rhs = self.wo5[:, dc * 2048 + ocl * 512:dc * 2048 + (ocl + 1) * 512]
                        fn = (lambda e, o=self.pa[bk][:, :], a=aoa, r=rhs,
                              stt=(dc == 0), sp=(dc == 7):
                              e.matmul(o, a, r, start=stt, stop=sp))
                        if dc == 7:
                            v_mm = em.eop("tensor", fn, inc=TT)
                        else:
                            em.eop("tensor", fn)

                    ssl = tt % 2
                    em.wait("scalar", TT, v_mm)
                    em.waitrel("scalar", rel_strip[ssl])
                    dsts = self.strip5[:, ssl * 512:(ssl + 1) * 512]
                    v_ev = em.eop("scalar", lambda e, o=dsts, i_=self.pa[bk][:, :]:
                                  e.activation(o, i_, AF.Identity), inc=AA)
                    rel_bank[bk] = (AA, v_ev)

                    em.wait("gpsimd", AA, v_ev)
                    gs = G0 if ssl == 0 else G1
                    last_dma = self.dma("gpsimd", gs,
                                        self.ypart[tt * 128:(tt + 1) * 128,
                                                   oc * 512:(oc + 1) * 512], dsts)
                    rel_strip[ssl] = (gs, last_dma)
            rel_woh = (TT, em.cnt[TT])
        self.rs_done = rs_done

    def emit_collectives(self):
        em = self.em
        em.barrier()
        v_cc = em.inc(CC)

        def f(e, SM, i_=self.ypart, o=self.yred):
            e.collective_compute(
                "ReduceScatter", ALU.add, replica_groups=RS_GROUPS,
                ins=[i_.ap().opt()], outs=[o.ap().opt()]).then_inc(SM[CC], 1)
        em.ops["gpsimd"].append(f)
        self.rs_done = [v_cc] * 8
        em.wait("gpsimd", CC, em.cnt[CC])
        em.barrier()

    def phase5b(self):
        em = self.em
        with self.p5:
            p5 = self.p5
            tpb = [self.psum("p5_tp0", [128, 1024], BF16, p5),
                   self.psum("p5_tp1", [128, 1024], BF16, p5)]
            lgp = self.psum("p5_lgp", [128, 512], F32, p5)
            wdtp = self.psum("p5_wdtp", [128, 512], BF16, p5)
            bmp = [self.psum("p5_bm0", [128, 512], F32, p5),
                   self.psum("p5_bm1", [128, 512], F32, p5)]

            v_rwfm = 0
            with contextlib.ExitStack() as prep:
                rwf = self.sbuf("p5_rwf", [64, DM], F32, prep)
                rwb = self.sbuf("p5_rwb", [128, DM], BF16, prep)
                ebf = self.sbuf("p5_ebf", [64, DM], F32, prep)

                if self.p5b_stage >= 1:
                    v_rw = self.dma("sync", LC, rwf[:, :], self.rw_in[:, :])
                    v_eb = self.dma("sync", LC, ebf[:, :], self.eb_in[:, :])
                    em.wait("scalar", LC, v_eb)
                    em.eop("scalar", lambda e, o=rwb, i_=rwf:
                           e.activation(o[0:64, :], i_[:, :], AF.Identity))
                    em.wait("scalar", LC, v_eb)
                    v_casts = em.eop("scalar", lambda e, o=self.eb_bf, i_=ebf:
                                     e.activation(o[:, :], i_[:, :], AF.Identity), inc=AA)

                if self.p5b_stage >= 2:
                    em.wait("tensor", AA, v_casts)
                    v_ev_g = [0, 0]
                    for g in range(4):
                        bk = g % 2
                        em.wait("tensor", VV, v_ev_g[bk])
                        for t8 in range(8):
                            dc = g * 8 + t8
                            fn = (lambda e, o=tpb[bk][:, t8 * 128:(t8 + 1) * 128],
                                  i_=rwb[:, dc * 128:(dc + 1) * 128], idn=self.idb_sb:
                                  e.transpose(o, i_, idn[:, :]))
                            if t8 == 7:
                                v_t = em.eop("tensor", fn, inc=TT)
                            else:
                                em.eop("tensor", fn)
                        if self.p5b_stage < 3:
                            continue
                        em.wait("vector", TT, v_t)
                        for t8 in range(8):
                            dc = g * 8 + t8
                            fn = (lambda e, o=self.rw_fm[:, dc * NE:(dc + 1) * NE],
                                  i_=tpb[bk][:, t8 * 128:t8 * 128 + NE]:
                                  e.tensor_copy(o, i_))
                            if t8 == 7:
                                v_rwfm = em.eop("vector", fn, inc=VV)
                            else:
                                em.eop("vector", fn)
                        v_ev_g[bk] = v_rwfm
                em.barrier()

            y_ring = self.sbuf("p5_y", [128, 2 * DM], F32, p5)
            xn5 = self.sbuf("p5_xn5", [128, 2 * DM], BF16, p5)
            xnfm = self.sbuf("p5_xnfm", [128, 32 * 128], BF16, p5)
            xnw = self.sbuf("p5_xnw", [128, DM], F32, p5)
            bout_sb = self.sbuf("p5_bout", [128, DM], BF16, p5)
            st5 = self.sbuf("p5_st5", [128, 16], F32, p5)
            lg = self.sbuf("p5_lg", [128, NE], F32, p5)
            ex5 = self.sbuf("p5_ex5", [128, NE], F32, p5)
            wk5 = self.sbuf("p5_wk5", [128, NE], F32, p5)
            eq5 = self.sbuf("p5_eq5", [128, NE], F32, p5)
            wd5 = self.sbuf("p5_wd5", [128, NE], F32, p5)
            wdb = self.sbuf("p5_wdb", [128, NE], BF16, p5)
            wdt = self.sbuf("p5_wdt", [64, 128], BF16, p5)
            cols5 = self.sbuf("p5_c5", [128, 8], F32, p5)

            v_bf32 = self.dma("sync", LC, xnw[:, :], self.bout_in[:, :])
            em.wait("scalar", LC, v_bf32)
            v_bout = em.eop("scalar", lambda e, o=bout_sb, i_=xnw:
                            e.activation(o[:, :], i_[:, :], AF.Identity), inc=AA)

            rel_y = [None, None]
            rel_xn5 = [None, None]
            rel_tpb = [(VV, v_rwfm), (VV, v_rwfm)]
            rel_lgp = None
            rel_wdtp = None
            rel_bmp = [None, None]
            rel_st5 = [[None, None], [None, None]]
            v_ex_prev = 0

            for t5 in range(self.t5_count):
                sl = t5 % 2
                ya = y_ring[:, sl * DM:(sl + 1) * DM]
                xa = xn5[:, sl * DM:(sl + 1) * DM]
                c0 = sl * 8

                sx = L0 if sl == 0 else L1
                em.waitrel("sync", rel_y[sl])
                v_x = self.dma("sync", sx, ya, self.xq_in[t5 * 128:(t5 + 1) * 128, :])
                em.wait("vector", sx, v_x)
                em.wait("scalar", sx, v_x)
                em.waitrel("scalar", rel_xn5[sl])

                v_rstd, v_sq, v_std = self.ln_stats(
                    ya, st5, c0, rel_st5[sl][0], rel_st5[sl][1], dump=xa)

                em.waitrel("vector", rel_xn5[sl])
                v_xn = em.eop("vector", lambda e, o=xa, i_=ya, a=st5, c=c0:
                              e.tensor_scalar(o, i_, a[:, c + 1:c + 2], a[:, c + 7:c + 8],
                                              ALU.add, ALU.mult), inc=VV)

                em.wait("tensor", VV, v_xn)
                ev_last = 0
                for g in range(4):
                    bk = g % 2
                    em.waitrel("tensor", rel_tpb[bk])
                    for t8 in range(8):
                        dc = g * 8 + t8
                        fn = (lambda e, o=tpb[bk][:, t8 * 128:(t8 + 1) * 128],
                              i_=xa[:, dc * 128:(dc + 1) * 128], idn=self.idb_sb:
                              e.transpose(o, i_, idn[:, :]))
                        if t8 == 7:
                            v_t = em.eop("tensor", fn, inc=TT)
                        else:
                            em.eop("tensor", fn)
                    em.wait("vector", TT, v_t)
                    ev_last = em.eop("vector",
                                     lambda e, o=xnfm[:, g * 1024:(g + 1) * 1024],
                                     i_=tpb[bk]:
                                     e.tensor_copy(o, i_[:, :]), inc=VV)
                    rel_tpb[bk] = (VV, ev_last)

                em.wait("tensor", VV, ev_last)
                em.waitrel("tensor", rel_lgp)
                for dc in range(32):
                    fn = (lambda e, o=lgp[:, 0:NE], x=xnfm[:, dc * 128:(dc + 1) * 128],
                          r=self.rw_fm[:, dc * NE:(dc + 1) * NE],
                          stt=(dc == 0), sp=(dc == 31):
                          e.matmul(o, x, r, start=stt, stop=sp))
                    if dc == 31:
                        v_lg = em.eop("tensor", fn, inc=TT)
                    else:
                        em.eop("tensor", fn)

                em.wait("vector", TT, v_lg)
                em.wait("vector", LC, self.v_consts)
                v_lgs = self.chain("vector", lambda e, o=lg, i_=lgp, r=self.rb_sb:
                                   e.tensor_tensor(o[:, :], i_[:, 0:NE], r[:, :], ALU.add),
                                   VV)
                rel_lgp = (VV, v_lgs)

                em.wait("vector", AA, v_ex_prev)
                v_m = em.eop("vector", lambda e, o=cols5, i_=lg:
                             e.reduce_max(o[:, 0:1], i_[:, :], axis=AX.X, negate=True),
                             inc=VV)
                em.wait("scalar", VV, v_m)
                v_ex = em.eop("scalar", lambda e, o=ex5, i_=lg, b=cols5:
                              e.activation(o[:, :], i_[:, :], AF.Exp, bias=b[:, 0:1],
                                           accum_out=b[:, 1:2]), inc=AA)
                v_ex_prev = v_ex
                em.wait("vector", AA, v_ex)

                self.chain("vector", lambda e, o=wk5, i_=ex5:
                           e.tensor_copy(o[:, :], i_[:, :]), VV)
                for it in range(KTOP - 1):
                    self.chain("vector", lambda e, o=cols5, i_=wk5:
                               e.reduce_max(o[:, 2:3], i_[:, :], axis=AX.X), VV)
                    self.chain("vector", lambda e, o=eq5, i_=wk5, m=cols5:
                               e.tensor_scalar(o[:, :], i_[:, :], m[:, 2:3], 1.0e7,
                                               ALU.is_equal, ALU.mult), VV)
                    self.chain("vector", lambda e, o=wk5, i_=eq5:
                               e.tensor_tensor(o[:, :], o[:, :], i_[:, :], ALU.subtract), VV)
                self.chain("vector", lambda e, o=cols5, i_=wk5:
                           e.reduce_max(o[:, 3:4], i_[:, :], axis=AX.X), VV)
                self.chain("vector", lambda e, o=eq5, i_=ex5, t=cols5:
                           e.tensor_scalar(o[:, :], i_[:, :], t[:, 3:4], None, ALU.is_ge), VV)
                self.chain("vector", lambda e, o=wd5, a=ex5, b=eq5:
                           e.tensor_tensor(o[:, :], a[:, :], b[:, :], ALU.mult), VV)
                self.chain("vector", lambda e, o=cols5:
                           e.reciprocal(o[:, 4:5], o[:, 1:2]), VV)
                self.chain("vector", lambda e, o=wd5, r=cols5:
                           e.tensor_scalar(o[:, :], o[:, :], r[:, 4:5], None, ALU.mult,
                                           ALU.add, accum_out=r[:, 5:6]), VV)
                v_wd = em.eop("vector", lambda e, o=wdb, i_=wd5:
                              e.tensor_copy(o[:, :], i_[:, :]), inc=VV)

                em.wait("tensor", VV, v_wd)
                em.waitrel("tensor", rel_wdtp)
                v_wt = em.eop("tensor", lambda e, o=wdtp[0:64, 0:128],
                              i_=wdb[:, 0:NE], idn=self.idb_sb:
                              e.transpose(o, i_, idn[:, :]), inc=TT)
                em.wait("vector", TT, v_wt)
                v_wte = em.eop("vector", lambda e, o=wdt, i_=wdtp:
                               e.tensor_copy(o[:, :], i_[0:64, 0:128]), inc=VV)
                rel_wdtp = (VV, v_wte)

                self.chain("vector", lambda e, o=cols5, s=st5, c=c0:
                           e.tensor_tensor(o[:, 6:7], s[:, c + 7:c + 8], o[:, 5:6], ALU.mult), VV)
                em.wait("vector", AA, v_bout)
                em.eop("vector", lambda e, o=xnw, i_=ya, s=st5, c=c0, r=cols5:
                       e.tensor_scalar(o[:, :], i_, s[:, c + 1:c + 2], r[:, 6:7],
                                       ALU.add, ALU.mult))
                em.wait("vector", AA, v_bout)
                em.eop("vector", lambda e, o=xnw, b=bout_sb:
                       e.tensor_tensor(o[:, :], o[:, :], b[:, :], ALU.add))
                em.eop("vector", lambda e, o=ya, a=xnw:
                       e.tensor_tensor(o, o, a[:, :], ALU.add))

                for oc in range(8):
                    bk = oc % 2
                    em.wait("tensor", VV, v_wte)
                    em.waitrel("tensor", rel_bmp[bk])
                    v_bm = em.eop("tensor", lambda e, o=bmp[bk][:, :], w=wdt,
                                  b=self.eb_bf[:, oc * 512:(oc + 1) * 512]:
                                  e.matmul(o, w[:, :], b, start=True, stop=True), inc=TT)
                    em.wait("vector", TT, v_bm)
                    v_add = em.eop("vector", lambda e, o=ya[:, oc * 512:(oc + 1) * 512],
                                   b=bmp[bk][:, :]:
                                   e.tensor_tensor(o, o, b, ALU.add), inc=VV)
                    rel_bmp[bk] = (VV, v_add)

                v_fin = em.cnt[VV]
                rel_xn5[sl] = (TT, em.cnt[TT])
                rel_st5[sl][0] = (AA, v_std)
                rel_st5[sl][1] = (VV, v_fin)
                gy = G2 if sl == 0 else G3
                em.wait("gpsimd", VV, v_fin)
                v_store = self.dma("gpsimd", gy, self.ysp[t5 * 128:(t5 + 1) * 128, :], ya)
                rel_y[sl] = (gy, v_store)

            self.emit_collectives()
            self.p5a.close()

            y2 = self.sbuf("p5_y2", [128, 2 * DM], F32, p5)
            yb = self.sbuf("p5_yb", [128, 2 * 512], BF16, p5)
            rel_y2 = [None, None]
            rel_yb = [None, None]
            for t5 in range(8):
                sl = t5 % 2
                ya = y2[:, sl * DM:(sl + 1) * DM]
                sx = L0 if sl == 0 else L1
                em.waitrel("sync", rel_y2[sl])
                v_x = self.dma("sync", sx, ya, self.ysp[t5 * 128:(t5 + 1) * 128, :])
                em.wait("vector", sx, v_x)
                v_fin = 0
                for oc in range(8):
                    ybsl = oc % 2
                    sy = L2 if ybsl == 0 else L3
                    em.waitrel("sync", rel_yb[ybsl])
                    v_yr = self.dma("sync", sy, yb[:, ybsl * 512:(ybsl + 1) * 512],
                                    self.yred[t5 * 128:(t5 + 1) * 128,
                                              oc * 512:(oc + 1) * 512])
                    em.wait("vector", sy, v_yr)
                    v_fin = em.eop("vector",
                                   lambda e, o=ya[:, oc * 512:(oc + 1) * 512],
                                   r=yb[:, ybsl * 512:(ybsl + 1) * 512]:
                                   e.tensor_tensor(o, o, r, ALU.add), inc=VV)
                    rel_yb[ybsl] = (VV, v_fin)
                gy = G0 if sl == 0 else G1
                em.wait("gpsimd", VV, v_fin)
                v_store = self.dma("gpsimd", gy,
                                   self.y_out[t5 * 128:(t5 + 1) * 128, :], ya)
                rel_y2[sl] = (gy, v_store)

            em.wait("gpsimd", G0, em.cnt[G0])
            em.wait("gpsimd", G1, em.cnt[G1])

    def emit_debug(self):
        em = self.em
        if not self.dbg:
            return
        em.barrier()
        src_map = {"dbg_xn": self.xn_sc, "dbg_wq": self.wq_sc, "dbg_qkv": self.qkv_sc,
                   "dbg_ao": self.ao_sc, "dbg_wout": self.wout_bf}
        if self.upto >= 4:
            src_map["dbg_ypart"] = self.ypart
            src_map["dbg_yred"] = self.yred
        for name, dst in self.dbg.items():
            src = src_map[name]
            v = self.dma("gpsimd", G0, dst.ap(), src.ap())
            em.wait("gpsimd", G0, v)



def _alibi_slopes(n_heads=32, alibi_bias_max=8):
    _n = 2 ** math.ceil(math.log2(n_heads))
    m = np.arange(1, _n + 1, dtype=np.float64) * (alibi_bias_max / _n)
    slopes = 1.0 / np.power(2.0, m)
    if _n != n_heads:
        slopes = np.concatenate([slopes[1::2], slopes[::2]])[:n_heads]
    return slopes


def _build_const_inputs(heads):
    bf = ml_dtypes.bfloat16
    slopes = _alibi_slopes()
    pos = np.arange(S, dtype=np.float64) - (S - 1)
    al3 = np.zeros((3 * NHL, S), dtype=bf)
    for i, h in enumerate(heads):
        a = slopes[h] * pos
        hi = a.astype(np.float32).astype(bf)
        r1 = a - hi.astype(np.float64)
        mid = r1.astype(np.float32).astype(bf)
        r2 = r1 - mid.astype(np.float64)
        lo = r2.astype(np.float32).astype(bf)
        al3[3 * i + 0] = hi
        al3[3 * i + 1] = mid
        al3[3 * i + 2] = lo

    masks = np.zeros((4 * 128, 512), dtype=np.float32)
    ii = np.arange(128)[:, None]
    jj = np.arange(512)[None, :]
    for v in range(4):
        wd = (v + 1) * 128
        m = np.where(jj <= (wd - 128) + ii, 0.0, np.float32(NEG_BIG)).astype(np.float32)
        m[:, wd:] = 0.0
        masks[v * 128:(v + 1) * 128] = m

    ones3 = np.ones((3, 128), dtype=bf)
    idb = np.eye(128, dtype=np.float32).astype(bf)
    return al3, masks, ones3, idb


_BUILD_CACHE = {}


def get_program(upto=5, debug=(), t5_count=8, p5b_stage=3):
    key = (upto, tuple(sorted(debug)), t5_count, p5b_stage)
    if key not in _BUILD_CACHE:
        _BUILD_CACHE[key] = Builder(upto=upto, debug=debug, t5_count=t5_count,
                                    p5b_stage=p5b_stage).build()
    return _BUILD_CACHE[key]


def make_in_maps(L0_attn_x, L0_attn_w_qkv, L0_attn_b_qkv, L0_attn_w_out,
                 L0_attn_b_out, L0_ffn_router_weight, L0_ffn_router_bias,
                 L0_ffn_expert_bias):
    x = np.ascontiguousarray(np.asarray(L0_attn_x, dtype=np.float32))
    wqkv = np.asarray(L0_attn_w_qkv, dtype=np.float32)
    bqkv = np.asarray(L0_attn_b_qkv, dtype=np.float32)
    wout = np.asarray(L0_attn_w_out, dtype=np.float32)
    bout = np.asarray(L0_attn_b_out, dtype=np.float32)
    rw = np.ascontiguousarray(np.asarray(L0_ffn_router_weight, dtype=np.float32))
    rb = np.asarray(L0_ffn_router_bias, dtype=np.float32)
    eb = np.ascontiguousarray(np.asarray(L0_ffn_expert_bias, dtype=np.float32))

    in_maps = []
    for c in range(8):
        b, g = c // 4, c % 4
        heads = [g + 4 * j for j in range(NHL)]
        rows = []
        for kind in range(3):
            for h in heads:
                rows.append(wqkv[kind * DM + 128 * h: kind * DM + 128 * (h + 1)])
        wqkv_c = np.ascontiguousarray(np.concatenate(rows, axis=0))
        bq = []
        for kind in range(3):
            for h in heads:
                bq.append(bqkv[kind * DM + 128 * h: kind * DM + 128 * (h + 1)])
        bqkv_c = np.ascontiguousarray(np.stack(bq, axis=1).astype(np.float32))
        wout_c = np.ascontiguousarray(
            np.concatenate([wout[:, 128 * h:128 * (h + 1)] for h in heads], axis=1))
        al3, masks, ones3, idb = _build_const_inputs(heads)
        in_maps.append({
            "x": x[b],
            "xq": np.ascontiguousarray(x[b, 1024 * g:1024 * (g + 1)]),
            "wqkv": wqkv_c,
            "bqkv": bqkv_c,
            "wout": wout_c,
            "bout_bc": np.ascontiguousarray(np.tile(bout[None, :], (128, 1))),
            "rw": rw,
            "rb_bc": np.ascontiguousarray(np.tile(rb[None, :], (128, 1))),
            "eb": eb,
            "alibi3": al3,
            "ones3": ones3,
            "masks": masks,
            "idb": idb,
        })
    return in_maps


def kernel(L0_attn_x, L0_attn_w_qkv, L0_attn_b_qkv, L0_attn_w_out, L0_attn_b_out,
           L0_ffn_router_weight, L0_ffn_router_bias, L0_ffn_expert_bias):
    nc = get_program()
    in_maps = make_in_maps(L0_attn_x, L0_attn_w_qkv, L0_attn_b_qkv, L0_attn_w_out,
                           L0_attn_b_out, L0_ffn_router_weight, L0_ffn_router_bias,
                           L0_ffn_expert_bias)
    res = run_bass_kernel_spmd(nc, in_maps, core_ids=list(range(8)))
    out = np.empty((2, S, DM), dtype=np.float32)
    for c in range(8):
        b, g = c // 4, c % 4
        out[b, 1024 * g:1024 * (g + 1), :] = res.results[c]["y"]
    return out


# version 11
# speedup vs baseline: 1.1365x; 1.0201x over previous
import math
import contextlib

import numpy as np
import ml_dtypes

import concourse.bass as bass
import concourse.mybir as mybir
from concourse.bass_utils import run_bass_kernel_spmd

F32 = mybir.dt.float32
BF16 = mybir.dt.bfloat16
AF = mybir.ActivationFunctionType
ALU = mybir.AluOpType
AX = mybir.AxisListType

S = 4096
DM = 4096
HD = 128
NHL = 8
NCHUNK = 3 * NHL
NT = S // 128
NTBLK = 8
NQT = 32
NE = 64
KTOP = 8
Q_SCALE = 1.0 / math.sqrt(HD)
NEG_BIG = -1.0e9
EPS = 1e-5
RS_GROUPS = [[0, 1, 2, 3], [4, 5, 6, 7]]

def _slot_window(j):
    slope = 2.0 ** (-(4 * j + 3 + 1) / 4.0)
    w = 384.0 / slope
    if w >= S:
        return S
    return int(math.ceil(w / 512.0) * 512)

SLOT_W = [_slot_window(j) for j in range(NHL)]

L0, L1, L2, L3, L4, L5, LC = "l0", "l1", "l2", "l3", "l4", "l5", "lc"
G0, G1, G2, G3 = "g0", "g1", "g2", "g3"
TT, VV, AA, CC = "tt", "vv", "aa", "cc"
ALL_SEMS = (L0, L1, L2, L3, L4, L5, LC, G0, G1, G2, G3, TT, VV, AA, CC)


class Em:

    ENGS = ("sync", "tensor", "vector", "scalar", "gpsimd")

    def __init__(self):
        self.ops = {e: [] for e in self.ENGS}
        self.cnt = {n: 0 for n in ALL_SEMS}
        self.lastwait = {e: {} for e in self.ENGS}

    def inc(self, sem, by=1):
        self.cnt[sem] += by
        return self.cnt[sem]

    def wait(self, eng, sem, val):
        if val is None or val <= 0:
            return
        if self.lastwait[eng].get(sem, 0) >= val:
            return
        self.lastwait[eng][sem] = val

        def f(e, SM, sem=sem, val=val):
            e.wait_ge(SM[sem], val)

        self.ops[eng].append(f)

    def waitrel(self, eng, rel):
        if rel is None:
            return
        if isinstance(rel, list):
            for r in rel:
                self.waitrel(eng, r)
        else:
            self.wait(eng, rel[0], rel[1])

    def op(self, eng, fn):
        self.ops[eng].append(lambda e, SM, fn=fn: fn(e))

    def eop(self, eng, fn, inc=None):
        if inc is None:
            self.op(eng, fn)
            return None
        v = self.inc(inc)

        def g(e, SM, fn=fn, s=inc):
            fn(e).then_inc(SM[s], 1)

        self.ops[eng].append(g)
        return v

    def barrier(self):
        snap = dict(self.cnt)
        for e in self.ENGS:
            for s, v in snap.items():
                self.wait(e, s, v)


class Builder:
    def __init__(self, upto=5, debug=(), t5_count=8, p5b_stage=3):
        self.upto = upto
        self.t5_count = t5_count
        self.p5b_stage = p5b_stage
        self.debug = debug
        self.nc = bass.Bass("TRN2", target_bir_lowering=False)
        self.em = Em()
        self.stack = contextlib.ExitStack()

    def sbuf(self, name, shape, dtype, st=None, tdest=False, side=None):
        return (st or self.stack).enter_context(
            self.nc.sbuf_tensor(name, shape, dtype, is_transpose_dest=tdest,
                                side=side))

    def psum(self, name, shape, dtype, st=None):
        return (st or self.stack).enter_context(self.nc.psum_tensor(name, shape, dtype))

    def chain(self, eng, fn, sem):
        v = self.em.eop(eng, fn, inc=sem)
        self.em.wait(eng, sem, v)
        return v

    def dma(self, eng, sem, out, in_, transpose=False):
        v = self.em.inc(sem, 16)

        if transpose:
            def f(e, SM, o=out, i=in_, s=sem):
                e.dma_start_transpose(out=o, in_=i).then_inc(SM[s], 16)
        else:
            def f(e, SM, o=out, i=in_, s=sem):
                e.dma_start(out=o, in_=i).then_inc(SM[s], 16)

        self.em.ops[eng].append(f)
        return v

    def build(self):
        nc = self.nc
        em = self.em

        self.x_in = nc.dram_tensor("x", [S, DM], F32, kind="ExternalInput")
        self.xq_in = nc.dram_tensor("xq", [1024, DM], F32, kind="ExternalInput")
        self.wqkv_in = nc.dram_tensor("wqkv", [NCHUNK * 128, DM], F32, kind="ExternalInput")
        self.bqkv_in = nc.dram_tensor("bqkv", [128, NCHUNK], F32, kind="ExternalInput")
        self.wout_in = nc.dram_tensor("wout", [DM, NHL * 128], F32, kind="ExternalInput")
        self.bout_in = nc.dram_tensor("bout_bc", [128, DM], F32, kind="ExternalInput")
        self.rw_in = nc.dram_tensor("rw", [NE, DM], F32, kind="ExternalInput")
        self.rb_in = nc.dram_tensor("rb_bc", [128, NE], F32, kind="ExternalInput")
        self.eb_in = nc.dram_tensor("eb", [NE, DM], F32, kind="ExternalInput")
        self.alibi3_in = nc.dram_tensor("alibi3", [3 * NHL, S], BF16, kind="ExternalInput")
        self.ones3_in = nc.dram_tensor("ones3", [3, 128], BF16, kind="ExternalInput")
        self.masks_in = nc.dram_tensor("masks", [4 * 128, 512], F32, kind="ExternalInput")
        self.idb_in = nc.dram_tensor("idb", [128, 128], BF16, kind="ExternalInput")
        self.y_out = nc.dram_tensor("y", [1024, DM], F32, kind="ExternalOutput")

        self.xn_sc = nc.dram_tensor("xn_sc", [S, DM], BF16)
        self.wq_sc = nc.dram_tensor("wq_sc", [128, NCHUNK * 32 * 128], BF16)
        self.qkv_sc = nc.dram_tensor("qkv_sc", [NCHUNK * 128, S], BF16)
        self.ao_sc = nc.dram_tensor("ao_sc", [NHL * 128, S], BF16)
        self.wout_bf = nc.dram_tensor("wout_bf", [DM, NHL * 128], BF16)
        self.ypart = nc.dram_tensor("ypart", [S, DM], BF16)
        self.yred = nc.dram_tensor("yred", [1024, DM], BF16)

        dbg = {}
        for name, shape, dt in (("dbg_xn", [S, DM], BF16),
                                ("dbg_wq", [128, NCHUNK * 32 * 128], BF16),
                                ("dbg_qkv", [NCHUNK * 128, S], BF16),
                                ("dbg_ao", [NHL * 128, S], BF16),
                                ("dbg_wout", [DM, NHL * 128], BF16),
                                ("dbg_ypart", [S, DM], BF16),
                                ("dbg_yred", [1024, DM], BF16)):
            if name in self.debug:
                dbg[name] = nc.dram_tensor(name, shape, dt, kind="ExternalOutput")
        self.dbg = dbg

        with self.stack:
            sems = {}
            for n in ALL_SEMS:
                sems[n] = self.stack.enter_context(nc.semaphore(n))

            self.masks_sb = self.sbuf("masks_sb", [128, 4 * 512], F32)
            self.idb_sb = self.sbuf("idb_sb", [128, 128], BF16)
            self.ones3_sb = self.sbuf("ones3_sb", [3, 128], BF16)
            self.bqs_sb = self.sbuf("bqs_sb", [128, NCHUNK], F32)
            self.rw_fm = self.sbuf("rw_fm", [128, 32 * NE], BF16)
            self.eb_bf = self.sbuf("eb_bf", [NE, DM], BF16)
            self.rb_sb = self.sbuf("rb_sb", [128, NE], F32)

            for v in range(4):
                self.dma("sync", LC, self.masks_sb[:, v * 512:(v + 1) * 512],
                         self.masks_in[v * 128:(v + 1) * 128, :])
            self.dma("sync", LC, self.idb_sb[:, :], self.idb_in[:, :])
            self.dma("sync", LC, self.ones3_sb[:, :], self.ones3_in[:, :])
            self.v_consts = self.dma("sync", LC, self.rb_sb[:, :], self.rb_in[:, :])

            self.phase1()
            em.barrier()
            if self.upto >= 2:
                self.phase2()
                em.barrier()
            if self.upto >= 3:
                self.phase3()
                em.barrier()
            if self.upto >= 4:
                self.phase5a()
            if self.upto >= 5:
                self.phase5b()
            elif self.upto == 4:
                self.emit_collectives()
                self.p5.close()

            self.emit_debug()
            em.barrier()

            with nc.Block() as block:
                ops = em.ops

                @block.sync
                def _(e):
                    for f in ops["sync"]:
                        f(e, sems)

                @block.tensor
                def _(e):
                    for f in ops["tensor"]:
                        f(e, sems)

                @block.vector
                def _(e):
                    for f in ops["vector"]:
                        f(e, sems)

                @block.scalar
                def _(e):
                    for f in ops["scalar"]:
                        f(e, sems)

                @block.gpsimd
                def _(e):
                    for f in ops["gpsimd"]:
                        f(e, sems)

        return nc

    def ln_stats(self, xa, stat, c0, prev_act_rel, prev_vec_rel, dump):
        em = self.em
        em.waitrel("vector", prev_act_rel)

        self.chain("vector", lambda e, o=stat, c=c0, i_=xa:
                   e.reduce_sum(o[:, c:c + 1], i_, axis=AX.X, negate=True), VV)
        self.chain("vector", lambda e, o=stat, c=c0:
                   e.tensor_scalar(o[:, c + 1:c + 2], o[:, c:c + 1], 1.0 / DM, None, ALU.mult), VV)

        em.waitrel("scalar", prev_vec_rel)

        v_sq = em.eop("scalar", lambda e, o=dump, i_=xa, a=stat, c=c0:
                      e.activation(o, i_, AF.Square, accum_out=a[:, c + 2:c + 3]),
                      inc=AA)

        em.wait("vector", AA, v_sq)
        self.chain("vector", lambda e, o=stat, c=c0:
                   e.tensor_scalar(o[:, c + 3:c + 4], o[:, c + 2:c + 3], 1.0 / DM, None, ALU.mult), VV)
        self.chain("vector", lambda e, o=stat, c=c0:
                   e.tensor_tensor(o[:, c + 4:c + 5], o[:, c + 1:c + 2], o[:, c + 1:c + 2], ALU.mult), VV)
        self.chain("vector", lambda e, o=stat, c=c0:
                   e.tensor_tensor(o[:, c + 5:c + 6], o[:, c + 3:c + 4], o[:, c + 4:c + 5], ALU.subtract), VV)
        v_var = self.chain("vector", lambda e, o=stat, c=c0:
                           e.tensor_scalar(o[:, c + 5:c + 6], o[:, c + 5:c + 6], EPS, None, ALU.add), VV)

        em.wait("scalar", VV, v_var)
        v_std = em.eop("scalar", lambda e, o=stat, c=c0:
                       e.activation(o[:, c + 6:c + 7], o[:, c + 5:c + 6], AF.Sqrt), inc=AA)

        em.wait("vector", AA, v_std)
        v_rstd = self.chain("vector", lambda e, o=stat, c=c0:
                            e.reciprocal(o[:, c + 7:c + 8], o[:, c + 6:c + 7]), VV)
        return v_rstd, v_sq, v_std

    def phase1(self):
        em = self.em

        with contextlib.ExitStack() as p1:
            x_ring = self.sbuf("p1_x", [128, 2 * DM], F32, p1)
            xn_ring = self.sbuf("p1_xn", [128, 2 * DM], BF16, p1)
            stat = self.sbuf("p1_stat", [128, 16], F32, p1)
            wrow_f = self.sbuf("p1_wf", [128, 2 * DM], F32, p1)
            wrow_b = self.sbuf("p1_wb", [128, 2 * DM], BF16, p1)
            wqe = self.sbuf("p1_wqe", [128, 2 * 1024], BF16, p1)
            bq_raw = self.sbuf("p1_bq", [128, NCHUNK], F32, p1)
            tp_bank = [self.psum("p1_tp0", [128, 1024], BF16, p1),
                       self.psum("p1_tp1", [128, 1024], BF16, p1)]

            v_bq = self.dma("sync", LC, bq_raw[:, :], self.bqkv_in[:, :])
            em.wait("vector", LC, v_bq)
            self.chain("vector", lambda e, o=self.bqs_sb, i=bq_raw:
                       e.tensor_scalar(o[:, 0:NCHUNK], i[:, 0:NCHUNK], 1.0, None, ALU.mult), VV)
            self.chain("vector", lambda e, o=self.bqs_sb, i=bq_raw:
                       e.tensor_scalar(o[:, 0:NHL], i[:, 0:NHL], Q_SCALE, None, ALU.mult), VV)

            rel_x = [None, None]
            rel_xn = [None, None]
            rel_stat_a = [None, None]
            rel_stat_v = [None, None]
            rel_wf = [None, None]
            rel_wb = [None, None]
            rel_wqe = [None, None]
            rel_tp = [None, None]

            for i in range(NT):
                sl = i % 2
                xa = x_ring[:, sl * DM:(sl + 1) * DM]
                xna = xn_ring[:, sl * DM:(sl + 1) * DM]
                c0 = sl * 8

                sx = L0 if sl == 0 else L1
                em.waitrel("sync", rel_x[sl])
                v_ld = self.dma("sync", sx, xa, self.x_in[i * 128:(i + 1) * 128, :])
                em.wait("vector", sx, v_ld)
                em.wait("scalar", sx, v_ld)
                em.waitrel("scalar", rel_xn[sl])

                v_rstd, v_sq, v_std = self.ln_stats(
                    xa, stat, c0, rel_stat_a[sl], rel_stat_v[sl], dump=xna)

                em.waitrel("vector", rel_xn[sl])
                v_xn = em.eop("vector", lambda e, o=xna, i_=xa, a=stat, c=c0:
                              e.tensor_scalar(o, i_, a[:, c + 1:c + 2], a[:, c + 7:c + 8],
                                              ALU.add, ALU.mult), inc=VV)
                rel_x[sl] = [(VV, v_xn), (AA, v_sq)]
                rel_stat_a[sl] = (AA, v_std)
                rel_stat_v[sl] = (VV, v_xn)

                gx = G0 if sl == 0 else G1
                em.wait("gpsimd", VV, v_xn)
                v_w = self.dma("gpsimd", gx, self.xn_sc[i * 128:(i + 1) * 128, :], xna)
                rel_xn[sl] = (gx, v_w)

                if i < NCHUNK:
                    j = i
                    ws = j % 2
                    wfa = wrow_f[:, ws * DM:(ws + 1) * DM]
                    wba = wrow_b[:, ws * DM:(ws + 1) * DM]
                    sw = L2 if ws == 0 else L3
                    em.waitrel("sync", rel_wf[ws])
                    v_wl = self.dma("sync", sw, wfa, self.wqkv_in[j * 128:(j + 1) * 128, :])
                    em.wait("scalar", sw, v_wl)
                    em.waitrel("scalar", rel_wb[ws])
                    v_cast = em.eop("scalar", lambda e, o=wba, i_=wfa:
                                    e.activation(o, i_, AF.Identity), inc=AA)
                    rel_wf[ws] = (AA, v_cast)

                    em.wait("tensor", AA, v_cast)
                    v_tp_last = 0
                    for g in range(4):
                        bk = (j * 4 + g) % 2
                        es = (j * 4 + g) % 2
                        em.waitrel("tensor", rel_tp[bk])
                        for t8 in range(8):
                            t = g * 8 + t8
                            fn = (lambda e, o=tp_bank[bk][:, t8 * 128:(t8 + 1) * 128],
                                  i_=wba[:, t * 128:(t + 1) * 128], idn=self.idb_sb:
                                  e.transpose(o, i_, idn[:, :]))
                            if t8 == 7:
                                v_tp = em.eop("tensor", fn, inc=TT)
                            else:
                                em.eop("tensor", fn)
                        v_tp_last = v_tp
                        em.wait("vector", TT, v_tp)
                        em.waitrel("vector", rel_wqe[es])
                        dst = wqe[:, es * 1024:(es + 1) * 1024]
                        v_ev = em.eop("vector", lambda e, o=dst, i_=tp_bank[bk]:
                                      e.tensor_copy(o, i_[:, :]), inc=VV)
                        rel_tp[bk] = (VV, v_ev)
                        ge = G2 if es == 0 else G3
                        em.wait("gpsimd", VV, v_ev)
                        col = (j * 32 + g * 8) * 128
                        v_d = self.dma("gpsimd", ge,
                                       self.wq_sc[:, col:col + 1024], dst)
                        rel_wqe[es] = (ge, v_d)
                    rel_wb[ws] = (TT, v_tp_last)

    def phase2(self):
        em = self.em
        with contextlib.ExitStack() as p2:
            xnb = self.sbuf("p2_xnb", [128, 2 * 32 * 512], BF16, p2, tdest=True)
            panel = self.sbuf("p2_panel", [128, 8 * 32 * 128], BF16, p2)
            strip = self.sbuf("p2_strip", [128, 2 * 8 * 512], BF16, p2)
            banks = [self.psum(f"p2_b{i}", [128, 512], F32, p2) for i in range(8)]

            wof = self.sbuf("p2_wof", [128, 2 * 1024], F32, p2)
            wob = self.sbuf("p2_wob", [128, 2 * 1024], BF16, p2)

            rel_xnb = [None, None]
            rel_strip = [None, None]
            bank_rel = [None] * 8
            rel_panel = None
            rel_wof = [None, None]
            rel_wob = [None, None]

            def wout_prep(j):
                ws = j % 2
                sw = L2 if ws == 0 else L3
                gw = G2 if ws == 0 else G3
                wfa = wof[:, ws * 1024:(ws + 1) * 1024]
                wba = wob[:, ws * 1024:(ws + 1) * 1024]
                em.waitrel("sync", rel_wof[ws])
                v_wl = self.dma("sync", sw, wfa, self.wout_in[j * 128:(j + 1) * 128, :])
                em.wait("scalar", sw, v_wl)
                em.waitrel("scalar", rel_wob[ws])
                v_c = em.eop("scalar", lambda e, o=wba, i_=wfa:
                             e.activation(o, i_, AF.Identity), inc=AA)
                rel_wof[ws] = (AA, v_c)
                em.wait("gpsimd", AA, v_c)
                v_d = self.dma("gpsimd", gw,
                               self.wout_bf[j * 128:(j + 1) * 128, :], wba)
                rel_wob[ws] = (gw, v_d)

            def issue_loads(tb):
                sl = tb % 2
                sem = L0 if sl == 0 else L1
                em.waitrel("sync", rel_xnb[sl])
                v = 0
                for dc in range(32):
                    dst = xnb[:, (sl * 32 + dc) * 512:(sl * 32 + dc + 1) * 512]
                    src = self.xn_sc[tb * 512:(tb + 1) * 512, dc * 128:(dc + 1) * 128]
                    v = self.dma("sync", sem, dst, src, transpose=True)
                return (sem, v)

            for wpass in range(3):
                em.waitrel("sync", rel_panel)
                v_pan = self.dma("sync", L5, panel[:, :],
                                 self.wq_sc[:, wpass * 32768:(wpass + 1) * 32768])
                cur = issue_loads(0)
                for tb in range(NTBLK):
                    sl = tb % 2
                    if tb + 1 < NTBLK:
                        nxt = issue_loads(tb + 1)
                    else:
                        nxt = None
                    v_fin = [0] * 8
                    em.wait("tensor", L5, v_pan)
                    em.waitrel("tensor", cur)
                    for dc in range(32):
                        xa = xnb[:, (sl * 32 + dc) * 512:(sl * 32 + dc + 1) * 512]
                        for o8 in range(8):
                            oc = wpass * 8 + o8
                            if dc == 0:
                                em.waitrel("tensor", bank_rel[o8])
                            wsl = panel[:, (o8 * 32 + dc) * 128:(o8 * 32 + dc + 1) * 128]
                            fn = (lambda e, o=banks[o8][:, :], w=wsl, x=xa,
                                  st=(dc == 0), sp=(dc == 31):
                                  e.matmul(o, w, x, start=st, stop=sp))
                            if dc == 31:
                                v_fin[o8] = em.eop("tensor", fn, inc=TT)
                            else:
                                em.eop("tensor", fn)
                    rel_xnb[sl] = (TT, v_fin[7])

                    ssl = tb % 2
                    v_d_last = 0
                    for o8 in range(8):
                        oc = wpass * 8 + o8
                        em.wait("scalar", TT, v_fin[o8])
                        em.waitrel("scalar", rel_strip[ssl])
                        dsts = strip[:, (ssl * 8 + o8) * 512:(ssl * 8 + o8 + 1) * 512]
                        scl = Q_SCALE if wpass == 0 else 1.0
                        v_ev = em.eop("scalar",
                                      lambda e, o=dsts, i_=banks[o8][:, :],
                                      b=self.bqs_sb[:, oc:oc + 1], s=scl:
                                      e.activation(o, i_, AF.Identity, bias=b, scale=s),
                                      inc=AA)
                        bank_rel[o8] = (AA, v_ev)
                        em.wait("gpsimd", AA, v_ev)
                        gs = G0 if ssl == 0 else G1
                        v_d_last = self.dma(
                            "gpsimd", gs,
                            self.qkv_sc[oc * 128:(oc + 1) * 128, tb * 512:(tb + 1) * 512],
                            dsts)
                    rel_strip[ssl] = (gs, v_d_last)
                    slot = wpass * NTBLK + tb
                    if slot < 16:
                        wout_prep(2 * slot)
                        wout_prep(2 * slot + 1)
                    cur = nxt
                rel_panel = (TT, em.cnt[TT])

    def phase3(self):
        em = self.em
        with contextlib.ExitStack() as p3:
            q_sb = self.sbuf("p3_q", [128, 2 * S], BF16, p3)
            k_sb = self.sbuf("p3_k", [128, 2 * S], BF16, p3)
            v_sb = self.sbuf("p3_v", [128, 2 * S], BF16, p3)
            vt_sb = self.sbuf("p3_vt", [128, 2 * 32 * 128], BF16, p3)
            al_sb = self.sbuf("p3_al", [3, 2 * S], BF16, p3)
            srow = self.sbuf("p3_srow", [128, 3 * S], F32, p3)
            prow = self.sbuf("p3_prow", [128, 3 * S], BF16, p3)
            pt_sb = self.sbuf("p3_pt", [128, 32 * 128], BF16, p3)
            ao_sb = self.sbuf("p3_ao", [128, 2 * S], BF16, p3)
            zc = self.sbuf("p3_zc", [128, 12], F32, p3)

            sc_banks = [self.psum(f"p3_s{i}", [128, 512], F32, p3) for i in range(3)]
            tp_banks = [self.psum("p3_t0", [128, 1024], BF16, p3),
                        self.psum("p3_t1", [128, 1024], BF16, p3)]
            ao_banks = [self.psum("p3_aob0", [128, 512], F32, p3),
                        self.psum("p3_aob1", [128, 512], F32, p3)]

            st = dict(rel_pair=None, rel_sc=[None, None, None],
                      rel_tp=[None, None], rel_ao_bank=[None, None],
                      rel_ao_dma=[None, None],
                      sc_ctr=0, tp_ctr=0,
                      v_exp=[0, 0, 0], v_tpg=[0, 0, 0], v_max=[0, 0, 0],
                      v_nrm=[0, 0, 0])

            def load_pair(p):
                sem = L0 if p % 2 == 0 else L1
                em.waitrel("sync", st["rel_pair"])
                v = 0
                for hw in range(2):
                    h = 2 * p + hw
                    self.dma("sync", sem, q_sb[:, hw * S:(hw + 1) * S],
                             self.qkv_sc[h * 128:(h + 1) * 128, :])
                    self.dma("sync", sem, k_sb[:, hw * S:(hw + 1) * S],
                             self.qkv_sc[(NHL + h) * 128:(NHL + h + 1) * 128, :])
                    self.dma("sync", sem, v_sb[:, hw * S:(hw + 1) * S],
                             self.qkv_sc[(2 * NHL + h) * 128:(2 * NHL + h + 1) * 128, :])
                    v = self.dma("sync", sem, al_sb[:, hw * S:(hw + 1) * S],
                                 self.alibi3_in[3 * h:3 * h + 3, :])
                return (sem, v)

            def v_transposes(v_l):
                em.waitrel("tensor", v_l)
                for hw in range(2):
                    for g in range(4):
                        bk = st["tp_ctr"] % 2
                        st["tp_ctr"] += 1
                        em.waitrel("tensor", st["rel_tp"][bk])
                        for t8 in range(8):
                            t = g * 8 + t8
                            fn = (lambda e, o=tp_banks[bk][:, t8 * 128:(t8 + 1) * 128],
                                  i_=v_sb[:, hw * S + t * 128:hw * S + (t + 1) * 128],
                                  idn=self.idb_sb:
                                  e.transpose(o, i_, idn[:, :]))
                            if t8 == 7:
                                v_t = em.eop("tensor", fn, inc=TT)
                            else:
                                em.eop("tensor", fn)
                        em.wait("vector", TT, v_t)
                        v_e = em.eop("vector",
                                     lambda e,
                                     o=vt_sb[:, (hw * 4 + g) * 1024:(hw * 4 + g + 1) * 1024],
                                     i_=tp_banks[bk]:
                                     e.tensor_copy(o, i_[:, :]), inc=VV)
                        st["rel_tp"][bk] = (VV, v_e)

            def emit_scores_evac_max(it):
                i, qt, hw = it[0], it[1], it[2]
                L = (qt + 1) * 128
                k_lo = it[3]
                nk = (L - k_lo + 511) // 512
                psl = i % 3
                for j in range(nk):
                    c0 = k_lo + j * 512
                    w = min(512, L - c0)
                    bk = st["sc_ctr"] % 3
                    st["sc_ctr"] += 1
                    em.waitrel("tensor", st["rel_sc"][bk])
                    qa = q_sb[:, hw * S + qt * 128:hw * S + (qt + 1) * 128]
                    ka = k_sb[:, hw * S + c0:hw * S + c0 + w]
                    aa_ = al_sb[:, hw * S + c0:hw * S + c0 + w]
                    em.eop("tensor", lambda e, o=sc_banks[bk][:, 0:w], q=qa, k=ka:
                           e.matmul(o, q, k, start=True, stop=False))
                    v_mm = em.eop("tensor",
                                  lambda e, o=sc_banks[bk][:, 0:w], n=self.ones3_sb, a=aa_:
                                  e.matmul(o, n[:, :], a, start=False, stop=True), inc=TT)
                    dst = srow[:, psl * S + (c0 - k_lo):psl * S + (c0 - k_lo) + w]
                    if j == nk - 1:
                        mv = qt % 4
                        em.wait("vector", TT, v_mm)
                        em.wait("vector", AA, st["v_exp"][psl])
                        v = em.eop("vector",
                                   lambda e, o=dst, i_=sc_banks[bk][:, 0:w],
                                   m=self.masks_sb[:, mv * 512:mv * 512 + w]:
                                   e.tensor_tensor(o, i_, m, ALU.add), inc=VV)
                        st["rel_sc"][bk] = (VV, v)
                    else:
                        em.wait("scalar", TT, v_mm)
                        v = em.eop("scalar", lambda e, o=dst, i_=sc_banks[bk][:, 0:w]:
                                   e.activation(o, i_, AF.Identity), inc=AA)
                        st["rel_sc"][bk] = (AA, v)
                em.wait("vector", AA, em.cnt[AA])
                em.wait("vector", AA, st["v_exp"][psl])
                v_max = em.eop("vector",
                               lambda e, o=zc[:, psl * 4:psl * 4 + 1],
                               i_=srow[:, psl * S:psl * S + (L - k_lo)]:
                               e.reduce_max(o, i_, axis=AX.X, negate=True), inc=VV)
                st["v_max"][psl] = v_max

            def emit_exp_norm(it):
                i, qt, hw = it[0], it[1], it[2]
                L = (qt + 1) * 128 - it[3]
                psl = i % 3
                em.wait("scalar", VV, st["v_max"][psl])
                em.wait("scalar", TT, st["v_tpg"][psl])
                v_exp = em.eop("scalar",
                               lambda e, o=prow[:, psl * S:psl * S + L],
                               i_=srow[:, psl * S:psl * S + L],
                               b=zc[:, psl * 4:psl * 4 + 1],
                               a=zc[:, psl * 4 + 1:psl * 4 + 2]:
                               e.activation(o, i_, AF.Exp, bias=b, accum_out=a), inc=AA)
                st["v_exp"][psl] = v_exp

                em.wait("vector", AA, v_exp)
                self.chain("vector", lambda e, o=zc[:, psl * 4 + 2:psl * 4 + 3],
                           i_=zc[:, psl * 4 + 1:psl * 4 + 2]:
                           e.reciprocal(o, i_), VV)
                v_nrm = em.eop("vector",
                               lambda e, o=prow[:, psl * S:psl * S + L],
                               s=zc[:, psl * 4 + 2:psl * 4 + 3]:
                               e.tensor_scalar(o, o, s, None, ALU.mult), inc=VV)
                st["v_nrm"][psl] = v_nrm

            def emit_tp_av(it):
                i, qt, hw = it[0], it[1], it[2]
                k_lo = it[3]
                L = (qt + 1) * 128 - k_lo
                T = L // 128
                psl = i % 3
                em.wait("tensor", VV, st["v_nrm"][psl])
                v_e_last = 0
                v_tp_last = 0
                for g0 in range(0, T, 8):
                    gn = min(8, T - g0)
                    bk = st["tp_ctr"] % 2
                    st["tp_ctr"] += 1
                    em.waitrel("tensor", st["rel_tp"][bk])
                    for t8 in range(gn):
                        t = g0 + t8
                        fn = (lambda e, o=tp_banks[bk][:, t8 * 128:(t8 + 1) * 128],
                              i_=prow[:, psl * S + t * 128:psl * S + (t + 1) * 128],
                              idn=self.idb_sb:
                              e.transpose(o, i_, idn[:, :]))
                        if t8 == gn - 1:
                            v_tp_last = em.eop("tensor", fn, inc=TT)
                        else:
                            em.eop("tensor", fn)
                    em.wait("vector", TT, v_tp_last)
                    v_e_last = em.eop("vector",
                                      lambda e, o=pt_sb[:, g0 * 128:(g0 + gn) * 128],
                                      i_=tp_banks[bk][:, 0:gn * 128]:
                                      e.tensor_copy(o, i_), inc=VV)
                    st["rel_tp"][bk] = (VV, v_e_last)
                st["v_tpg"][psl] = v_tp_last

                apar = i % 2
                em.wait("tensor", VV, v_e_last)
                em.waitrel("tensor", st["rel_ao_bank"][apar])
                for t in range(T):
                    kt = k_lo // 128 + t
                    fn = (lambda e, o=ao_banks[apar][:, 0:128],
                          vt=vt_sb[:, hw * 4096 + kt * 128:hw * 4096 + (kt + 1) * 128],
                          pt=pt_sb[:, t * 128:(t + 1) * 128],
                          stt=(t == 0), sp=(t == T - 1):
                          e.matmul(o, vt, pt, start=stt, stop=sp))
                    if t == T - 1:
                        v_av = em.eop("tensor", fn, inc=TT)
                    else:
                        em.eop("tensor", fn)
                em.wait("vector", TT, v_av)
                em.waitrel("vector", st["rel_ao_dma"][hw])
                v_ao = em.eop("vector",
                              lambda e, o=ao_sb[:, hw * S + qt * 128:hw * S + (qt + 1) * 128],
                              i_=ao_banks[apar][:, 0:128]:
                              e.tensor_copy(o, i_), inc=VV)
                st["rel_ao_bank"][apar] = (VV, v_ao)
                return v_ao

            v_l = load_pair(0)
            em.waitrel("tensor", v_l)
            em.barrier()
            for p in range(NHL // 2):
                v_transposes(v_l)
                pend = []
                v_ao_last = [0, 0]
                for i in range(2 * NQT):
                    qt, hw = i // 2, i % 2
                    W = SLOT_W[2 * p + hw]
                    L = (qt + 1) * 128
                    k_lo = max(0, (L - W) // 512 * 512) if W < S else 0
                    it = (i, qt, hw, k_lo)
                    if len(pend) >= 3:
                        it3 = pend.pop(0)
                        v_ao_last[it3[2]] = emit_tp_av(it3)
                    emit_scores_evac_max(it)
                    pend.append(it)
                    if len(pend) >= 2:
                        emit_exp_norm(pend[-2])
                emit_exp_norm(pend[-1])
                for it3 in pend:
                    v_ao_last[it3[2]] = emit_tp_av(it3)
                st["rel_pair"] = (TT, em.cnt[TT])
                if p + 1 < NHL // 2:
                    v_l = load_pair(p + 1)
                for hw in range(2):
                    h = 2 * p + hw
                    ga = G0 if hw == 0 else G1
                    em.wait("gpsimd", VV, v_ao_last[hw])
                    v_d = self.dma("gpsimd", ga,
                                   self.ao_sc[h * 128:(h + 1) * 128, :],
                                   ao_sb[:, hw * S:(hw + 1) * S])
                    st["rel_ao_dma"][hw] = (ga, v_d)

    def phase5a(self):
        em = self.em
        self.p5 = contextlib.ExitStack()
        self.p5a = contextlib.ExitStack()

        self.ao5 = self.sbuf("p5_ao", [128, 8 * S], BF16, self.p5a, side="right")
        self.wo5 = self.sbuf("p5_wo", [128, 8 * S], BF16, self.p5a, tdest=True,
                             side="right")
        self.strip5 = self.sbuf("p5_strip", [128, 2 * 512], BF16, self.p5a,
                                side="right")
        self.pa = [self.psum("p5_a0", [128, 512], F32, self.p5),
                   self.psum("p5_a1", [128, 512], F32, self.p5)]

        for h in range(NHL):
            v_ao = self.dma("sync", L0, self.ao5[:, h * S:(h + 1) * S],
                            self.ao_sc[h * 128:(h + 1) * 128, :])
        for dc in range(NHL):
            v_wo = self.dma("sync", L5, self.wo5[:, dc * S:(dc + 1) * S],
                            self.wout_bf[:, dc * 128:(dc + 1) * 128], transpose=True)

        rel_bank = [None, None]
        rel_strip = [None, None]
        rs_done = []

        for oc in range(8):
            last_dma = 0
            for tt in range(32):
                bk = tt % 2
                em.waitrel("tensor", rel_bank[bk])
                em.wait("tensor", L0, v_ao)
                em.wait("tensor", L5, v_wo)
                for dc in range(8):
                    aoa = self.ao5[:, dc * S + tt * 128:dc * S + (tt + 1) * 128]
                    rhs = self.wo5[:, dc * S + oc * 512:dc * S + (oc + 1) * 512]
                    fn = (lambda e, o=self.pa[bk][:, :], a=aoa, r=rhs,
                          stt=(dc == 0), sp=(dc == 7):
                          e.matmul(o, a, r, start=stt, stop=sp))
                    if dc == 7:
                        v_mm = em.eop("tensor", fn, inc=TT)
                    else:
                        em.eop("tensor", fn)

                ssl = tt % 2
                em.wait("scalar", TT, v_mm)
                em.waitrel("scalar", rel_strip[ssl])
                dsts = self.strip5[:, ssl * 512:(ssl + 1) * 512]
                v_ev = em.eop("scalar", lambda e, o=dsts, i_=self.pa[bk][:, :]:
                              e.activation(o, i_, AF.Identity), inc=AA)
                rel_bank[bk] = (AA, v_ev)

                em.wait("gpsimd", AA, v_ev)
                gs = G0 if ssl == 0 else G1
                last_dma = self.dma("gpsimd", gs,
                                    self.ypart[tt * 128:(tt + 1) * 128,
                                               oc * 512:(oc + 1) * 512], dsts)
                rel_strip[ssl] = (gs, last_dma)
            rel_wo = (TT, em.cnt[TT])
        self.rs_done = rs_done

    def emit_collectives(self):
        em = self.em
        em.barrier()
        v_cc = em.inc(CC)

        def f(e, SM, i_=self.ypart, o=self.yred):
            e.collective_compute(
                "ReduceScatter", ALU.add, replica_groups=RS_GROUPS,
                ins=[i_.ap().opt()], outs=[o.ap().opt()]).then_inc(SM[CC], 1)
        em.ops["gpsimd"].append(f)
        self.rs_done = [v_cc] * 8
        em.wait("gpsimd", CC, em.cnt[CC])
        em.barrier()

    def phase5b(self):
        em = self.em
        with self.p5:
            p5 = self.p5
            tpb = [self.psum("p5_tp0", [128, 1024], BF16, p5),
                   self.psum("p5_tp1", [128, 1024], BF16, p5)]
            lgp = self.psum("p5_lgp", [128, 512], F32, p5)
            wdtp = self.psum("p5_wdtp", [128, 512], BF16, p5)
            bmp = [self.psum("p5_bm0", [128, 512], F32, p5),
                   self.psum("p5_bm1", [128, 512], F32, p5)]

            v_rwfm = 0
            with contextlib.ExitStack() as prep:
                rwf = self.sbuf("p5_rwf", [64, DM], F32, prep)
                rwb = self.sbuf("p5_rwb", [128, DM], BF16, prep)
                ebf = self.sbuf("p5_ebf", [64, DM], F32, prep)

                if self.p5b_stage >= 1:
                    v_rw = self.dma("sync", LC, rwf[:, :], self.rw_in[:, :])
                    v_eb = self.dma("sync", LC, ebf[:, :], self.eb_in[:, :])
                    em.wait("scalar", LC, v_eb)
                    em.eop("scalar", lambda e, o=rwb, i_=rwf:
                           e.activation(o[0:64, :], i_[:, :], AF.Identity))
                    em.wait("scalar", LC, v_eb)
                    v_casts = em.eop("scalar", lambda e, o=self.eb_bf, i_=ebf:
                                     e.activation(o[:, :], i_[:, :], AF.Identity), inc=AA)

                if self.p5b_stage >= 2:
                    em.wait("tensor", AA, v_casts)
                    v_ev_g = [0, 0]
                    for g in range(4):
                        bk = g % 2
                        em.wait("tensor", VV, v_ev_g[bk])
                        for t8 in range(8):
                            dc = g * 8 + t8
                            fn = (lambda e, o=tpb[bk][:, t8 * 128:(t8 + 1) * 128],
                                  i_=rwb[:, dc * 128:(dc + 1) * 128], idn=self.idb_sb:
                                  e.transpose(o, i_, idn[:, :]))
                            if t8 == 7:
                                v_t = em.eop("tensor", fn, inc=TT)
                            else:
                                em.eop("tensor", fn)
                        if self.p5b_stage < 3:
                            continue
                        em.wait("vector", TT, v_t)
                        for t8 in range(8):
                            dc = g * 8 + t8
                            fn = (lambda e, o=self.rw_fm[:, dc * NE:(dc + 1) * NE],
                                  i_=tpb[bk][:, t8 * 128:t8 * 128 + NE]:
                                  e.tensor_copy(o, i_))
                            if t8 == 7:
                                v_rwfm = em.eop("vector", fn, inc=VV)
                            else:
                                em.eop("vector", fn)
                        v_ev_g[bk] = v_rwfm
                em.barrier()

            self.emit_collectives()
            self.p5a.close()

            y_ring = self.sbuf("p5_y", [128, 2 * DM], F32, p5)
            xn5 = self.sbuf("p5_xn5", [128, 2 * DM], BF16, p5)
            xnfm = self.sbuf("p5_xnfm", [128, 32 * 128], BF16, p5)
            xnw = self.sbuf("p5_xnw", [128, DM], F32, p5)
            bout_sb = self.sbuf("p5_bout", [128, DM], F32, p5)
            yb = self.sbuf("p5_yb", [128, 4 * 512], BF16, p5)
            st5 = self.sbuf("p5_st5", [128, 16], F32, p5)
            lg = self.sbuf("p5_lg", [128, NE], F32, p5)
            ex5 = self.sbuf("p5_ex5", [128, NE], F32, p5)
            wk5 = self.sbuf("p5_wk5", [128, NE], F32, p5)
            eq5 = self.sbuf("p5_eq5", [128, NE], F32, p5)
            wd5 = self.sbuf("p5_wd5", [128, NE], F32, p5)
            wdb = self.sbuf("p5_wdb", [128, NE], BF16, p5)
            wdt = self.sbuf("p5_wdt", [64, 128], BF16, p5)
            cols5 = self.sbuf("p5_c5", [128, 8], F32, p5)

            v_bout = self.dma("sync", LC, bout_sb[:, :], self.bout_in[:, :])

            rel_y = [None, None]
            rel_xn5 = [None, None]
            rel_tpb = [(VV, v_rwfm), (VV, v_rwfm)]
            rel_lgp = None
            rel_wdtp = None
            rel_bmp = [None, None]
            rel_yb = [None, None, None, None]
            rel_st5 = [[None, None], [None, None]]
            v_ex_prev = 0

            for t5 in range(self.t5_count):
                sl = t5 % 2
                ya = y_ring[:, sl * DM:(sl + 1) * DM]
                xa = xn5[:, sl * DM:(sl + 1) * DM]
                c0 = sl * 8

                sx = L0 if sl == 0 else L1
                em.waitrel("sync", rel_y[sl])
                v_x = self.dma("sync", sx, ya, self.xq_in[t5 * 128:(t5 + 1) * 128, :])
                em.wait("vector", sx, v_x)
                em.wait("scalar", sx, v_x)
                em.waitrel("scalar", rel_xn5[sl])

                v_rstd, v_sq, v_std = self.ln_stats(
                    ya, st5, c0, rel_st5[sl][0], rel_st5[sl][1], dump=xa)

                em.waitrel("vector", rel_xn5[sl])
                v_xn = em.eop("vector", lambda e, o=xa, i_=ya, a=st5, c=c0:
                              e.tensor_scalar(o, i_, a[:, c + 1:c + 2], a[:, c + 7:c + 8],
                                              ALU.add, ALU.mult), inc=VV)

                em.wait("tensor", VV, v_xn)
                ev_last = 0
                for g in range(4):
                    bk = g % 2
                    em.waitrel("tensor", rel_tpb[bk])
                    for t8 in range(8):
                        dc = g * 8 + t8
                        fn = (lambda e, o=tpb[bk][:, t8 * 128:(t8 + 1) * 128],
                              i_=xa[:, dc * 128:(dc + 1) * 128], idn=self.idb_sb:
                              e.transpose(o, i_, idn[:, :]))
                        if t8 == 7:
                            v_t = em.eop("tensor", fn, inc=TT)
                        else:
                            em.eop("tensor", fn)
                    em.wait("vector", TT, v_t)
                    ev_last = em.eop("vector",
                                     lambda e, o=xnfm[:, g * 1024:(g + 1) * 1024],
                                     i_=tpb[bk]:
                                     e.tensor_copy(o, i_[:, :]), inc=VV)
                    rel_tpb[bk] = (VV, ev_last)

                em.wait("tensor", VV, ev_last)
                em.waitrel("tensor", rel_lgp)
                for dc in range(32):
                    fn = (lambda e, o=lgp[:, 0:NE], x=xnfm[:, dc * 128:(dc + 1) * 128],
                          r=self.rw_fm[:, dc * NE:(dc + 1) * NE],
                          stt=(dc == 0), sp=(dc == 31):
                          e.matmul(o, x, r, start=stt, stop=sp))
                    if dc == 31:
                        v_lg = em.eop("tensor", fn, inc=TT)
                    else:
                        em.eop("tensor", fn)

                em.wait("vector", TT, v_lg)
                em.wait("vector", LC, self.v_consts)
                v_lgs = self.chain("vector", lambda e, o=lg, i_=lgp, r=self.rb_sb:
                                   e.tensor_tensor(o[:, :], i_[:, 0:NE], r[:, :], ALU.add),
                                   VV)
                rel_lgp = (VV, v_lgs)

                em.wait("vector", AA, v_ex_prev)
                v_m = em.eop("vector", lambda e, o=cols5, i_=lg:
                             e.reduce_max(o[:, 0:1], i_[:, :], axis=AX.X, negate=True),
                             inc=VV)
                em.wait("scalar", VV, v_m)
                v_ex = em.eop("scalar", lambda e, o=ex5, i_=lg, b=cols5:
                              e.activation(o[:, :], i_[:, :], AF.Exp, bias=b[:, 0:1],
                                           accum_out=b[:, 1:2]), inc=AA)
                v_ex_prev = v_ex
                em.wait("vector", AA, v_ex)

                self.chain("vector", lambda e, o=wk5, i_=ex5:
                           e.tensor_copy(o[:, :], i_[:, :]), VV)
                for it in range(KTOP - 1):
                    self.chain("vector", lambda e, o=cols5, i_=wk5:
                               e.reduce_max(o[:, 2:3], i_[:, :], axis=AX.X), VV)
                    self.chain("vector", lambda e, o=eq5, i_=wk5, m=cols5:
                               e.tensor_scalar(o[:, :], i_[:, :], m[:, 2:3], 1.0e7,
                                               ALU.is_equal, ALU.mult), VV)
                    self.chain("vector", lambda e, o=wk5, i_=eq5:
                               e.tensor_tensor(o[:, :], o[:, :], i_[:, :], ALU.subtract), VV)
                self.chain("vector", lambda e, o=cols5, i_=wk5:
                           e.reduce_max(o[:, 3:4], i_[:, :], axis=AX.X), VV)
                self.chain("vector", lambda e, o=eq5, i_=ex5, t=cols5:
                           e.tensor_scalar(o[:, :], i_[:, :], t[:, 3:4], None, ALU.is_ge), VV)
                self.chain("vector", lambda e, o=wd5, a=ex5, b=eq5:
                           e.tensor_tensor(o[:, :], a[:, :], b[:, :], ALU.mult), VV)
                self.chain("vector", lambda e, o=cols5:
                           e.reciprocal(o[:, 4:5], o[:, 1:2]), VV)
                self.chain("vector", lambda e, o=wd5, r=cols5:
                           e.tensor_scalar(o[:, :], o[:, :], r[:, 4:5], None, ALU.mult,
                                           ALU.add, accum_out=r[:, 5:6]), VV)
                v_wd = em.eop("vector", lambda e, o=wdb, i_=wd5:
                              e.tensor_copy(o[:, :], i_[:, :]), inc=VV)

                em.wait("tensor", VV, v_wd)
                em.waitrel("tensor", rel_wdtp)
                v_wt = em.eop("tensor", lambda e, o=wdtp[0:64, 0:128],
                              i_=wdb[:, 0:NE], idn=self.idb_sb:
                              e.transpose(o, i_, idn[:, :]), inc=TT)
                em.wait("vector", TT, v_wt)
                v_wte = em.eop("vector", lambda e, o=wdt, i_=wdtp:
                               e.tensor_copy(o[:, :], i_[0:64, 0:128]), inc=VV)
                rel_wdtp = (VV, v_wte)

                self.chain("vector", lambda e, o=cols5, s=st5, c=c0:
                           e.tensor_tensor(o[:, 6:7], s[:, c + 7:c + 8], o[:, 5:6], ALU.mult), VV)
                em.eop("vector", lambda e, o=xnw, i_=ya, s=st5, c=c0, r=cols5:
                       e.tensor_scalar(o[:, :], i_, s[:, c + 1:c + 2], r[:, 6:7],
                                       ALU.add, ALU.mult))
                em.wait("vector", LC, v_bout)
                em.eop("vector", lambda e, o=xnw, b=bout_sb:
                       e.tensor_tensor(o[:, :], o[:, :], b[:, :], ALU.add))
                em.eop("vector", lambda e, o=ya, a=xnw:
                       e.tensor_tensor(o, o, a[:, :], ALU.add))

                for oc in range(8):
                    bk = oc % 2
                    em.wait("tensor", VV, v_wte)
                    em.waitrel("tensor", rel_bmp[bk])
                    v_bm = em.eop("tensor", lambda e, o=bmp[bk][:, :], w=wdt,
                                  b=self.eb_bf[:, oc * 512:(oc + 1) * 512]:
                                  e.matmul(o, w[:, :], b, start=True, stop=True), inc=TT)

                    ybsl = oc % 4
                    sy = (L2, L3, L4, L5)[ybsl]
                    em.wait("sync", CC, self.rs_done[oc])
                    em.waitrel("sync", rel_yb[ybsl])
                    v_yr = self.dma("sync", sy, yb[:, ybsl * 512:(ybsl + 1) * 512],
                                    self.yred[t5 * 128:(t5 + 1) * 128,
                                              oc * 512:(oc + 1) * 512])

                    em.wait("vector", sy, v_yr)
                    em.wait("vector", TT, v_bm)
                    em.eop("vector", lambda e, o=ya[:, oc * 512:(oc + 1) * 512],
                           r=yb[:, ybsl * 512:(ybsl + 1) * 512]:
                           e.tensor_tensor(o, o, r, ALU.add))
                    v_add = em.eop("vector", lambda e, o=ya[:, oc * 512:(oc + 1) * 512],
                                   b=bmp[bk][:, :]:
                                   e.tensor_tensor(o, o, b, ALU.add), inc=VV)
                    rel_bmp[bk] = (VV, v_add)
                    rel_yb[ybsl] = (VV, v_add)

                v_fin = em.cnt[VV]
                rel_xn5[sl] = (TT, em.cnt[TT])
                rel_st5[sl][0] = (AA, v_std)
                rel_st5[sl][1] = (VV, v_fin)
                gy = G0 if sl == 0 else G1
                em.wait("gpsimd", VV, v_fin)
                v_store = self.dma("gpsimd", gy, self.y_out[t5 * 128:(t5 + 1) * 128, :], ya)
                rel_y[sl] = (gy, v_store)

            em.wait("gpsimd", G0, em.cnt[G0])
            em.wait("gpsimd", G1, em.cnt[G1])

    def emit_debug(self):
        em = self.em
        if not self.dbg:
            return
        em.barrier()
        src_map = {"dbg_xn": self.xn_sc, "dbg_wq": self.wq_sc, "dbg_qkv": self.qkv_sc,
                   "dbg_ao": self.ao_sc, "dbg_wout": self.wout_bf}
        if self.upto >= 4:
            src_map["dbg_ypart"] = self.ypart
            src_map["dbg_yred"] = self.yred
        for name, dst in self.dbg.items():
            src = src_map[name]
            v = self.dma("gpsimd", G0, dst.ap(), src.ap())
            em.wait("gpsimd", G0, v)



def _alibi_slopes(n_heads=32, alibi_bias_max=8):
    _n = 2 ** math.ceil(math.log2(n_heads))
    m = np.arange(1, _n + 1, dtype=np.float64) * (alibi_bias_max / _n)
    slopes = 1.0 / np.power(2.0, m)
    if _n != n_heads:
        slopes = np.concatenate([slopes[1::2], slopes[::2]])[:n_heads]
    return slopes


def _build_const_inputs(heads):
    bf = ml_dtypes.bfloat16
    slopes = _alibi_slopes()
    pos = np.arange(S, dtype=np.float64) - (S - 1)
    al3 = np.zeros((3 * NHL, S), dtype=bf)
    for i, h in enumerate(heads):
        a = slopes[h] * pos
        hi = a.astype(np.float32).astype(bf)
        r1 = a - hi.astype(np.float64)
        mid = r1.astype(np.float32).astype(bf)
        r2 = r1 - mid.astype(np.float64)
        lo = r2.astype(np.float32).astype(bf)
        al3[3 * i + 0] = hi
        al3[3 * i + 1] = mid
        al3[3 * i + 2] = lo

    masks = np.zeros((4 * 128, 512), dtype=np.float32)
    ii = np.arange(128)[:, None]
    jj = np.arange(512)[None, :]
    for v in range(4):
        wd = (v + 1) * 128
        m = np.where(jj <= (wd - 128) + ii, 0.0, np.float32(NEG_BIG)).astype(np.float32)
        m[:, wd:] = 0.0
        masks[v * 128:(v + 1) * 128] = m

    ones3 = np.ones((3, 128), dtype=bf)
    idb = np.eye(128, dtype=np.float32).astype(bf)
    return al3, masks, ones3, idb


_BUILD_CACHE = {}


def get_program(upto=5, debug=(), t5_count=8, p5b_stage=3):
    key = (upto, tuple(sorted(debug)), t5_count, p5b_stage)
    if key not in _BUILD_CACHE:
        _BUILD_CACHE[key] = Builder(upto=upto, debug=debug, t5_count=t5_count,
                                    p5b_stage=p5b_stage).build()
    return _BUILD_CACHE[key]


def make_in_maps(L0_attn_x, L0_attn_w_qkv, L0_attn_b_qkv, L0_attn_w_out,
                 L0_attn_b_out, L0_ffn_router_weight, L0_ffn_router_bias,
                 L0_ffn_expert_bias):
    x = np.ascontiguousarray(np.asarray(L0_attn_x, dtype=np.float32))
    wqkv = np.asarray(L0_attn_w_qkv, dtype=np.float32)
    bqkv = np.asarray(L0_attn_b_qkv, dtype=np.float32)
    wout = np.asarray(L0_attn_w_out, dtype=np.float32)
    bout = np.asarray(L0_attn_b_out, dtype=np.float32)
    rw = np.ascontiguousarray(np.asarray(L0_ffn_router_weight, dtype=np.float32))
    rb = np.asarray(L0_ffn_router_bias, dtype=np.float32)
    eb = np.ascontiguousarray(np.asarray(L0_ffn_expert_bias, dtype=np.float32))

    in_maps = []
    for c in range(8):
        b, g = c // 4, c % 4
        heads = [g + 4 * j for j in range(NHL)]
        rows = []
        for kind in range(3):
            for h in heads:
                rows.append(wqkv[kind * DM + 128 * h: kind * DM + 128 * (h + 1)])
        wqkv_c = np.ascontiguousarray(np.concatenate(rows, axis=0))
        bq = []
        for kind in range(3):
            for h in heads:
                bq.append(bqkv[kind * DM + 128 * h: kind * DM + 128 * (h + 1)])
        bqkv_c = np.ascontiguousarray(np.stack(bq, axis=1).astype(np.float32))
        wout_c = np.ascontiguousarray(
            np.concatenate([wout[:, 128 * h:128 * (h + 1)] for h in heads], axis=1))
        al3, masks, ones3, idb = _build_const_inputs(heads)
        in_maps.append({
            "x": x[b],
            "xq": np.ascontiguousarray(x[b, 1024 * g:1024 * (g + 1)]),
            "wqkv": wqkv_c,
            "bqkv": bqkv_c,
            "wout": wout_c,
            "bout_bc": np.ascontiguousarray(np.tile(bout[None, :], (128, 1))),
            "rw": rw,
            "rb_bc": np.ascontiguousarray(np.tile(rb[None, :], (128, 1))),
            "eb": eb,
            "alibi3": al3,
            "ones3": ones3,
            "masks": masks,
            "idb": idb,
        })
    return in_maps


def kernel(L0_attn_x, L0_attn_w_qkv, L0_attn_b_qkv, L0_attn_w_out, L0_attn_b_out,
           L0_ffn_router_weight, L0_ffn_router_bias, L0_ffn_expert_bias):
    nc = get_program()
    in_maps = make_in_maps(L0_attn_x, L0_attn_w_qkv, L0_attn_b_qkv, L0_attn_w_out,
                           L0_attn_b_out, L0_ffn_router_weight, L0_ffn_router_bias,
                           L0_ffn_expert_bias)
    res = run_bass_kernel_spmd(nc, in_maps, core_ids=list(range(8)))
    out = np.empty((2, S, DM), dtype=np.float32)
    for c in range(8):
        b, g = c // 4, c % 4
        out[b, 1024 * g:1024 * (g + 1), :] = res.results[c]["y"]
    return out


# version 12 (final)
# speedup vs baseline: 1.1384x; 1.0017x over previous
import math
import contextlib

import numpy as np
import ml_dtypes

import concourse.bass as bass
import concourse.mybir as mybir
from concourse.bass_utils import run_bass_kernel_spmd

F32 = mybir.dt.float32
BF16 = mybir.dt.bfloat16
AF = mybir.ActivationFunctionType
ALU = mybir.AluOpType
AX = mybir.AxisListType

S = 4096
DM = 4096
HD = 128
NHL = 8
NCHUNK = 3 * NHL
NT = S // 128
NTBLK = 8
NQT = 32
NE = 64
KTOP = 8
Q_SCALE = 1.0 / math.sqrt(HD)
NEG_BIG = -1.0e9
EPS = 1e-5
RS_GROUPS = [[0, 1, 2, 3], [4, 5, 6, 7]]

def _slot_window(j):
    slope = 2.0 ** (-(4 * j + 3 + 1) / 4.0)
    w = 384.0 / slope
    if w >= S:
        return S
    return int(math.ceil(w / 512.0) * 512)

SLOT_W = [_slot_window(j) for j in range(NHL)]

L0, L1, L2, L3, L4, L5, LC = "l0", "l1", "l2", "l3", "l4", "l5", "lc"
G0, G1, G2, G3 = "g0", "g1", "g2", "g3"
TT, VV, AA, CC = "tt", "vv", "aa", "cc"
ALL_SEMS = (L0, L1, L2, L3, L4, L5, LC, G0, G1, G2, G3, TT, VV, AA, CC)


class Em:

    ENGS = ("sync", "tensor", "vector", "scalar", "gpsimd")

    def __init__(self):
        self.ops = {e: [] for e in self.ENGS}
        self.cnt = {n: 0 for n in ALL_SEMS}
        self.lastwait = {e: {} for e in self.ENGS}

    def inc(self, sem, by=1):
        self.cnt[sem] += by
        return self.cnt[sem]

    def wait(self, eng, sem, val):
        if val is None or val <= 0:
            return
        if self.lastwait[eng].get(sem, 0) >= val:
            return
        self.lastwait[eng][sem] = val

        def f(e, SM, sem=sem, val=val):
            e.wait_ge(SM[sem], val)

        self.ops[eng].append(f)

    def waitrel(self, eng, rel):
        if rel is None:
            return
        if isinstance(rel, list):
            for r in rel:
                self.waitrel(eng, r)
        else:
            self.wait(eng, rel[0], rel[1])

    def op(self, eng, fn):
        self.ops[eng].append(lambda e, SM, fn=fn: fn(e))

    def eop(self, eng, fn, inc=None):
        if inc is None:
            self.op(eng, fn)
            return None
        v = self.inc(inc)

        def g(e, SM, fn=fn, s=inc):
            fn(e).then_inc(SM[s], 1)

        self.ops[eng].append(g)
        return v

    def barrier(self):
        snap = dict(self.cnt)
        for e in self.ENGS:
            for s, v in snap.items():
                self.wait(e, s, v)


class Builder:
    def __init__(self, upto=5, debug=(), t5_count=8, p5b_stage=3):
        self.upto = upto
        self.t5_count = t5_count
        self.p5b_stage = p5b_stage
        self.debug = debug
        self.nc = bass.Bass("TRN2", target_bir_lowering=False)
        self.em = Em()
        self.stack = contextlib.ExitStack()

    def sbuf(self, name, shape, dtype, st=None, tdest=False, side=None):
        return (st or self.stack).enter_context(
            self.nc.sbuf_tensor(name, shape, dtype, is_transpose_dest=tdest,
                                side=side))

    def psum(self, name, shape, dtype, st=None):
        return (st or self.stack).enter_context(self.nc.psum_tensor(name, shape, dtype))

    def chain(self, eng, fn, sem):
        v = self.em.eop(eng, fn, inc=sem)
        self.em.wait(eng, sem, v)
        return v

    def dma(self, eng, sem, out, in_, transpose=False):
        v = self.em.inc(sem, 16)

        if transpose:
            def f(e, SM, o=out, i=in_, s=sem):
                e.dma_start_transpose(out=o, in_=i).then_inc(SM[s], 16)
        else:
            def f(e, SM, o=out, i=in_, s=sem):
                e.dma_start(out=o, in_=i).then_inc(SM[s], 16)

        self.em.ops[eng].append(f)
        return v

    def build(self):
        nc = self.nc
        em = self.em

        self.x_in = nc.dram_tensor("x", [S, DM], F32, kind="ExternalInput")
        self.xq_in = nc.dram_tensor("xq", [1024, DM], F32, kind="ExternalInput")
        self.wqkv_in = nc.dram_tensor("wqkv", [NCHUNK * 128, DM], F32, kind="ExternalInput")
        self.bqkv_in = nc.dram_tensor("bqkv", [128, NCHUNK], F32, kind="ExternalInput")
        self.wout_in = nc.dram_tensor("wout", [DM, NHL * 128], F32, kind="ExternalInput")
        self.bout_in = nc.dram_tensor("bout_bc", [128, DM], F32, kind="ExternalInput")
        self.rw_in = nc.dram_tensor("rw", [NE, DM], F32, kind="ExternalInput")
        self.rb_in = nc.dram_tensor("rb_bc", [128, NE], F32, kind="ExternalInput")
        self.eb_in = nc.dram_tensor("eb", [NE, DM], F32, kind="ExternalInput")
        self.alibi3_in = nc.dram_tensor("alibi3", [3 * NHL, S], BF16, kind="ExternalInput")
        self.ones3_in = nc.dram_tensor("ones3", [3, 128], BF16, kind="ExternalInput")
        self.masks_in = nc.dram_tensor("masks", [4 * 128, 512], F32, kind="ExternalInput")
        self.idb_in = nc.dram_tensor("idb", [128, 128], BF16, kind="ExternalInput")
        self.y_out = nc.dram_tensor("y", [1024, DM], F32, kind="ExternalOutput")

        self.xn_sc = nc.dram_tensor("xn_sc", [S, DM], BF16)
        self.wq_sc = nc.dram_tensor("wq_sc", [128, NCHUNK * 32 * 128], BF16)
        self.qkv_sc = nc.dram_tensor("qkv_sc", [NCHUNK * 128, S], BF16)
        self.ao_sc = nc.dram_tensor("ao_sc", [NHL * 128, S], BF16)
        self.wout_bf = nc.dram_tensor("wout_bf", [DM, NHL * 128], BF16)
        self.ypart = nc.dram_tensor("ypart", [S, DM], BF16)
        self.yred = nc.dram_tensor("yred", [1024, DM], BF16)

        dbg = {}
        for name, shape, dt in (("dbg_xn", [S, DM], BF16),
                                ("dbg_wq", [128, NCHUNK * 32 * 128], BF16),
                                ("dbg_qkv", [NCHUNK * 128, S], BF16),
                                ("dbg_ao", [NHL * 128, S], BF16),
                                ("dbg_wout", [DM, NHL * 128], BF16),
                                ("dbg_ypart", [S, DM], BF16),
                                ("dbg_yred", [1024, DM], BF16)):
            if name in self.debug:
                dbg[name] = nc.dram_tensor(name, shape, dt, kind="ExternalOutput")
        self.dbg = dbg

        with self.stack:
            sems = {}
            for n in ALL_SEMS:
                sems[n] = self.stack.enter_context(nc.semaphore(n))

            self.masks_sb = self.sbuf("masks_sb", [128, 4 * 512], F32)
            self.idb_sb = self.sbuf("idb_sb", [128, 128], BF16)
            self.ones3_sb = self.sbuf("ones3_sb", [3, 128], BF16)
            self.bqs_sb = self.sbuf("bqs_sb", [128, NCHUNK], F32)
            self.rw_fm = self.sbuf("rw_fm", [128, 32 * NE], BF16)
            self.eb_bf = self.sbuf("eb_bf", [NE, DM], BF16)
            self.rb_sb = self.sbuf("rb_sb", [128, NE], F32)

            for v in range(4):
                self.dma("sync", LC, self.masks_sb[:, v * 512:(v + 1) * 512],
                         self.masks_in[v * 128:(v + 1) * 128, :])
            self.dma("sync", LC, self.idb_sb[:, :], self.idb_in[:, :])
            self.dma("sync", LC, self.ones3_sb[:, :], self.ones3_in[:, :])
            self.v_consts = self.dma("sync", LC, self.rb_sb[:, :], self.rb_in[:, :])

            self.phase1()
            em.barrier()
            if self.upto >= 2:
                self.phase2()
                em.barrier()
            if self.upto >= 3:
                self.phase3()
                em.barrier()
            if self.upto >= 4:
                self.phase5a()
            if self.upto >= 5:
                self.phase5b()
            elif self.upto == 4:
                self.emit_collectives()
                self.p5.close()

            self.emit_debug()
            em.barrier()

            with nc.Block() as block:
                ops = em.ops

                @block.sync
                def _(e):
                    for f in ops["sync"]:
                        f(e, sems)

                @block.tensor
                def _(e):
                    for f in ops["tensor"]:
                        f(e, sems)

                @block.vector
                def _(e):
                    for f in ops["vector"]:
                        f(e, sems)

                @block.scalar
                def _(e):
                    for f in ops["scalar"]:
                        f(e, sems)

                @block.gpsimd
                def _(e):
                    for f in ops["gpsimd"]:
                        f(e, sems)

        return nc

    def ln_stats(self, xa, stat, c0, prev_act_rel, prev_vec_rel, dump):
        em = self.em
        em.waitrel("vector", prev_act_rel)

        self.chain("vector", lambda e, o=stat, c=c0, i_=xa:
                   e.reduce_sum(o[:, c:c + 1], i_, axis=AX.X, negate=True), VV)
        self.chain("vector", lambda e, o=stat, c=c0:
                   e.tensor_scalar(o[:, c + 1:c + 2], o[:, c:c + 1], 1.0 / DM, None, ALU.mult), VV)

        em.waitrel("scalar", prev_vec_rel)

        v_sq = em.eop("scalar", lambda e, o=dump, i_=xa, a=stat, c=c0:
                      e.activation(o, i_, AF.Square, accum_out=a[:, c + 2:c + 3]),
                      inc=AA)

        em.wait("vector", AA, v_sq)
        self.chain("vector", lambda e, o=stat, c=c0:
                   e.tensor_scalar(o[:, c + 3:c + 4], o[:, c + 2:c + 3], 1.0 / DM, None, ALU.mult), VV)
        self.chain("vector", lambda e, o=stat, c=c0:
                   e.tensor_tensor(o[:, c + 4:c + 5], o[:, c + 1:c + 2], o[:, c + 1:c + 2], ALU.mult), VV)
        self.chain("vector", lambda e, o=stat, c=c0:
                   e.tensor_tensor(o[:, c + 5:c + 6], o[:, c + 3:c + 4], o[:, c + 4:c + 5], ALU.subtract), VV)
        v_var = self.chain("vector", lambda e, o=stat, c=c0:
                           e.tensor_scalar(o[:, c + 5:c + 6], o[:, c + 5:c + 6], EPS, None, ALU.add), VV)

        em.wait("scalar", VV, v_var)
        v_std = em.eop("scalar", lambda e, o=stat, c=c0:
                       e.activation(o[:, c + 6:c + 7], o[:, c + 5:c + 6], AF.Sqrt), inc=AA)

        em.wait("vector", AA, v_std)
        v_rstd = self.chain("vector", lambda e, o=stat, c=c0:
                            e.reciprocal(o[:, c + 7:c + 8], o[:, c + 6:c + 7]), VV)
        return v_rstd, v_sq, v_std

    def phase1(self):
        em = self.em

        with contextlib.ExitStack() as p1:
            x_ring = self.sbuf("p1_x", [128, 2 * DM], F32, p1)
            xn_ring = self.sbuf("p1_xn", [128, 2 * DM], BF16, p1)
            stat = self.sbuf("p1_stat", [128, 16], F32, p1)
            wrow_f = self.sbuf("p1_wf", [128, 2 * DM], F32, p1)
            wrow_b = self.sbuf("p1_wb", [128, 2 * DM], BF16, p1)
            wqe = self.sbuf("p1_wqe", [128, 2 * 1024], BF16, p1)
            bq_raw = self.sbuf("p1_bq", [128, NCHUNK], F32, p1)
            tp_bank = [self.psum("p1_tp0", [128, 1024], BF16, p1),
                       self.psum("p1_tp1", [128, 1024], BF16, p1)]

            v_bq = self.dma("sync", LC, bq_raw[:, :], self.bqkv_in[:, :])
            em.wait("vector", LC, v_bq)
            self.chain("vector", lambda e, o=self.bqs_sb, i=bq_raw:
                       e.tensor_scalar(o[:, 0:NCHUNK], i[:, 0:NCHUNK], 1.0, None, ALU.mult), VV)
            self.chain("vector", lambda e, o=self.bqs_sb, i=bq_raw:
                       e.tensor_scalar(o[:, 0:NHL], i[:, 0:NHL], Q_SCALE, None, ALU.mult), VV)

            rel_x = [None, None]
            rel_xn = [None, None]
            rel_stat_a = [None, None]
            rel_stat_v = [None, None]
            rel_wf = [None, None]
            rel_wb = [None, None]
            rel_wqe = [None, None]
            rel_tp = [None, None]

            for i in range(NT):
                sl = i % 2
                xa = x_ring[:, sl * DM:(sl + 1) * DM]
                xna = xn_ring[:, sl * DM:(sl + 1) * DM]
                c0 = sl * 8

                sx = L0 if sl == 0 else L1
                em.waitrel("sync", rel_x[sl])
                v_ld = self.dma("sync", sx, xa, self.x_in[i * 128:(i + 1) * 128, :])
                em.wait("vector", sx, v_ld)
                em.wait("scalar", sx, v_ld)
                em.waitrel("scalar", rel_xn[sl])

                v_rstd, v_sq, v_std = self.ln_stats(
                    xa, stat, c0, rel_stat_a[sl], rel_stat_v[sl], dump=xna)

                em.waitrel("vector", rel_xn[sl])
                v_xn = em.eop("vector", lambda e, o=xna, i_=xa, a=stat, c=c0:
                              e.tensor_scalar(o, i_, a[:, c + 1:c + 2], a[:, c + 7:c + 8],
                                              ALU.add, ALU.mult), inc=VV)
                rel_x[sl] = [(VV, v_xn), (AA, v_sq)]
                rel_stat_a[sl] = (AA, v_std)
                rel_stat_v[sl] = (VV, v_xn)

                gx = G0 if sl == 0 else G1
                em.wait("gpsimd", VV, v_xn)
                v_w = self.dma("gpsimd", gx, self.xn_sc[i * 128:(i + 1) * 128, :], xna)
                rel_xn[sl] = (gx, v_w)

                if i < NCHUNK:
                    j = i
                    ws = j % 2
                    wfa = wrow_f[:, ws * DM:(ws + 1) * DM]
                    wba = wrow_b[:, ws * DM:(ws + 1) * DM]
                    sw = L2 if ws == 0 else L3
                    em.waitrel("sync", rel_wf[ws])
                    v_wl = self.dma("sync", sw, wfa, self.wqkv_in[j * 128:(j + 1) * 128, :])
                    em.wait("scalar", sw, v_wl)
                    em.waitrel("scalar", rel_wb[ws])
                    v_cast = em.eop("scalar", lambda e, o=wba, i_=wfa:
                                    e.activation(o, i_, AF.Identity), inc=AA)
                    rel_wf[ws] = (AA, v_cast)

                    em.wait("tensor", AA, v_cast)
                    v_tp_last = 0
                    for g in range(4):
                        bk = (j * 4 + g) % 2
                        es = (j * 4 + g) % 2
                        em.waitrel("tensor", rel_tp[bk])
                        for t8 in range(8):
                            t = g * 8 + t8
                            fn = (lambda e, o=tp_bank[bk][:, t8 * 128:(t8 + 1) * 128],
                                  i_=wba[:, t * 128:(t + 1) * 128], idn=self.idb_sb:
                                  e.transpose(o, i_, idn[:, :]))
                            if t8 == 7:
                                v_tp = em.eop("tensor", fn, inc=TT)
                            else:
                                em.eop("tensor", fn)
                        v_tp_last = v_tp
                        em.wait("vector", TT, v_tp)
                        em.waitrel("vector", rel_wqe[es])
                        dst = wqe[:, es * 1024:(es + 1) * 1024]
                        v_ev = em.eop("vector", lambda e, o=dst, i_=tp_bank[bk]:
                                      e.tensor_copy(o, i_[:, :]), inc=VV)
                        rel_tp[bk] = (VV, v_ev)
                        ge = G2 if es == 0 else G3
                        em.wait("gpsimd", VV, v_ev)
                        col = (j * 32 + g * 8) * 128
                        v_d = self.dma("gpsimd", ge,
                                       self.wq_sc[:, col:col + 1024], dst)
                        rel_wqe[es] = (ge, v_d)
                    rel_wb[ws] = (TT, v_tp_last)

    def phase2(self):
        em = self.em
        with contextlib.ExitStack() as p2:
            xnb = self.sbuf("p2_xnb", [128, 2 * 32 * 512], BF16, p2, tdest=True)
            panel = self.sbuf("p2_panel", [128, 8 * 32 * 128], BF16, p2)
            strip = self.sbuf("p2_strip", [128, 2 * 8 * 512], BF16, p2)
            banks = [self.psum(f"p2_b{i}", [128, 512], F32, p2) for i in range(8)]

            wof = self.sbuf("p2_wof", [128, 2 * 1024], F32, p2)
            wob = self.sbuf("p2_wob", [128, 2 * 1024], BF16, p2)

            rel_xnb = [None, None]
            rel_strip = [None, None]
            bank_rel = [None] * 8
            rel_panel = None
            rel_wof = [None, None]
            rel_wob = [None, None]

            def wout_prep(j):
                ws = j % 2
                sw = L2 if ws == 0 else L3
                gw = G2 if ws == 0 else G3
                wfa = wof[:, ws * 1024:(ws + 1) * 1024]
                wba = wob[:, ws * 1024:(ws + 1) * 1024]
                em.waitrel("sync", rel_wof[ws])
                v_wl = self.dma("sync", sw, wfa, self.wout_in[j * 128:(j + 1) * 128, :])
                em.wait("scalar", sw, v_wl)
                em.waitrel("scalar", rel_wob[ws])
                v_c = em.eop("scalar", lambda e, o=wba, i_=wfa:
                             e.activation(o, i_, AF.Identity), inc=AA)
                rel_wof[ws] = (AA, v_c)
                em.wait("gpsimd", AA, v_c)
                v_d = self.dma("gpsimd", gw,
                               self.wout_bf[j * 128:(j + 1) * 128, :], wba)
                rel_wob[ws] = (gw, v_d)

            def issue_loads(tb):
                sl = tb % 2
                sem = L0 if sl == 0 else L1
                em.waitrel("sync", rel_xnb[sl])
                v = 0
                for dc in range(32):
                    dst = xnb[:, (sl * 32 + dc) * 512:(sl * 32 + dc + 1) * 512]
                    src = self.xn_sc[tb * 512:(tb + 1) * 512, dc * 128:(dc + 1) * 128]
                    v = self.dma("sync", sem, dst, src, transpose=True)
                return (sem, v)

            for wpass in range(3):
                em.waitrel("sync", rel_panel)
                v_pan = self.dma("sync", L5, panel[:, :],
                                 self.wq_sc[:, wpass * 32768:(wpass + 1) * 32768])
                cur = issue_loads(0)
                for tb in range(NTBLK):
                    sl = tb % 2
                    if tb + 1 < NTBLK:
                        nxt = issue_loads(tb + 1)
                    else:
                        nxt = None
                    v_fin = [0] * 8
                    em.wait("tensor", L5, v_pan)
                    em.waitrel("tensor", cur)
                    for dc in range(32):
                        xa = xnb[:, (sl * 32 + dc) * 512:(sl * 32 + dc + 1) * 512]
                        for o8 in range(8):
                            oc = wpass * 8 + o8
                            if dc == 0:
                                em.waitrel("tensor", bank_rel[o8])
                            wsl = panel[:, (o8 * 32 + dc) * 128:(o8 * 32 + dc + 1) * 128]
                            fn = (lambda e, o=banks[o8][:, :], w=wsl, x=xa,
                                  st=(dc == 0), sp=(dc == 31):
                                  e.matmul(o, w, x, start=st, stop=sp))
                            if dc == 31:
                                v_fin[o8] = em.eop("tensor", fn, inc=TT)
                            else:
                                em.eop("tensor", fn)
                    rel_xnb[sl] = (TT, v_fin[7])

                    ssl = tb % 2
                    v_d_last = 0
                    for o8 in range(8):
                        oc = wpass * 8 + o8
                        em.wait("scalar", TT, v_fin[o8])
                        em.waitrel("scalar", rel_strip[ssl])
                        dsts = strip[:, (ssl * 8 + o8) * 512:(ssl * 8 + o8 + 1) * 512]
                        scl = Q_SCALE if wpass == 0 else 1.0
                        v_ev = em.eop("scalar",
                                      lambda e, o=dsts, i_=banks[o8][:, :],
                                      b=self.bqs_sb[:, oc:oc + 1], s=scl:
                                      e.activation(o, i_, AF.Identity, bias=b, scale=s),
                                      inc=AA)
                        bank_rel[o8] = (AA, v_ev)
                        em.wait("gpsimd", AA, v_ev)
                        gs = G0 if ssl == 0 else G1
                        v_d_last = self.dma(
                            "gpsimd", gs,
                            self.qkv_sc[oc * 128:(oc + 1) * 128, tb * 512:(tb + 1) * 512],
                            dsts)
                    rel_strip[ssl] = (gs, v_d_last)
                    slot = wpass * NTBLK + tb
                    if slot < 16:
                        wout_prep(2 * slot)
                        wout_prep(2 * slot + 1)
                    cur = nxt
                rel_panel = (TT, em.cnt[TT])

    def phase3(self):
        em = self.em
        with contextlib.ExitStack() as p3:
            q_sb = self.sbuf("p3_q", [128, 2 * S], BF16, p3)
            k_sb = self.sbuf("p3_k", [128, 2 * S], BF16, p3)
            v_sb = self.sbuf("p3_v", [128, 2 * S], BF16, p3)
            vt_sb = self.sbuf("p3_vt", [128, 2 * 32 * 128], BF16, p3)
            al_sb = self.sbuf("p3_al", [3, 2 * S], BF16, p3)
            srow = self.sbuf("p3_srow", [128, 3 * S], F32, p3)
            prow = self.sbuf("p3_prow", [128, 3 * S], BF16, p3)
            pt_sb = self.sbuf("p3_pt", [128, 32 * 128], BF16, p3)
            ao_sb = self.sbuf("p3_ao", [128, 2 * S], BF16, p3)
            zc = self.sbuf("p3_zc", [128, 12], F32, p3)

            sc_banks = [self.psum(f"p3_s{i}", [128, 512], F32, p3) for i in range(3)]
            tp_banks = [self.psum("p3_t0", [128, 1024], BF16, p3),
                        self.psum("p3_t1", [128, 1024], BF16, p3)]
            ao_banks = [self.psum("p3_aob0", [128, 512], F32, p3),
                        self.psum("p3_aob1", [128, 512], F32, p3)]

            st = dict(rel_pair=None, rel_sc=[None, None, None],
                      rel_tp=[None, None], rel_ao_bank=[None, None],
                      rel_ao_dma=[None, None],
                      sc_ctr=0, tp_ctr=0,
                      v_exp=[0, 0, 0], v_tpg=[0, 0, 0], v_max=[0, 0, 0],
                      v_nrm=[0, 0, 0])

            def load_pair(p):
                sem = L0 if p % 2 == 0 else L1
                em.waitrel("sync", st["rel_pair"])
                v = 0
                for hw in range(2):
                    h = 2 * p + hw
                    self.dma("sync", sem, q_sb[:, hw * S:(hw + 1) * S],
                             self.qkv_sc[h * 128:(h + 1) * 128, :])
                    self.dma("sync", sem, k_sb[:, hw * S:(hw + 1) * S],
                             self.qkv_sc[(NHL + h) * 128:(NHL + h + 1) * 128, :])
                    self.dma("sync", sem, v_sb[:, hw * S:(hw + 1) * S],
                             self.qkv_sc[(2 * NHL + h) * 128:(2 * NHL + h + 1) * 128, :])
                    v = self.dma("sync", sem, al_sb[:, hw * S:(hw + 1) * S],
                                 self.alibi3_in[3 * h:3 * h + 3, :])
                return (sem, v)

            def v_transposes(v_l):
                em.waitrel("tensor", v_l)
                for hw in range(2):
                    for g in range(4):
                        bk = st["tp_ctr"] % 2
                        st["tp_ctr"] += 1
                        em.waitrel("tensor", st["rel_tp"][bk])
                        for t8 in range(8):
                            t = g * 8 + t8
                            fn = (lambda e, o=tp_banks[bk][:, t8 * 128:(t8 + 1) * 128],
                                  i_=v_sb[:, hw * S + t * 128:hw * S + (t + 1) * 128],
                                  idn=self.idb_sb:
                                  e.transpose(o, i_, idn[:, :]))
                            if t8 == 7:
                                v_t = em.eop("tensor", fn, inc=TT)
                            else:
                                em.eop("tensor", fn)
                        em.wait("vector", TT, v_t)
                        v_e = em.eop("vector",
                                     lambda e,
                                     o=vt_sb[:, (hw * 4 + g) * 1024:(hw * 4 + g + 1) * 1024],
                                     i_=tp_banks[bk]:
                                     e.tensor_copy(o, i_[:, :]), inc=VV)
                        st["rel_tp"][bk] = (VV, v_e)

            def emit_scores_evac_max(it):
                i, qt, hw = it[0], it[1], it[2]
                L = (qt + 1) * 128
                k_lo = it[3]
                nk = (L - k_lo + 511) // 512
                psl = i % 3
                v_aev = 0
                for j in range(nk):
                    c0 = k_lo + j * 512
                    w = min(512, L - c0)
                    bk = st["sc_ctr"] % 3
                    st["sc_ctr"] += 1
                    em.waitrel("tensor", st["rel_sc"][bk])
                    qa = q_sb[:, hw * S + qt * 128:hw * S + (qt + 1) * 128]
                    ka = k_sb[:, hw * S + c0:hw * S + c0 + w]
                    aa_ = al_sb[:, hw * S + c0:hw * S + c0 + w]
                    em.eop("tensor", lambda e, o=sc_banks[bk][:, 0:w], q=qa, k=ka:
                           e.matmul(o, q, k, start=True, stop=False))
                    v_mm = em.eop("tensor",
                                  lambda e, o=sc_banks[bk][:, 0:w], n=self.ones3_sb, a=aa_:
                                  e.matmul(o, n[:, :], a, start=False, stop=True), inc=TT)
                    dst = srow[:, psl * S + (c0 - k_lo):psl * S + (c0 - k_lo) + w]
                    if j == nk - 1:
                        mv = qt % 4
                        em.wait("vector", TT, v_mm)
                        em.wait("vector", AA, st["v_exp"][psl])
                        v = em.eop("vector",
                                   lambda e, o=dst, i_=sc_banks[bk][:, 0:w],
                                   m=self.masks_sb[:, mv * 512:mv * 512 + w]:
                                   e.tensor_tensor(o, i_, m, ALU.add), inc=VV)
                        st["rel_sc"][bk] = (VV, v)
                    else:
                        em.wait("scalar", TT, v_mm)
                        v = em.eop("scalar", lambda e, o=dst, i_=sc_banks[bk][:, 0:w]:
                                   e.activation(o, i_, AF.Identity), inc=AA)
                        st["rel_sc"][bk] = (AA, v)
                        v_aev = v
                em.wait("vector", AA, v_aev)
                em.wait("vector", AA, st["v_exp"][psl])
                v_max = em.eop("vector",
                               lambda e, o=zc[:, psl * 4:psl * 4 + 1],
                               i_=srow[:, psl * S:psl * S + (L - k_lo)]:
                               e.reduce_max(o, i_, axis=AX.X, negate=True), inc=VV)
                st["v_max"][psl] = v_max

            def emit_exp_norm(it):
                i, qt, hw = it[0], it[1], it[2]
                L = (qt + 1) * 128 - it[3]
                psl = i % 3
                em.wait("scalar", VV, st["v_max"][psl])
                em.wait("scalar", TT, st["v_tpg"][psl])
                v_exp = em.eop("scalar",
                               lambda e, o=prow[:, psl * S:psl * S + L],
                               i_=srow[:, psl * S:psl * S + L],
                               b=zc[:, psl * 4:psl * 4 + 1],
                               a=zc[:, psl * 4 + 1:psl * 4 + 2]:
                               e.activation(o, i_, AF.Exp, bias=b, accum_out=a), inc=AA)
                st["v_exp"][psl] = v_exp

                em.wait("vector", AA, v_exp)
                self.chain("vector", lambda e, o=zc[:, psl * 4 + 2:psl * 4 + 3],
                           i_=zc[:, psl * 4 + 1:psl * 4 + 2]:
                           e.reciprocal(o, i_), VV)
                v_nrm = em.eop("vector",
                               lambda e, o=prow[:, psl * S:psl * S + L],
                               s=zc[:, psl * 4 + 2:psl * 4 + 3]:
                               e.tensor_scalar(o, o, s, None, ALU.mult), inc=VV)
                st["v_nrm"][psl] = v_nrm

            def emit_tp_av(it):
                i, qt, hw = it[0], it[1], it[2]
                k_lo = it[3]
                L = (qt + 1) * 128 - k_lo
                T = L // 128
                psl = i % 3
                em.wait("tensor", VV, st["v_nrm"][psl])
                v_e_last = 0
                v_tp_last = 0
                for g0 in range(0, T, 8):
                    gn = min(8, T - g0)
                    bk = st["tp_ctr"] % 2
                    st["tp_ctr"] += 1
                    em.waitrel("tensor", st["rel_tp"][bk])
                    for t8 in range(gn):
                        t = g0 + t8
                        fn = (lambda e, o=tp_banks[bk][:, t8 * 128:(t8 + 1) * 128],
                              i_=prow[:, psl * S + t * 128:psl * S + (t + 1) * 128],
                              idn=self.idb_sb:
                              e.transpose(o, i_, idn[:, :]))
                        if t8 == gn - 1:
                            v_tp_last = em.eop("tensor", fn, inc=TT)
                        else:
                            em.eop("tensor", fn)
                    em.wait("vector", TT, v_tp_last)
                    v_e_last = em.eop("vector",
                                      lambda e, o=pt_sb[:, g0 * 128:(g0 + gn) * 128],
                                      i_=tp_banks[bk][:, 0:gn * 128]:
                                      e.tensor_copy(o, i_), inc=VV)
                    st["rel_tp"][bk] = (VV, v_e_last)
                st["v_tpg"][psl] = v_tp_last

                apar = i % 2
                em.wait("tensor", VV, v_e_last)
                em.waitrel("tensor", st["rel_ao_bank"][apar])
                for t in range(T):
                    kt = k_lo // 128 + t
                    fn = (lambda e, o=ao_banks[apar][:, 0:128],
                          vt=vt_sb[:, hw * 4096 + kt * 128:hw * 4096 + (kt + 1) * 128],
                          pt=pt_sb[:, t * 128:(t + 1) * 128],
                          stt=(t == 0), sp=(t == T - 1):
                          e.matmul(o, vt, pt, start=stt, stop=sp))
                    if t == T - 1:
                        v_av = em.eop("tensor", fn, inc=TT)
                    else:
                        em.eop("tensor", fn)
                em.wait("vector", TT, v_av)
                em.waitrel("vector", st["rel_ao_dma"][hw])
                v_ao = em.eop("vector",
                              lambda e, o=ao_sb[:, hw * S + qt * 128:hw * S + (qt + 1) * 128],
                              i_=ao_banks[apar][:, 0:128]:
                              e.tensor_copy(o, i_), inc=VV)
                st["rel_ao_bank"][apar] = (VV, v_ao)
                return v_ao

            v_l = load_pair(0)
            em.waitrel("tensor", v_l)
            em.barrier()
            for p in range(NHL // 2):
                v_transposes(v_l)
                pend = []
                v_ao_last = [0, 0]
                for i in range(2 * NQT):
                    qt, hw = i // 2, i % 2
                    W = SLOT_W[2 * p + hw]
                    L = (qt + 1) * 128
                    k_lo = max(0, (L - W) // 512 * 512) if W < S else 0
                    it = (i, qt, hw, k_lo)
                    if len(pend) >= 3:
                        it3 = pend.pop(0)
                        v_ao_last[it3[2]] = emit_tp_av(it3)
                    emit_scores_evac_max(it)
                    pend.append(it)
                    if len(pend) >= 2:
                        emit_exp_norm(pend[-2])
                emit_exp_norm(pend[-1])
                for it3 in pend:
                    v_ao_last[it3[2]] = emit_tp_av(it3)
                st["rel_pair"] = (TT, em.cnt[TT])
                if p + 1 < NHL // 2:
                    v_l = load_pair(p + 1)
                for hw in range(2):
                    h = 2 * p + hw
                    ga = G0 if hw == 0 else G1
                    em.wait("gpsimd", VV, v_ao_last[hw])
                    v_d = self.dma("gpsimd", ga,
                                   self.ao_sc[h * 128:(h + 1) * 128, :],
                                   ao_sb[:, hw * S:(hw + 1) * S])
                    st["rel_ao_dma"][hw] = (ga, v_d)

    def phase5a(self):
        em = self.em
        self.p5 = contextlib.ExitStack()
        self.p5a = contextlib.ExitStack()

        self.ao5 = self.sbuf("p5_ao", [128, 8 * S], BF16, self.p5a, side="right")
        self.wo5 = self.sbuf("p5_wo", [128, 8 * S], BF16, self.p5a, tdest=True,
                             side="right")
        self.strip5 = self.sbuf("p5_strip", [128, 2 * 512], BF16, self.p5a,
                                side="right")
        self.pa = [self.psum("p5_a0", [128, 512], F32, self.p5),
                   self.psum("p5_a1", [128, 512], F32, self.p5)]

        for h in range(NHL):
            v_ao = self.dma("sync", L0, self.ao5[:, h * S:(h + 1) * S],
                            self.ao_sc[h * 128:(h + 1) * 128, :])
        for dc in range(NHL):
            v_wo = self.dma("sync", L5, self.wo5[:, dc * S:(dc + 1) * S],
                            self.wout_bf[:, dc * 128:(dc + 1) * 128], transpose=True)

        rel_bank = [None, None]
        rel_strip = [None, None]
        rs_done = []

        for oc in range(8):
            last_dma = 0
            for tt in range(32):
                bk = tt % 2
                em.waitrel("tensor", rel_bank[bk])
                em.wait("tensor", L0, v_ao)
                em.wait("tensor", L5, v_wo)
                for dc in range(8):
                    aoa = self.ao5[:, dc * S + tt * 128:dc * S + (tt + 1) * 128]
                    rhs = self.wo5[:, dc * S + oc * 512:dc * S + (oc + 1) * 512]
                    fn = (lambda e, o=self.pa[bk][:, :], a=aoa, r=rhs,
                          stt=(dc == 0), sp=(dc == 7):
                          e.matmul(o, a, r, start=stt, stop=sp))
                    if dc == 7:
                        v_mm = em.eop("tensor", fn, inc=TT)
                    else:
                        em.eop("tensor", fn)

                ssl = tt % 2
                em.wait("scalar", TT, v_mm)
                em.waitrel("scalar", rel_strip[ssl])
                dsts = self.strip5[:, ssl * 512:(ssl + 1) * 512]
                v_ev = em.eop("scalar", lambda e, o=dsts, i_=self.pa[bk][:, :]:
                              e.activation(o, i_, AF.Identity), inc=AA)
                rel_bank[bk] = (AA, v_ev)

                em.wait("gpsimd", AA, v_ev)
                gs = G0 if ssl == 0 else G1
                last_dma = self.dma("gpsimd", gs,
                                    self.ypart[tt * 128:(tt + 1) * 128,
                                               oc * 512:(oc + 1) * 512], dsts)
                rel_strip[ssl] = (gs, last_dma)
            rel_wo = (TT, em.cnt[TT])
        self.rs_done = rs_done

    def emit_collectives(self):
        em = self.em
        em.barrier()
        v_cc = em.inc(CC)

        def f(e, SM, i_=self.ypart, o=self.yred):
            e.collective_compute(
                "ReduceScatter", ALU.add, replica_groups=RS_GROUPS,
                ins=[i_.ap().opt()], outs=[o.ap().opt()]).then_inc(SM[CC], 1)
        em.ops["gpsimd"].append(f)
        self.rs_done = [v_cc] * 8
        em.wait("gpsimd", CC, em.cnt[CC])
        em.barrier()

    def phase5b(self):
        em = self.em
        with self.p5:
            p5 = self.p5
            tpb = [self.psum("p5_tp0", [128, 1024], BF16, p5),
                   self.psum("p5_tp1", [128, 1024], BF16, p5)]
            lgp = self.psum("p5_lgp", [128, 512], F32, p5)
            wdtp = self.psum("p5_wdtp", [128, 512], BF16, p5)
            bmp = [self.psum("p5_bm0", [128, 512], F32, p5),
                   self.psum("p5_bm1", [128, 512], F32, p5)]

            v_rwfm = 0
            with contextlib.ExitStack() as prep:
                rwf = self.sbuf("p5_rwf", [64, DM], F32, prep)
                rwb = self.sbuf("p5_rwb", [128, DM], BF16, prep)
                ebf = self.sbuf("p5_ebf", [64, DM], F32, prep)

                if self.p5b_stage >= 1:
                    v_rw = self.dma("sync", LC, rwf[:, :], self.rw_in[:, :])
                    v_eb = self.dma("sync", LC, ebf[:, :], self.eb_in[:, :])
                    em.wait("scalar", LC, v_eb)
                    em.eop("scalar", lambda e, o=rwb, i_=rwf:
                           e.activation(o[0:64, :], i_[:, :], AF.Identity))
                    em.wait("scalar", LC, v_eb)
                    v_casts = em.eop("scalar", lambda e, o=self.eb_bf, i_=ebf:
                                     e.activation(o[:, :], i_[:, :], AF.Identity), inc=AA)

                if self.p5b_stage >= 2:
                    em.wait("tensor", AA, v_casts)
                    v_ev_g = [0, 0]
                    for g in range(4):
                        bk = g % 2
                        em.wait("tensor", VV, v_ev_g[bk])
                        for t8 in range(8):
                            dc = g * 8 + t8
                            fn = (lambda e, o=tpb[bk][:, t8 * 128:(t8 + 1) * 128],
                                  i_=rwb[:, dc * 128:(dc + 1) * 128], idn=self.idb_sb:
                                  e.transpose(o, i_, idn[:, :]))
                            if t8 == 7:
                                v_t = em.eop("tensor", fn, inc=TT)
                            else:
                                em.eop("tensor", fn)
                        if self.p5b_stage < 3:
                            continue
                        em.wait("vector", TT, v_t)
                        for t8 in range(8):
                            dc = g * 8 + t8
                            fn = (lambda e, o=self.rw_fm[:, dc * NE:(dc + 1) * NE],
                                  i_=tpb[bk][:, t8 * 128:t8 * 128 + NE]:
                                  e.tensor_copy(o, i_))
                            if t8 == 7:
                                v_rwfm = em.eop("vector", fn, inc=VV)
                            else:
                                em.eop("vector", fn)
                        v_ev_g[bk] = v_rwfm
                em.barrier()

            self.emit_collectives()
            self.p5a.close()

            y_ring = self.sbuf("p5_y", [128, 2 * DM], F32, p5)
            xn5 = self.sbuf("p5_xn5", [128, 2 * DM], BF16, p5)
            xnfm = self.sbuf("p5_xnfm", [128, 32 * 128], BF16, p5)
            xnw = self.sbuf("p5_xnw", [128, DM], F32, p5)
            bout_sb = self.sbuf("p5_bout", [128, DM], F32, p5)
            yb = self.sbuf("p5_yb", [128, 4 * 512], BF16, p5)
            st5 = self.sbuf("p5_st5", [128, 16], F32, p5)
            lg = self.sbuf("p5_lg", [128, NE], F32, p5)
            ex5 = self.sbuf("p5_ex5", [128, NE], F32, p5)
            wk5 = self.sbuf("p5_wk5", [128, NE], F32, p5)
            eq5 = self.sbuf("p5_eq5", [128, NE], F32, p5)
            wd5 = self.sbuf("p5_wd5", [128, NE], F32, p5)
            wdb = self.sbuf("p5_wdb", [128, NE], BF16, p5)
            wdt = self.sbuf("p5_wdt", [64, 128], BF16, p5)
            cols5 = self.sbuf("p5_c5", [128, 8], F32, p5)

            v_bout = self.dma("sync", LC, bout_sb[:, :], self.bout_in[:, :])

            rel_y = [None, None]
            rel_xn5 = [None, None]
            rel_tpb = [(VV, v_rwfm), (VV, v_rwfm)]
            rel_lgp = None
            rel_wdtp = None
            rel_bmp = [None, None]
            rel_yb = [None, None, None, None]
            rel_st5 = [[None, None], [None, None]]
            v_ex_prev = 0

            for t5 in range(self.t5_count):
                sl = t5 % 2
                ya = y_ring[:, sl * DM:(sl + 1) * DM]
                xa = xn5[:, sl * DM:(sl + 1) * DM]
                c0 = sl * 8

                sx = L0 if sl == 0 else L1
                em.waitrel("sync", rel_y[sl])
                v_x = self.dma("sync", sx, ya, self.xq_in[t5 * 128:(t5 + 1) * 128, :])
                em.wait("vector", sx, v_x)
                em.wait("scalar", sx, v_x)
                em.waitrel("scalar", rel_xn5[sl])

                v_rstd, v_sq, v_std = self.ln_stats(
                    ya, st5, c0, rel_st5[sl][0], rel_st5[sl][1], dump=xa)

                em.waitrel("vector", rel_xn5[sl])
                v_xn = em.eop("vector", lambda e, o=xa, i_=ya, a=st5, c=c0:
                              e.tensor_scalar(o, i_, a[:, c + 1:c + 2], a[:, c + 7:c + 8],
                                              ALU.add, ALU.mult), inc=VV)

                em.wait("tensor", VV, v_xn)
                ev_last = 0
                for g in range(4):
                    bk = g % 2
                    em.waitrel("tensor", rel_tpb[bk])
                    for t8 in range(8):
                        dc = g * 8 + t8
                        fn = (lambda e, o=tpb[bk][:, t8 * 128:(t8 + 1) * 128],
                              i_=xa[:, dc * 128:(dc + 1) * 128], idn=self.idb_sb:
                              e.transpose(o, i_, idn[:, :]))
                        if t8 == 7:
                            v_t = em.eop("tensor", fn, inc=TT)
                        else:
                            em.eop("tensor", fn)
                    em.wait("vector", TT, v_t)
                    ev_last = em.eop("vector",
                                     lambda e, o=xnfm[:, g * 1024:(g + 1) * 1024],
                                     i_=tpb[bk]:
                                     e.tensor_copy(o, i_[:, :]), inc=VV)
                    rel_tpb[bk] = (VV, ev_last)

                em.wait("tensor", VV, ev_last)
                em.waitrel("tensor", rel_lgp)
                for dc in range(32):
                    fn = (lambda e, o=lgp[:, 0:NE], x=xnfm[:, dc * 128:(dc + 1) * 128],
                          r=self.rw_fm[:, dc * NE:(dc + 1) * NE],
                          stt=(dc == 0), sp=(dc == 31):
                          e.matmul(o, x, r, start=stt, stop=sp))
                    if dc == 31:
                        v_lg = em.eop("tensor", fn, inc=TT)
                    else:
                        em.eop("tensor", fn)

                em.wait("vector", TT, v_lg)
                em.wait("vector", LC, self.v_consts)
                v_lgs = self.chain("vector", lambda e, o=lg, i_=lgp, r=self.rb_sb:
                                   e.tensor_tensor(o[:, :], i_[:, 0:NE], r[:, :], ALU.add),
                                   VV)
                rel_lgp = (VV, v_lgs)

                em.wait("vector", AA, v_ex_prev)
                v_m = em.eop("vector", lambda e, o=cols5, i_=lg:
                             e.reduce_max(o[:, 0:1], i_[:, :], axis=AX.X, negate=True),
                             inc=VV)
                em.wait("scalar", VV, v_m)
                v_ex = em.eop("scalar", lambda e, o=ex5, i_=lg, b=cols5:
                              e.activation(o[:, :], i_[:, :], AF.Exp, bias=b[:, 0:1],
                                           accum_out=b[:, 1:2]), inc=AA)
                v_ex_prev = v_ex
                em.wait("vector", AA, v_ex)

                self.chain("vector", lambda e, o=wk5, i_=ex5:
                           e.tensor_copy(o[:, :], i_[:, :]), VV)
                for it in range(KTOP - 1):
                    self.chain("vector", lambda e, o=cols5, i_=wk5:
                               e.reduce_max(o[:, 2:3], i_[:, :], axis=AX.X), VV)
                    self.chain("vector", lambda e, o=eq5, i_=wk5, m=cols5:
                               e.tensor_scalar(o[:, :], i_[:, :], m[:, 2:3], 1.0e7,
                                               ALU.is_equal, ALU.mult), VV)
                    self.chain("vector", lambda e, o=wk5, i_=eq5:
                               e.tensor_tensor(o[:, :], o[:, :], i_[:, :], ALU.subtract), VV)
                self.chain("vector", lambda e, o=cols5, i_=wk5:
                           e.reduce_max(o[:, 3:4], i_[:, :], axis=AX.X), VV)
                self.chain("vector", lambda e, o=eq5, i_=ex5, t=cols5:
                           e.tensor_scalar(o[:, :], i_[:, :], t[:, 3:4], None, ALU.is_ge), VV)
                self.chain("vector", lambda e, o=wd5, a=ex5, b=eq5:
                           e.tensor_tensor(o[:, :], a[:, :], b[:, :], ALU.mult), VV)
                self.chain("vector", lambda e, o=cols5:
                           e.reciprocal(o[:, 4:5], o[:, 1:2]), VV)
                self.chain("vector", lambda e, o=wd5, r=cols5:
                           e.tensor_scalar(o[:, :], o[:, :], r[:, 4:5], None, ALU.mult,
                                           ALU.add, accum_out=r[:, 5:6]), VV)
                v_wd = em.eop("vector", lambda e, o=wdb, i_=wd5:
                              e.tensor_copy(o[:, :], i_[:, :]), inc=VV)

                em.wait("tensor", VV, v_wd)
                em.waitrel("tensor", rel_wdtp)
                v_wt = em.eop("tensor", lambda e, o=wdtp[0:64, 0:128],
                              i_=wdb[:, 0:NE], idn=self.idb_sb:
                              e.transpose(o, i_, idn[:, :]), inc=TT)
                em.wait("vector", TT, v_wt)
                v_wte = em.eop("vector", lambda e, o=wdt, i_=wdtp:
                               e.tensor_copy(o[:, :], i_[0:64, 0:128]), inc=VV)
                rel_wdtp = (VV, v_wte)

                self.chain("vector", lambda e, o=cols5, s=st5, c=c0:
                           e.tensor_tensor(o[:, 6:7], s[:, c + 7:c + 8], o[:, 5:6], ALU.mult), VV)
                em.eop("vector", lambda e, o=xnw, i_=ya, s=st5, c=c0, r=cols5:
                       e.tensor_scalar(o[:, :], i_, s[:, c + 1:c + 2], r[:, 6:7],
                                       ALU.add, ALU.mult))
                em.wait("vector", LC, v_bout)
                em.eop("vector", lambda e, o=xnw, b=bout_sb:
                       e.tensor_tensor(o[:, :], o[:, :], b[:, :], ALU.add))
                em.eop("vector", lambda e, o=ya, a=xnw:
                       e.tensor_tensor(o, o, a[:, :], ALU.add))

                for oc in range(8):
                    bk = oc % 2
                    em.wait("tensor", VV, v_wte)
                    em.waitrel("tensor", rel_bmp[bk])
                    v_bm = em.eop("tensor", lambda e, o=bmp[bk][:, :], w=wdt,
                                  b=self.eb_bf[:, oc * 512:(oc + 1) * 512]:
                                  e.matmul(o, w[:, :], b, start=True, stop=True), inc=TT)

                    ybsl = oc % 4
                    sy = (L2, L3, L4, L5)[ybsl]
                    em.wait("sync", CC, self.rs_done[oc])
                    em.waitrel("sync", rel_yb[ybsl])
                    v_yr = self.dma("sync", sy, yb[:, ybsl * 512:(ybsl + 1) * 512],
                                    self.yred[t5 * 128:(t5 + 1) * 128,
                                              oc * 512:(oc + 1) * 512])

                    em.wait("vector", sy, v_yr)
                    em.wait("vector", TT, v_bm)
                    em.eop("vector", lambda e, o=ya[:, oc * 512:(oc + 1) * 512],
                           r=yb[:, ybsl * 512:(ybsl + 1) * 512]:
                           e.tensor_tensor(o, o, r, ALU.add))
                    v_add = em.eop("vector", lambda e, o=ya[:, oc * 512:(oc + 1) * 512],
                                   b=bmp[bk][:, :]:
                                   e.tensor_tensor(o, o, b, ALU.add), inc=VV)
                    rel_bmp[bk] = (VV, v_add)
                    rel_yb[ybsl] = (VV, v_add)

                v_fin = em.cnt[VV]
                rel_xn5[sl] = (TT, em.cnt[TT])
                rel_st5[sl][0] = (AA, v_std)
                rel_st5[sl][1] = (VV, v_fin)
                gy = G0 if sl == 0 else G1
                em.wait("gpsimd", VV, v_fin)
                v_store = self.dma("gpsimd", gy, self.y_out[t5 * 128:(t5 + 1) * 128, :], ya)
                rel_y[sl] = (gy, v_store)

            em.wait("gpsimd", G0, em.cnt[G0])
            em.wait("gpsimd", G1, em.cnt[G1])

    def emit_debug(self):
        em = self.em
        if not self.dbg:
            return
        em.barrier()
        src_map = {"dbg_xn": self.xn_sc, "dbg_wq": self.wq_sc, "dbg_qkv": self.qkv_sc,
                   "dbg_ao": self.ao_sc, "dbg_wout": self.wout_bf}
        if self.upto >= 4:
            src_map["dbg_ypart"] = self.ypart
            src_map["dbg_yred"] = self.yred
        for name, dst in self.dbg.items():
            src = src_map[name]
            v = self.dma("gpsimd", G0, dst.ap(), src.ap())
            em.wait("gpsimd", G0, v)



def _alibi_slopes(n_heads=32, alibi_bias_max=8):
    _n = 2 ** math.ceil(math.log2(n_heads))
    m = np.arange(1, _n + 1, dtype=np.float64) * (alibi_bias_max / _n)
    slopes = 1.0 / np.power(2.0, m)
    if _n != n_heads:
        slopes = np.concatenate([slopes[1::2], slopes[::2]])[:n_heads]
    return slopes


def _build_const_inputs(heads):
    bf = ml_dtypes.bfloat16
    slopes = _alibi_slopes()
    pos = np.arange(S, dtype=np.float64) - (S - 1)
    al3 = np.zeros((3 * NHL, S), dtype=bf)
    for i, h in enumerate(heads):
        a = slopes[h] * pos
        hi = a.astype(np.float32).astype(bf)
        r1 = a - hi.astype(np.float64)
        mid = r1.astype(np.float32).astype(bf)
        r2 = r1 - mid.astype(np.float64)
        lo = r2.astype(np.float32).astype(bf)
        al3[3 * i + 0] = hi
        al3[3 * i + 1] = mid
        al3[3 * i + 2] = lo

    masks = np.zeros((4 * 128, 512), dtype=np.float32)
    ii = np.arange(128)[:, None]
    jj = np.arange(512)[None, :]
    for v in range(4):
        wd = (v + 1) * 128
        m = np.where(jj <= (wd - 128) + ii, 0.0, np.float32(NEG_BIG)).astype(np.float32)
        m[:, wd:] = 0.0
        masks[v * 128:(v + 1) * 128] = m

    ones3 = np.ones((3, 128), dtype=bf)
    idb = np.eye(128, dtype=np.float32).astype(bf)
    return al3, masks, ones3, idb


_BUILD_CACHE = {}


def get_program(upto=5, debug=(), t5_count=8, p5b_stage=3):
    key = (upto, tuple(sorted(debug)), t5_count, p5b_stage)
    if key not in _BUILD_CACHE:
        _BUILD_CACHE[key] = Builder(upto=upto, debug=debug, t5_count=t5_count,
                                    p5b_stage=p5b_stage).build()
    return _BUILD_CACHE[key]


def make_in_maps(L0_attn_x, L0_attn_w_qkv, L0_attn_b_qkv, L0_attn_w_out,
                 L0_attn_b_out, L0_ffn_router_weight, L0_ffn_router_bias,
                 L0_ffn_expert_bias):
    x = np.ascontiguousarray(np.asarray(L0_attn_x, dtype=np.float32))
    wqkv = np.asarray(L0_attn_w_qkv, dtype=np.float32)
    bqkv = np.asarray(L0_attn_b_qkv, dtype=np.float32)
    wout = np.asarray(L0_attn_w_out, dtype=np.float32)
    bout = np.asarray(L0_attn_b_out, dtype=np.float32)
    rw = np.ascontiguousarray(np.asarray(L0_ffn_router_weight, dtype=np.float32))
    rb = np.asarray(L0_ffn_router_bias, dtype=np.float32)
    eb = np.ascontiguousarray(np.asarray(L0_ffn_expert_bias, dtype=np.float32))

    in_maps = []
    for c in range(8):
        b, g = c // 4, c % 4
        heads = [g + 4 * j for j in range(NHL)]
        rows = []
        for kind in range(3):
            for h in heads:
                rows.append(wqkv[kind * DM + 128 * h: kind * DM + 128 * (h + 1)])
        wqkv_c = np.ascontiguousarray(np.concatenate(rows, axis=0))
        bq = []
        for kind in range(3):
            for h in heads:
                bq.append(bqkv[kind * DM + 128 * h: kind * DM + 128 * (h + 1)])
        bqkv_c = np.ascontiguousarray(np.stack(bq, axis=1).astype(np.float32))
        wout_c = np.ascontiguousarray(
            np.concatenate([wout[:, 128 * h:128 * (h + 1)] for h in heads], axis=1))
        al3, masks, ones3, idb = _build_const_inputs(heads)
        in_maps.append({
            "x": x[b],
            "xq": np.ascontiguousarray(x[b, 1024 * g:1024 * (g + 1)]),
            "wqkv": wqkv_c,
            "bqkv": bqkv_c,
            "wout": wout_c,
            "bout_bc": np.ascontiguousarray(np.tile(bout[None, :], (128, 1))),
            "rw": rw,
            "rb_bc": np.ascontiguousarray(np.tile(rb[None, :], (128, 1))),
            "eb": eb,
            "alibi3": al3,
            "ones3": ones3,
            "masks": masks,
            "idb": idb,
        })
    return in_maps


def kernel(L0_attn_x, L0_attn_w_qkv, L0_attn_b_qkv, L0_attn_w_out, L0_attn_b_out,
           L0_ffn_router_weight, L0_ffn_router_bias, L0_ffn_expert_bias):
    nc = get_program()
    in_maps = make_in_maps(L0_attn_x, L0_attn_w_qkv, L0_attn_b_qkv, L0_attn_w_out,
                           L0_attn_b_out, L0_ffn_router_weight, L0_ffn_router_bias,
                           L0_ffn_expert_bias)
    res = run_bass_kernel_spmd(nc, in_maps, core_ids=list(range(8)))
    out = np.empty((2, S, DM), dtype=np.float32)
    for c in range(8):
        b, g = c // 4, c % 4
        out[b, 1024 * g:1024 * (g + 1), :] = res.results[c]["y"]
    return out
